# version 11
# baseline (speedup 1.0000x reference)
import math
import numpy as np
import concourse.bass as bass
import concourse.mybir as mybir
from concourse.bass_utils import run_bass_kernel_spmd
from concourse.bass_core_rust import IndirectOffsetOnAxis

F32 = mybir.dt.float32
BF16 = mybir.dt.bfloat16
I32 = mybir.dt.int32
AF = mybir.ActivationFunctionType
ALU = mybir.AluOpType
AX = mybir.AxisListType

NT = 4096
T = 2048
D = 1024
NTILE = NT // 128
EPS = 1e-5


class SemCtr:
    def __init__(self, h):
        self.h = h
        self.n = 0
        self.base = 0

    def inc(self, inst, k=1):
        inst.then_inc(self.h, k)
        self.n += k
        return inst

    def wait(self, eng, val):
        eng.wait_ge(self.h, self.base + val)

    @property
    def rel(self):
        return self.n - self.base


class Arena:
    def __init__(self, nc, kb_per_part=184):
        self.bytes = kb_per_part * 1024
        self.t = nc.alloc_sbuf_tensor("arena", [128, self.bytes // 4], F32)
        self.off = 0

    def reset(self):
        self.off = 0

    def alloc(self, shape, dtype):
        assert len(shape) == 2 and shape[0] <= 128
        isz = 2 if dtype == BF16 else 4
        need = shape[1] * isz
        off = (self.off + 511) & ~511
        assert off + need <= self.bytes, f"arena overflow {off}+{need}"
        self.off = off + need
        ap = self.t.ap()[0:shape[0], :]
        if dtype == BF16:
            return ap.bitcast(BF16)[:, off // 2: off // 2 + shape[1]]
        if dtype == I32:
            return ap.bitcast(I32)[:, off // 4: off // 4 + shape[1]]
        return ap[:, off // 4: off // 4 + shape[1]]


class Ctx:
    def __init__(self, nc):
        self.nc = nc
        self.block = None
        self.sems = {}
        self.local_map = {}
        self.ar = Arena(nc)
        self.pp = [nc.alloc_psum_tensor(f"pp{i}", [128, 512], F32) for i in range(8)]
        self.S = {}
        self.consts = None

    def sem(self, name):
        if name.startswith("selfsync") or name == "c_ld":
            if name not in self.sems:
                self.sems[name] = SemCtr(self.nc.semaphore(name).__enter__())
            return self.sems[name]
        if name in self.local_map:
            return self.local_map[name]
        idx = len(self.local_map)
        key = f"pool{idx}"
        if key not in self.sems:
            self.sems[key] = SemCtr(self.nc.semaphore(key).__enter__())
        s = self.sems[key]
        s.base = s.n
        self.local_map[name] = s
        return s

    def new_phase(self):
        self.local_map = {}

    def sec(self, eng):
        return getattr(self.block, eng)

    def fence(self, eng_obj, inst):
        s = self.sem("selfsync_" + type(eng_obj).__name__)
        s.inc(inst)
        eng_obj.wait_ge(s.h, s.n)


def view3(ap, g, c):
    return ap.rearrange("p (g c) -> p g c", g=g, c=c)



def ph1_ln0(ctx):
    nc, ar, S = ctx.nc, ctx.ar, ctx.S
    ar.reset()
    x = S["x"]
    xn_f = S["xn0_f32"]
    xn_b = S["xn0_bf"]
    gn1ab = S["gn1ab"]
    stat_tmp = S["ph1_stat"]
    dbg = S["p1dbg"]
    C = ctx.consts

    xt = [ar.alloc([128, 1024], F32) for _ in range(2)]
    sq = [ar.alloc([128, 1024], F32) for _ in range(2)]
    dump = ar.alloc([128, 1024], F32)
    xnf = [ar.alloc([128, 1024], F32) for _ in range(2)]
    xnb = [ar.alloc([128, 1024], BF16) for _ in range(2)]
    gcat = [ar.alloc([128, 32], F32) for _ in range(2)]
    stats = ar.alloc([128, 128], F32)
    statq = ar.alloc([128, 128], F32)
    vv = ar.alloc([128, 32], F32)
    rstd = ar.alloc([128, 32], F32)
    stat = ar.alloc([128, 8], F32)
    rfm = ar.alloc([128, 8], F32)
    mfm = ar.alloc([128, 8], F32)
    afm = ar.alloc([128, 8], F32)
    bfm = ar.alloc([128, 8], F32)

    s_x = ctx.sem("p1_x")
    sA = ctx.sem("p1_A")
    sSB = ctx.sem("p1_SB")
    s_x2 = ctx.sem("p1_x2")
    sD = ctx.sem("p1_D")
    sE = ctx.sem("p1_E")
    sF = ctx.sem("p1_F")
    sG = ctx.sem("p1_G")
    sST = ctx.sem("p1_ST")
    sV2 = ctx.sem("p1_V2")
    sC2 = ctx.sem("p1_C2")
    sPM = ctx.sem("p1_PM")
    sV3 = ctx.sem("p1_V3")
    sAB = ctx.sem("p1_AB")

    pp0, pp1 = ctx.pp[0], ctx.pp[1]

    for i in range(NTILE):
        buf = i % 2

        @ctx.sec("sync")
        def _(sy, i=i, buf=buf):
            if i >= 2:
                sA.wait(sy, i - 1)
            s_x.inc(sy.dma_start(out=xt[buf][:], in_=x[i * 128:(i + 1) * 128, :]), 16)

        @ctx.sec("scalar")
        def _(sc, i=i, buf=buf):
            s_x.wait(sc, 16 * (i + 1))
            sc.activation(dump[:], xt[buf][:], AF.Identity, scale=1.0 / 1024,
                          accum_out=stats[:, 4 * i:4 * i + 1])
            sA.inc(sc.activation(dump[:], xt[buf][:], AF.Square, scale=1.0 / 32,
                                 accum_out=statq[:, 4 * i:4 * i + 1]))

    @ctx.sec("vector")
    def _(v):
        sA.wait(v, NTILE)
        muv = view3(stats, 32, 4)[:, :, 0:1]
        msqv = view3(statq, 32, 4)[:, :, 0:1]
        inst = v.tensor_mul(vv[:].rearrange("p (g c) -> p g c", c=1), muv, muv)
        ctx.fence(v, inst)
        inst = v.scalar_tensor_tensor(vv[:].rearrange("p (g c) -> p g c", c=1),
                                      vv[:].rearrange("p (g c) -> p g c", c=1),
                                      -1.0, msqv, ALU.mult, ALU.add)
        ctx.fence(v, inst)
        sA.inc(v.tensor_scalar_add(vv[:], vv[:], EPS))

    @ctx.sec("scalar")
    def _(sc):
        sA.wait(sc, NTILE + 1)
        inst = sc.activation(rstd[:], vv[:], AF.Ln)
        ctx.fence(sc, inst)
        sSB.inc(sc.activation(rstd[:], rstd[:], AF.Exp, scale=-0.5))

    for i in range(NTILE):
        b = i // 16
        buf = i % 2
        first, last = i % 16 == 0, i % 16 == 15

        @ctx.sec("sync")
        def _(sy, i=i, buf=buf):
            if i < 2:
                sA.wait(sy, NTILE)
            else:
                sD.wait(sy, i - 1)
                sE.wait(sy, i - 1)
            s_x2.inc(sy.dma_start(out=xt[buf][:], in_=x[i * 128:(i + 1) * 128, :]), 16)

        @ctx.sec("vector")
        def _(v, i=i, buf=buf):
            s_x2.wait(v, 16 * (i + 1))
            sSB.wait(v, 1)
            if i >= 2:
                sE.wait(v, i - 1)
                sST.wait(v, 32 * (i - 1))
                sG.wait(v, i - 1)
            v.tensor_scalar(xnf[buf][:], xt[buf][:], stats[:, 4 * i:4 * i + 1],
                            rstd[:, i:i + 1], ALU.subtract, ALU.mult)
            v.tensor_copy(xnb[buf][:], xnf[buf][:])
            sD.inc(v.reduce_sum(gcat[buf][:, 0:16], view3(xnf[buf], 16, 64), axis=AX.X))

        @ctx.sec("scalar")
        def _(sc, i=i, buf=buf):
            sD.wait(sc, i + 1)
            sE.inc(sc.activation(sq[buf][:], xnf[buf][:], AF.Square))

        @ctx.sec("vector")
        def _(v, i=i, buf=buf):
            sE.wait(v, i + 1)
            sF.inc(v.reduce_sum(gcat[buf][:, 16:32], view3(sq[buf], 16, 64), axis=AX.X))

        @ctx.sec("tensor")
        def _(t, i=i, buf=buf, first=first, last=last):
            if i == 0:
                t.wait_ge(ctx.cld.h, ctx.cld.n)
            sF.wait(t, i + 1)
            if i == 16:
                sV2.wait(t, 4)
            t.matmul(pp0[0:16, 0:1], gcat[buf][:, 0:16], C["ones"][:, 0:1],
                     start=first, stop=last)
            sG.inc(t.matmul(pp1[0:16, 0:1], gcat[buf][:, 16:32], C["ones"][:, 0:1],
                            start=first, stop=last))

        @ctx.sec("sync")
        def _(sy, i=i, buf=buf):
            sD.wait(sy, i + 1)
            sST.inc(sy.dma_start(out=xn_f[i * 128:(i + 1) * 128, :], in_=xnf[buf][:]), 16)
            sST.inc(sy.dma_start(out=xn_b[i * 128:(i + 1) * 128, :], in_=xnb[buf][:]), 16)

        if last:
            @ctx.sec("vector")
            def _(v, b=b):
                if b == 0:
                    v.wait_ge(ctx.cld.h, ctx.cld.n)
                sG.wait(v, 16 * (b + 1))
                if b == 1:
                    sPM.wait(v, 64)
                v.tensor_scalar_mul(stat[0:16, 0:1], pp0[0:16, 0:1], 1.0 / (64 * 2048))
                inst = v.tensor_scalar_mul(stat[0:16, 1:2], pp1[0:16, 0:1], 1.0 / (64 * 2048))
                ctx.fence(v, inst)
                inst = v.tensor_mul(stat[0:16, 2:3], stat[0:16, 0:1], stat[0:16, 0:1])
                ctx.fence(v, inst)
                inst = v.scalar_tensor_tensor(stat[0:16, 2:3], stat[0:16, 2:3], -1.0,
                                              stat[0:16, 1:2], ALU.mult, ALU.add)
                ctx.fence(v, inst)
                sV2.inc(v.tensor_scalar_add(stat[0:16, 2:3], stat[0:16, 2:3], EPS), 4)

            @ctx.sec("scalar")
            def _(sc, b=b):
                sV2.wait(sc, 4 * (b + 1))
                inst = sc.activation(stat[0:16, 3:4], stat[0:16, 2:3], AF.Ln)
                ctx.fence(sc, inst)
                sC2.inc(sc.activation(stat[0:16, 3:4], stat[0:16, 3:4], AF.Exp, scale=-0.5))

            @ctx.sec("sync")
            def _(sy, b=b):
                sC2.wait(sy, b + 1)
                sPM.inc(sy.dma_start(
                    out=stat_tmp[b, 0].rearrange("(h c) -> c h", h=2, c=8),
                    in_=stat[0:16, 3:4], allow_slow_non_contiguous=True), 16)
                sPM.inc(sy.dma_start(
                    out=stat_tmp[b, 1].rearrange("(h c) -> c h", h=2, c=8),
                    in_=stat[0:16, 0:1], allow_slow_non_contiguous=True), 16)
                sPM.wait(sy, 64 * b + 32)
                if b == 1:
                    sV3.wait(sy, 3)
                src_r = stat_tmp[b, 0].rearrange("(h c) -> h c", h=2) \
                    .rearrange("h (c u) -> h u c", u=1).broadcast_to([2, 64, 8])
                src_m = stat_tmp[b, 1].rearrange("(h c) -> h c", h=2) \
                    .rearrange("h (c u) -> h u c", u=1).broadcast_to([2, 64, 8])
                sPM.inc(sy.dma_start(out=rfm[:], in_=src_r), 16)
                sPM.inc(sy.dma_start(out=mfm[:], in_=src_m), 16)

            @ctx.sec("vector")
            def _(v, b=b):
                sPM.wait(v, 64 * (b + 1))
                if b == 1:
                    sAB.wait(v, 32)
                inst = v.tensor_mul(afm[:], rfm[:], C["gn1_g"][:])
                ctx.fence(v, inst)
                inst = v.tensor_mul(bfm[:], mfm[:], afm[:])
                ctx.fence(v, inst)
                sV3.inc(v.scalar_tensor_tensor(bfm[:], bfm[:], -1.0, C["gn1_b"][:],
                                               ALU.mult, ALU.add), 3)

            @ctx.sec("sync")
            def _(sy, b=b):
                sV3.wait(sy, 3 * (b + 1))
                sAB.inc(sy.dma_start(out=gn1ab[b, 0], in_=afm[:]), 16)
                sAB.inc(sy.dma_start(out=gn1ab[b, 1], in_=bfm[:]), 16)
                if b == 1:
                    sDBG = ctx.sem("p1_dbg")
                    sDBG.inc(sy.dma_start(out=dbg[:, 0:32], in_=view3(stats, 32, 4)[:, :, 0:1], allow_slow_non_contiguous=True), 16)
                    sDBG.inc(sy.dma_start(out=dbg[:, 32:64], in_=view3(statq, 32, 4)[:, :, 0:1], allow_slow_non_contiguous=True), 16)
                    sDBG.inc(sy.dma_start(out=dbg[:, 64:96], in_=vv[:]), 16)
                    sDBG.inc(sy.dma_start(out=dbg[:, 96:128], in_=rstd[:]), 16)






def ph2a_qk0(ctx):
    nc, ar, S = ctx.nc, ctx.ar, ctx.S
    ar.reset()
    xn_b = S["xn0_bf"]
    gn1ab = S["gn1ab"]

    wq = ar.alloc([128, 8192], BF16)
    wk = ar.alloc([128, 8192], BF16)
    ab = ar.alloc([128, 32], F32)
    ga = [ar.alloc([128, 512], BF16) for _ in range(16)]
    qk_bf = [ar.alloc([128, 512], BF16) for _ in range(4)]
    sqv = [ar.alloc([128, 512], F32) for _ in range(2)]
    ssm = ar.alloc([128, 32], F32)
    fq = ar.alloc([128, 32], F32)
    fk = ar.alloc([128, 32], F32)

    s_w = ctx.sem("p2_w")
    s_a = ctx.sem("p2_a")
    s_ab = ctx.sem("p2_ab")
    s_mm = ctx.sem("p2_mm")
    s_ev = ctx.sem("p2_ev")
    s_sq = ctx.sem("p2_sq")
    s_qm = ctx.sem("p2_qm")
    s_st = ctx.sem("p2_st")
    s_f = ctx.sem("p2_f")

    @ctx.sec("gpsimd")
    def _(g):
        load_wT(ctx, g, S["qwT"], 1024, 1024, wq, s_w)
        load_wT(ctx, g, S["kwT"], 1024, 1024, wk, s_w)

    @ctx.sec("sync")
    def _(sy):
        for b in range(2):
            s_ab.inc(sy.dma_start(out=ab[:, 16 * b:16 * b + 8], in_=gn1ab[b, 0]), 16)
            s_ab.inc(sy.dma_start(out=ab[:, 16 * b + 8:16 * b + 16], in_=gn1ab[b, 1]), 16)

    ps_ss = ctx.pp[4]
    first_ss = [True]

    NTB = 8
    for tb in range(NTB):
        bt = tb // 4
        gbuf = [ga[(tb % 2) * 8 + dc] for dc in range(8)]

        @ctx.sec("sync")
        def _(sy, tb=tb, gbuf=gbuf):
            if tb >= 2:
                s_mm.wait(sy, (tb - 1) * 16)
            for dc in range(8):
                tload(ctx, sy, xn_b, tb * 512, 512, dc * 128, 128, gbuf[dc], s_a)

        @ctx.sec("vector")
        def _(v, tb=tb, bt=bt, gbuf=gbuf):
            s_a.wait(v, 16 * 8 * (tb + 1))
            if tb == 0:
                s_ab.wait(v, 64)
            for dc in range(8):
                v.tensor_scalar(gbuf[dc][:], gbuf[dc][:],
                                ab[:, 16 * bt + dc:16 * bt + dc + 1],
                                ab[:, 16 * bt + 8 + dc:16 * bt + 9 + dc],
                                ALU.mult, ALU.add)
            s_a.inc(v.engine_nop())

        for ti, (wsb, scr, colbase) in enumerate([(wq, "qT0", 0), (wk, "kT0", 16)]):
            for oc in range(8):
                pb = ctx.pp[(oc + ti * 8) % 2]
                obuf = qk_bf[(oc + ti * 8) % 4]
                sqb = sqv[(oc + ti * 8) % 2]

                @ctx.sec("tensor")
                def _(t, tb=tb, ti=ti, oc=oc, pb=pb, wsb=wsb, gbuf=gbuf):
                    if tb == 0 and ti == 0 and oc == 0:
                        s_w.wait(t, 32)
                    s_a.wait(t, (16 * 8 + 1) * (tb + 1))
                    k = (tb * 16 + ti * 8 + oc)
                    if k >= 2:
                        s_ev.wait(t, k - 1)
                        s_sq.wait(t, k - 1)
                    for dc in range(8):
                        inst = t.matmul(pb[:, :],
                                        wslice(wsb, 1024, dc, oc * 128, oc * 128 + 128),
                                        gbuf[dc][:], start=(dc == 0), stop=(dc == 7))
                    s_mm.inc(inst)

                @ctx.sec("vector")
                def _(v, tb=tb, ti=ti, oc=oc, pb=pb, obuf=obuf):
                    k = (tb * 16 + ti * 8 + oc)
                    s_mm.wait(v, k + 1)
                    if k >= 4:
                        s_st.wait(v, 16 * (k - 3))
                    s_ev.inc(v.tensor_copy(obuf[:], pb[:, :]))

                @ctx.sec("scalar")
                def _(sc, tb=tb, ti=ti, oc=oc, pb=pb, sqb=sqb):
                    k = (tb * 16 + ti * 8 + oc)
                    s_mm.wait(sc, k + 1)
                    if k >= 2:
                        s_qm.wait(sc, 4 * (k - 1))
                    s_sq.inc(sc.activation(sqb[:], pb[:, :], AF.Square))

                @ctx.sec("tensor")
                def _(t, tb=tb, ti=ti, oc=oc, sqb=sqb, colbase=colbase):
                    k = (tb * 16 + ti * 8 + oc)
                    s_sq.wait(t, k + 1)
                    for tc in range(4):
                        gtc = tb * 4 + tc
                        col = colbase * 2 + gtc
                        st = first_ss[0]
                        first_ss[0] = False
                        s_qm.inc(t.matmul(ps_ss[:, col:col + 1],
                                          sqb[:, tc * 128:tc * 128 + 128],
                                          ctx.consts["ones"][:, 0:1],
                                          start=st, stop=True, skip_group_check=True))

                @ctx.sec("gpsimd")
                def _(g, tb=tb, ti=ti, oc=oc, obuf=obuf, scr=scr):
                    k = (tb * 16 + ti * 8 + oc)
                    s_ev.wait(g, k + 1)
                    s_st.inc(g.dma_start(
                        out=S[scr][oc * 128:(oc + 1) * 128, tb * 512:(tb + 1) * 512],
                        in_=obuf[:]), 16)

    @ctx.sec("vector")
    def _(v):
        s_qm.wait(v, 4 * 16 * NTB)
        inst = v.tensor_scalar_max(ssm[:], ps_ss[:, 0:32], 1e-24)
        ctx.fence(v, inst)
        s_f.inc(v.engine_nop())

    @ctx.sec("scalar")
    def _(sc):
        s_f.wait(sc, 1)
        inst = sc.activation(fq[:], ssm[:], AF.Ln)
        ctx.fence(sc, inst)
        s_f.inc(sc.activation(fq[:], fq[:], AF.Exp, scale=-0.5))

    @ctx.sec("vector")
    def _(v):
        s_f.wait(v, 2)
        inst = v.tensor_scalar_max(ssm[:], ps_ss[:, 32:64], 1e-24)
        ctx.fence(v, inst)
        s_f.inc(v.engine_nop())

    @ctx.sec("scalar")
    def _(sc):
        s_f.wait(sc, 3)
        inst = sc.activation(fk[:], ssm[:], AF.Ln)
        ctx.fence(sc, inst)
        s_f.inc(sc.activation(fk[:], fk[:], AF.Exp, scale=-0.5))

    @ctx.sec("sync")
    def _(sy):
        s_f.wait(sy, 4)
        s_f.inc(sy.dma_start(out=S["fq0"], in_=fq[:]), 16)
        s_f.inc(sy.dma_start(out=S["fk0"], in_=fk[:]), 16)



def ph2b_kv0(ctx):
    nc, ar, S = ctx.nc, ctx.ar, ctx.S
    ar.reset()
    xn_b = S["xn0_bf"]
    gn1ab = S["gn1ab"]

    wv = ar.alloc([128, 8192], BF16)
    wk = ar.alloc([128, 8192], BF16)
    ab = ar.alloc([128, 32], F32)
    fkc = ar.alloc([128, 32], F32)
    ga = [ar.alloc([128, 512], BF16) for _ in range(16)]
    ob = [ar.alloc([128, 512], BF16) for _ in range(4)]

    s_w = ctx.sem("p2b_w")
    s_a = ctx.sem("p2b_a")
    s_mm = ctx.sem("p2b_mm")
    s_ev = ctx.sem("p2b_ev")
    s_st = ctx.sem("p2b_st")

    @ctx.sec("gpsimd")
    def _(g):
        load_wT(ctx, g, S["vwT"], 1024, 1024, wv, s_w)
        load_wT(ctx, g, S["kwT"], 1024, 1024, wk, s_w)

    @ctx.sec("sync")
    def _(sy):
        for b in range(2):
            s_w.inc(sy.dma_start(out=ab[:, 16 * b:16 * b + 8], in_=gn1ab[b, 0]), 16)
            s_w.inc(sy.dma_start(out=ab[:, 16 * b + 8:16 * b + 16], in_=gn1ab[b, 1]), 16)
        s_w.inc(sy.dma_start(out=fkc[:], in_=S["fk0"]), 16)

    for tb in range(8):
        bt = tb // 4
        gbuf = [ga[(tb % 2) * 8 + dc] for dc in range(8)]

        @ctx.sec("sync")
        def _(sy, tb=tb, gbuf=gbuf):
            if tb >= 2:
                s_mm.wait(sy, (tb - 1) * 16)
            for dc in range(8):
                tload(ctx, sy, xn_b, tb * 512, 512, dc * 128, 128, gbuf[dc], s_a)

        @ctx.sec("vector")
        def _(v, tb=tb, bt=bt, gbuf=gbuf):
            s_a.wait(v, 16 * 8 * (tb + 1))
            if tb == 0:
                s_w.wait(v, 32 + 80)
            for dc in range(8):
                v.tensor_scalar(gbuf[dc][:], gbuf[dc][:],
                                ab[:, 16 * bt + dc:16 * bt + dc + 1],
                                ab[:, 16 * bt + 8 + dc:16 * bt + 9 + dc],
                                ALU.mult, ALU.add)
            s_a.inc(v.engine_nop())

        for tc in range(4):
            gt = tb * 4 + tc
            for ti, (wsb, scr) in enumerate([(wv, "vtm0"), (wk, "ktm0")]):
                for oc in range(2):
                    k = gt * 4 + ti * 2 + oc
                    pb = ctx.pp[k % 2]
                    obuf = ob[k % 4]

                    @ctx.sec("tensor")
                    def _(t, tb=tb, tc=tc, ti=ti, oc=oc, k=k, pb=pb, wsb=wsb, gbuf=gbuf):
                        if k == 0:
                            s_w.wait(t, 112)
                        s_a.wait(t, (16 * 8 + 1) * (tb + 1))
                        if k >= 2:
                            s_ev.wait(t, k - 1)
                        for dc in range(8):
                            inst = t.matmul(pb[:, :],
                                            gbuf[dc][:, tc * 128:tc * 128 + 128],
                                            wslice(wsb, 1024, dc, oc * 512, oc * 512 + 512),
                                            start=(dc == 0), stop=(dc == 7))
                        s_mm.inc(inst)

                    @ctx.sec("vector")
                    def _(v, ti=ti, k=k, gt=gt, pb=pb, obuf=obuf):
                        s_mm.wait(v, k + 1)
                        if k >= 4:
                            s_st.wait(v, 16 * (k - 3))
                        if ti == 0:
                            s_ev.inc(v.tensor_copy(obuf[:], pb[:, :]))
                        else:
                            s_ev.inc(v.tensor_scalar_mul(obuf[:], pb[:, :],
                                                         fkc[:, gt:gt + 1]))

                    @ctx.sec("sync")
                    def _(sy, k=k, gt=gt, oc=oc, obuf=obuf, scr=scr):
                        s_ev.wait(sy, k + 1)
                        s_st.inc(sy.dma_start(
                            out=S[scr][gt * 128:(gt + 1) * 128, oc * 512:(oc + 1) * 512],
                            in_=obuf[:]), 16)




def ph3a_ret0(ctx):
    nc, ar, S = ctx.nc, ctx.ar, ctx.S
    ar.reset()

    qT_sb = ar.alloc([128, 16384], BF16)
    kT_sb = ar.alloc([128, 16384], BF16)
    mask_sb = ar.alloc([128, 2048], F32)
    cross_sb = ar.alloc([128, 1024], F32)
    kdec_sb = ar.alloc([128, 1024], F32)
    fq_sb = ar.alloc([128, 32], F32)
    fk_sb = ar.alloc([128, 32], F32)
    kbuf = [ar.alloc([128, 1024], BF16) for _ in range(2)]
    vbuf = [ar.alloc([128, 1024], BF16) for _ in range(2)]
    ksc = [ar.alloc([128, 1024], BF16) for _ in range(2)]
    qc_bf = ar.alloc([128, 1024], BF16)
    simb = ar.alloc([128, 2048], BF16)
    S_sb = ar.alloc([128, 512], F32)
    S_bf = ar.alloc([128, 1024], BF16)
    o_sb = [ar.alloc([128, 1024], BF16) for _ in range(2)]

    s_c = ctx.sem("p3_c")
    s_qk = ctx.sem("p3_qk")
    s_kv = ctx.sem("p3_kv")
    s_pre = ctx.sem("p3_pre")
    s_simm = ctx.sem("p3_simm")
    s_tin = ctx.sem("p3_tin")
    s_sev = ctx.sem("p3_sev")
    s_tch = ctx.sem("p3_tch")
    s_sst = ctx.sem("p3_sst")
    s_sbf = ctx.sem("p3_sbf")
    s_oev = ctx.sem("p3_oev")
    s_ost = ctx.sem("p3_ost")

    import math as _m
    gam = [1.0 - _m.exp(lo) for lo in
           [(_m.log(1 / 32.) + (_m.log(1 / 512.) - _m.log(1 / 32.)) * h / 15.0)
            for h in range(16)]]
    gC = [g ** 128 for g in gam]

    @ctx.sec("sync")
    def _(sy):
        s_c.inc(sy.dma_start(out=mask_sb[:], in_=S["mask0"]), 16)
        s_c.inc(sy.dma_start(out=cross_sb[:], in_=S["cross0"]), 16)
        s_c.inc(sy.dma_start(out=kdec_sb[:], in_=S["kdec0f"]), 16)
        s_c.inc(sy.dma_start(out=fq_sb[:], in_=S["fq0"]), 16)
        s_c.inc(sy.dma_start(out=fk_sb[:], in_=S["fk0"]), 16)

    for b in range(2):
        @ctx.sec("sync")
        def _(sy, b=b):
            if b == 1:
                s_tch.wait(sy, 16)
                s_pre.wait(sy, 16)
            src_q = S["qT0"].rearrange("(c p) t -> p c t", p=128)[:, :, b * 2048:(b + 1) * 2048]
            src_k = S["kT0"].rearrange("(c p) t -> p c t", p=128)[:, :, b * 2048:(b + 1) * 2048]
            s_qk.inc(sy.dma_start(out=qT_sb.rearrange("p (c t) -> p c t", c=8), in_=src_q), 16)
            s_qk.inc(sy.dma_start(out=kT_sb.rearrange("p (c t) -> p c t", c=8), in_=src_k), 16)

        @ctx.sec("vector")
        def _(v, b=b):
            if b == 1:
                s_tch.wait(v, 16)
            v.memset(S_sb[:], 0.0)
            s_sbf.inc(v.memset(S_bf[:], 0.0))

        for j in range(16):
            jj = b * 16 + j

            @ctx.sec("sync")
            def _(sy, jj=jj, j=j):
                if jj >= 2:
                    s_tch.wait(sy, jj - 1)
                buf = jj % 2
                s_kv.inc(sy.dma_start(out=kbuf[buf][:],
                                      in_=S["ktm0"][jj * 128:(jj + 1) * 128, :]), 16)
                s_kv.inc(sy.dma_start(out=vbuf[buf][:],
                                      in_=S["vtm0"][jj * 128:(jj + 1) * 128, :]), 16)

            @ctx.sec("vector")
            def _(v, b=b, j=j, jj=jj):
                buf = jj % 2
                s_kv.wait(v, 32 * (jj + 1))
                s_qk.wait(v, 32 * (b + 1))
                if jj == 0:
                    s_c.wait(v, 80)
                if jj >= 1:
                    s_tch.wait(v, jj)
                v.tensor_mul(ksc[buf][:], kbuf[buf][:], kdec_sb[:])
                for g in range(8):
                    v.tensor_mul(qc_bf[:, g * 128:(g + 1) * 128],
                                 qT_sb[:, g * 2048 + j * 128:g * 2048 + j * 128 + 128],
                                 cross_sb[:, g * 128:(g + 1) * 128])
                s_pre.inc(v.engine_nop())

            for h in range(16):
                @ctx.sec("tensor")
                def _(t, b=b, j=j, jj=jj, h=h):
                    if jj == 0 and h == 0:
                        s_c.wait(t, 80)
                    s_qk.wait(t, 32 * (b + 1))
                    hh = jj * 16 + h
                    if hh >= 4:
                        s_sev.wait(t, hh - 3)
                    p0 = 64 * (h % 2)
                    c = h // 2
                    col = c * 2048 + j * 128
                    s_simm.inc(t.matmul(
                        ctx.pp[4 + h % 4][:, 0:128],
                        kT_sb[p0:p0 + 64, col:col + 128],
                        qT_sb[p0:p0 + 64, col:col + 128],
                        start=True, stop=True))

            for h in range(16):
                @ctx.sec("vector")
                def _(v, jj=jj, h=h):
                    hh = jj * 16 + h
                    s_simm.wait(v, hh + 1)
                    if jj >= 1:
                        s_tin.wait(v, (jj - 1) * 16 + h + 1)
                    s_sev.inc(v.scalar_tensor_tensor(
                        simb[:, h * 128:(h + 1) * 128], ctx.pp[4 + h % 4][:, 0:128],
                        fk_sb[:, jj:jj + 1], mask_sb[:, h * 128:(h + 1) * 128],
                        ALU.mult, ALU.mult))

            sst_target = s_sst.rel

            @ctx.sec("tensor")
            def _(t, b=b, j=j, jj=jj, sst_target=sst_target):
                buf = jj % 2
                s_pre.wait(t, jj + 1)
                s_oev.wait(t, 2 * jj)
                last = None
                for h in range(16):
                    hh = jj * 16 + h
                    s_sev.wait(t, hh + 1)
                    pb = ctx.pp[0] if h < 8 else ctx.pp[1]
                    last = t.matmul(pb[:, (h % 8) * 64:(h % 8) * 64 + 64],
                                    simb[:, h * 128:(h + 1) * 128],
                                    vbuf[buf][:, h * 64:h * 64 + 64],
                                    start=(h % 8 == 0), stop=False, skip_group_check=True)
                    s_tin.inc(last)
                if j > 0:
                    s_sbf.wait(t, b * 17 + 1 + j)
                    for g in range(8):
                        pb = ctx.pp[0] if g < 4 else ctx.pp[1]
                        last = t.matmul(pb[:, (g % 4) * 128:(g % 4) * 128 + 128],
                                        qc_bf[:, g * 128:(g + 1) * 128],
                                        S_bf[:, g * 128:(g + 1) * 128],
                                        start=False, stop=(g % 4 == 3),
                                        skip_group_check=True)
                if j < 15:
                    s_sst.wait(t, sst_target)
                    for g in range(8):
                        t.matmul(ctx.pp[2][:, g * 64:g * 64 + 64],
                                 ksc[buf][:, g * 128:(g + 1) * 128],
                                 vbuf[buf][:, (2 * g) * 64:(2 * g) * 64 + 64],
                                 start=(g == 0), stop=True, skip_group_check=True)
                        last = t.matmul(
                            ctx.pp[3][:, g * 64:g * 64 + 64],
                            ksc[buf][:, g * 128:(g + 1) * 128],
                            vbuf[buf][:, (2 * g + 1) * 64:(2 * g + 1) * 64 + 64],
                            start=(g == 0), stop=True, skip_group_check=True)
                s_tch.inc(last)

            if j < 15:
                @ctx.sec("vector")
                def _(v, b=b, j=j, jj=jj):
                    s_tch.wait(v, jj + 1)
                    for g in range(8):
                        s_sst.inc(v.scalar_tensor_tensor(
                            S_sb[0:64, g * 64:g * 64 + 64], S_sb[0:64, g * 64:g * 64 + 64],
                            gC[2 * g], ctx.pp[2][0:64, g * 64:g * 64 + 64],
                            ALU.mult, ALU.add))
                        s_sst.inc(v.scalar_tensor_tensor(
                            S_sb[64:128, g * 64:g * 64 + 64], S_sb[64:128, g * 64:g * 64 + 64],
                            gC[2 * g + 1], ctx.pp[3][64:128, g * 64:g * 64 + 64],
                            ALU.mult, ALU.add))
                    for g in range(8):
                        v.tensor_copy(S_bf[0:64, g * 128:g * 128 + 64],
                                      S_sb[0:64, g * 64:g * 64 + 64])
                        v.tensor_copy(S_bf[64:128, g * 128 + 64:g * 128 + 128],
                                      S_sb[64:128, g * 64:g * 64 + 64])
                    s_sbf.inc(v.engine_nop())
            else:
                @ctx.sec("vector")
                def _(v, jj=jj):
                    s_tch.wait(v, jj + 1)
                    s_sbf.inc(v.engine_nop())

            @ctx.sec("vector")
            def _(v, jj=jj):
                s_tch.wait(v, jj + 1)
                if jj >= 2:
                    s_ost.wait(v, 16 * (jj - 1))
                ob = o_sb[jj % 2]
                v.tensor_scalar_mul(ob[:, 0:512], ctx.pp[0][:, :], fq_sb[:, jj:jj + 1])
                s_oev.inc(v.tensor_scalar_mul(ob[:, 512:1024], ctx.pp[1][:, :],
                                              fq_sb[:, jj:jj + 1]), 2)

            @ctx.sec("sync")
            def _(sy, jj=jj):
                s_oev.wait(sy, 2 * (jj + 1))
                s_ost.inc(sy.dma_start(out=S["otm0"][jj * 128:(jj + 1) * 128, :],
                                       in_=o_sb[jj % 2][:]), 16)



def ph3b_oproj0(ctx):
    nc, ar, S = ctx.nc, ctx.ar, ctx.S
    ar.reset()

    wo = ar.alloc([128, 8192], BF16)
    obbc = ar.alloc([128, 1024], F32)
    oT = [ar.alloc([128, 512], BF16) for _ in range(16)]
    xnt = [ar.alloc([128, 1024], F32) for _ in range(2)]
    s1f = [ar.alloc([128, 1024], F32) for _ in range(2)]
    s1b = [ar.alloc([128, 1024], BF16) for _ in range(2)]
    sq = [ar.alloc([128, 1024], F32) for _ in range(2)]
    gcat = [ar.alloc([128, 32], F32) for _ in range(2)]
    stat = ar.alloc([128, 8], F32)
    rfm = ar.alloc([128, 8], F32)
    mfm = ar.alloc([128, 8], F32)
    afm = ar.alloc([128, 8], F32)
    bfm = ar.alloc([128, 8], F32)

    s_w = ctx.sem("p3b_w")
    s_a = ctx.sem("p3b_a")
    s_x = ctx.sem("p3b_x")
    s_mm = ctx.sem("p3b_mm")
    s_ev = ctx.sem("p3b_ev")
    s_e2 = ctx.sem("p3b_e2")
    s_f2 = ctx.sem("p3b_f2")
    s_g2 = ctx.sem("p3b_g2")
    s_st = ctx.sem("p3b_st")
    sV2 = ctx.sem("p3b_V2")
    sC2 = ctx.sem("p3b_C2")
    sPM = ctx.sem("p3b_PM")
    sV3 = ctx.sem("p3b_V3")
    sAB = ctx.sem("p3b_AB")

    @ctx.sec("gpsimd")
    def _(g):
        load_wT(ctx, g, S["owT"], 1024, 1024, wo, s_w)

    @ctx.sec("sync")
    def _(sy):
        s_w.inc(sy.dma_start(out=obbc[:], in_=S["ob_bc"]), 16)

    pp6, pp7 = ctx.pp[6], ctx.pp[7]

    for i in range(NTILE):
        b = i // 16
        buf = i % 2
        first, last = i % 16 == 0, i % 16 == 15
        tb = i // 4

        if i % 4 == 0:
            @ctx.sec("sync")
            def _(sy, i=i, tb=tb):
                if tb >= 2:
                    s_mm.wait(sy, 2 * (tb - 1) * 4)
                for dc in range(8):
                    tload(ctx, sy, S["otm0"], tb * 512, 512, dc * 128, 128,
                          oT[(tb % 2) * 8 + dc], s_a)

        @ctx.sec("sync")
        def _(sy, i=i, buf=buf):
            if i >= 2:
                s_ev.wait(sy, i - 1)
            s_x.inc(sy.dma_start(out=xnt[buf][:],
                                 in_=S["xn0_f32"][i * 128:(i + 1) * 128, :]), 16)

        for oc in range(2):
            @ctx.sec("tensor")
            def _(t, i=i, tb=tb, oc=oc):
                if i == 0 and oc == 0:
                    s_w.wait(t, 32)
                s_a.wait(t, 128 * (tb + 1))
                k = 2 * i + oc
                if k >= 4:
                    s_ev.wait(t, i - 1)
                for dc in range(8):
                    inst = t.matmul(ctx.pp[k % 4][:, :],
                                    oT[(tb % 2) * 8 + dc][:, (i % 4) * 128:(i % 4) * 128 + 128],
                                    wslice(wo, 1024, dc, oc * 512, oc * 512 + 512),
                                    start=(dc == 0), stop=(dc == 7))
                s_mm.inc(inst)

        @ctx.sec("vector")
        def _(v, i=i, buf=buf):
            s_mm.wait(v, 2 * (i + 1))
            s_x.wait(v, 16 * (i + 1))
            if i == 0:
                s_w.wait(v, 32)
            if i >= 2:
                s_e2.wait(v, i - 1)
                s_st.wait(v, 32 * (i - 1))
                s_g2.wait(v, i - 1)
            for oc in range(2):
                sl = slice(oc * 512, oc * 512 + 512)
                v.tensor_add(s1f[buf][:, sl], ctx.pp[(2 * i + oc) % 4][:, :], obbc[:, sl])
                v.tensor_add(s1f[buf][:, sl], s1f[buf][:, sl], xnt[buf][:, sl])
            v.tensor_copy(s1b[buf][:], s1f[buf][:])
            s_ev.inc(v.reduce_sum(gcat[buf][:, 0:16], view3(s1f[buf], 16, 64), axis=AX.X))

        @ctx.sec("scalar")
        def _(sc, i=i, buf=buf):
            s_ev.wait(sc, i + 1)
            s_e2.inc(sc.activation(sq[buf][:], s1f[buf][:], AF.Square))

        @ctx.sec("vector")
        def _(v, i=i, buf=buf):
            s_e2.wait(v, i + 1)
            s_f2.inc(v.reduce_sum(gcat[buf][:, 16:32], view3(sq[buf], 16, 64), axis=AX.X))

        @ctx.sec("tensor")
        def _(t, i=i, buf=buf, first=first, last=last):
            s_f2.wait(t, i + 1)
            if i == 16:
                sV2.wait(t, 4)
            t.matmul(pp6[0:16, 0:1], gcat[buf][:, 0:16], ctx.consts["ones"][:, 0:1],
                     start=first, stop=last)
            s_g2.inc(t.matmul(pp7[0:16, 0:1], gcat[buf][:, 16:32], ctx.consts["ones"][:, 0:1],
                              start=first, stop=last))

        @ctx.sec("sync")
        def _(sy, i=i, buf=buf):
            s_ev.wait(sy, i + 1)
            s_st.inc(sy.dma_start(out=S["s1_f32"][i * 128:(i + 1) * 128, :], in_=s1f[buf][:]), 16)
            s_st.inc(sy.dma_start(out=S["s1_bf"][i * 128:(i + 1) * 128, :], in_=s1b[buf][:]), 16)

        if last:
            @ctx.sec("vector")
            def _(v, b=b):
                if b == 0:
                    v.wait_ge(ctx.cld.h, ctx.cld.n)
                s_g2.wait(v, 16 * (b + 1))
                if b == 1:
                    sPM.wait(v, 64)
                v.tensor_scalar_mul(stat[0:16, 0:1], pp6[0:16, 0:1], 1.0 / (64 * 2048))
                inst = v.tensor_scalar_mul(stat[0:16, 1:2], pp7[0:16, 0:1], 1.0 / (64 * 2048))
                ctx.fence(v, inst)
                inst = v.tensor_mul(stat[0:16, 2:3], stat[0:16, 0:1], stat[0:16, 0:1])
                ctx.fence(v, inst)
                inst = v.scalar_tensor_tensor(stat[0:16, 2:3], stat[0:16, 2:3], -1.0,
                                              stat[0:16, 1:2], ALU.mult, ALU.add)
                ctx.fence(v, inst)
                sV2.inc(v.tensor_scalar_add(stat[0:16, 2:3], stat[0:16, 2:3], EPS), 4)

            @ctx.sec("scalar")
            def _(sc, b=b):
                sV2.wait(sc, 4 * (b + 1))
                inst = sc.activation(stat[0:16, 3:4], stat[0:16, 2:3], AF.Ln)
                ctx.fence(sc, inst)
                sC2.inc(sc.activation(stat[0:16, 3:4], stat[0:16, 3:4], AF.Exp, scale=-0.5))

            @ctx.sec("sync")
            def _(sy, b=b):
                sC2.wait(sy, b + 1)
                sPM.inc(sy.dma_start(
                    out=S["ph3_stat"][b, 0].rearrange("(h c) -> c h", h=2, c=8),
                    in_=stat[0:16, 3:4], allow_slow_non_contiguous=True), 16)
                sPM.inc(sy.dma_start(
                    out=S["ph3_stat"][b, 1].rearrange("(h c) -> c h", h=2, c=8),
                    in_=stat[0:16, 0:1], allow_slow_non_contiguous=True), 16)
                sPM.wait(sy, 64 * b + 32)
                if b == 1:
                    sV3.wait(sy, 3)
                src_r = S["ph3_stat"][b, 0].rearrange("(h c) -> h c", h=2) \
                    .rearrange("h (c u) -> h u c", u=1).broadcast_to([2, 64, 8])
                src_m = S["ph3_stat"][b, 1].rearrange("(h c) -> h c", h=2) \
                    .rearrange("h (c u) -> h u c", u=1).broadcast_to([2, 64, 8])
                sPM.inc(sy.dma_start(out=rfm[:], in_=src_r), 16)
                sPM.inc(sy.dma_start(out=mfm[:], in_=src_m), 16)

            @ctx.sec("vector")
            def _(v, b=b):
                sPM.wait(v, 64 * (b + 1))
                if b == 1:
                    sAB.wait(v, 32)
                inst = v.tensor_mul(afm[:], rfm[:], ctx.consts["gn2_g"][:])
                ctx.fence(v, inst)
                inst = v.tensor_mul(bfm[:], mfm[:], afm[:])
                ctx.fence(v, inst)
                sV3.inc(v.scalar_tensor_tensor(bfm[:], bfm[:], -1.0, ctx.consts["gn2_b"][:],
                                               ALU.mult, ALU.add), 3)

            @ctx.sec("sync")
            def _(sy, b=b):
                sV3.wait(sy, 3 * (b + 1))
                sAB.inc(sy.dma_start(out=S["gn2ab"][b, 0], in_=afm[:]), 16)
                sAB.inc(sy.dma_start(out=S["gn2ab"][b, 1], in_=bfm[:]), 16)




def ph4_ffn0(ctx):
    nc, ar, S = ctx.nc, ctx.ar, ctx.S
    ar.reset()

    w1 = ar.alloc([128, 16384], BF16)
    w2 = ar.alloc([128, 16384], BF16)
    b1c = ar.alloc([128, 16], F32)
    b2bc = ar.alloc([128, 1024], F32)
    ab = ar.alloc([128, 32], F32)
    ga = [ar.alloc([128, 512], BF16) for _ in range(16)]
    hT = [ar.alloc([128, 8192], BF16) for _ in range(2)]
    s1t = [ar.alloc([128, 1024], F32) for _ in range(2)]
    ob = [ar.alloc([128, 512], F32) for _ in range(4)]
    obb = [ar.alloc([128, 1024], F32) for _ in range(2)]

    s_w = ctx.sem("p4_w")
    s_a = ctx.sem("p4_a")
    s_m1 = ctx.sem("p4_m1")
    s_h = ctx.sem("p4_h")
    s_m2 = ctx.sem("p4_m2")
    s_x = ctx.sem("p4_x")
    s_ev = ctx.sem("p4_ev")
    s_st = ctx.sem("p4_st")

    @ctx.sec("gpsimd")
    def _(g):
        load_wT(ctx, g, S["w1T"], 1024, 2048, w1, s_w)
        load_wT(ctx, g, S["w2T"], 2048, 1024, w2, s_w)

    @ctx.sec("sync")
    def _(sy):
        s_w.inc(sy.dma_start(out=b1c[:], in_=S["b1c"]), 16)
        s_w.inc(sy.dma_start(out=b2bc[:], in_=S["b2_bc"]), 16)
        for b in range(2):
            s_w.inc(sy.dma_start(out=ab[:, 16 * b:16 * b + 8], in_=S["gn2ab"][b, 0]), 16)
            s_w.inc(sy.dma_start(out=ab[:, 16 * b + 8:16 * b + 16], in_=S["gn2ab"][b, 1]), 16)

    for tb in range(8):
        bt = tb // 4
        gbuf = [ga[(tb % 2) * 8 + dc] for dc in range(8)]
        hbuf = hT[tb % 2]

        @ctx.sec("sync")
        def _(sy, tb=tb, gbuf=gbuf):
            if tb >= 2:
                s_m1.wait(sy, (tb - 1) * 16)
            for dc in range(8):
                tload(ctx, sy, S["s1_bf"], tb * 512, 512, dc * 128, 128, gbuf[dc], s_a)

        @ctx.sec("vector")
        def _(v, tb=tb, bt=bt, gbuf=gbuf):
            s_a.wait(v, (16 * 8 + 1) * tb + 16 * 8)
            if tb == 0:
                s_w.wait(v, 128)
            for dc in range(8):
                v.tensor_scalar(gbuf[dc][:], gbuf[dc][:],
                                ab[:, 16 * bt + dc:16 * bt + dc + 1],
                                ab[:, 16 * bt + 8 + dc:16 * bt + 9 + dc],
                                ALU.mult, ALU.add)
            s_a.inc(v.engine_nop())

        for oc in range(16):
            k = tb * 16 + oc
            pb = ctx.pp[oc % 4]

            @ctx.sec("tensor")
            def _(t, tb=tb, oc=oc, k=k, pb=pb, gbuf=gbuf):
                if k == 0:
                    s_w.wait(t, 32)
                s_a.wait(t, (16 * 8 + 1) * (tb + 1))
                if k >= 4:
                    s_h.wait(t, k - 3)
                for dc in range(8):
                    inst = t.matmul(pb[:, :],
                                    wslice(w1, 2048, dc, oc * 128, oc * 128 + 128),
                                    gbuf[dc][:], start=(dc == 0), stop=(dc == 7))
                s_m1.inc(inst)

            @ctx.sec("scalar")
            def _(sc, tb=tb, oc=oc, k=k, pb=pb, hbuf=hbuf):
                s_m1.wait(sc, k + 1)
                if tb >= 2 and oc == 0:
                    s_m2.wait(sc, 8 * (tb - 1))
                s_h.inc(sc.activation(hbuf[:, oc * 512:(oc + 1) * 512], pb[:, :],
                                      AF.Silu, bias=b1c[:, oc:oc + 1]))

        for tc in range(4):
            i = tb * 4 + tc
            sbuf = s1t[tc % 2]

            @ctx.sec("sync")
            def _(sy, i=i, tc=tc, sbuf=sbuf):
                if i >= 2:
                    s_ev.wait(sy, i - 1)
                s_x.inc(sy.dma_start(out=sbuf[:],
                                     in_=S["s1_f32"][i * 128:(i + 1) * 128, :]), 16)

            for oc2 in range(2):
                kk = i * 2 + oc2
                pb2 = ctx.pp[4 + kk % 4]

                @ctx.sec("tensor")
                def _(t, tb=tb, tc=tc, oc2=oc2, kk=kk, pb2=pb2, hbuf=hbuf):
                    s_h.wait(t, 16 * (tb + 1))
                    if kk >= 4:
                        s_ev.wait(t, (kk - 4) // 2 + 1)
                    for hc in range(16):
                        inst = t.matmul(pb2[:, :],
                                        hbuf[:, hc * 512 + tc * 128:hc * 512 + tc * 128 + 128],
                                        wslice(w2, 1024, hc, oc2 * 512, oc2 * 512 + 512),
                                        start=(hc == 0), stop=(hc == 15))
                    s_m2.inc(inst)

            @ctx.sec("vector")
            def _(v, i=i, tc=tc, sbuf=sbuf):
                s_m2.wait(v, 2 * (i + 1))
                s_x.wait(v, 16 * (i + 1))
                if i >= 2:
                    s_st.wait(v, 16 * (i - 1))
                o = obb[i % 2]
                for oc2 in range(2):
                    kk = i * 2 + oc2
                    sl = slice(oc2 * 512, oc2 * 512 + 512)
                    v.tensor_add(o[:, sl], ctx.pp[4 + kk % 4][:, :], b2bc[:, sl])
                    v.tensor_add(o[:, sl], o[:, sl], sbuf[:, sl])
                s_ev.inc(v.engine_nop())

            @ctx.sec("sync")
            def _(sy, i=i):
                s_ev.wait(sy, i + 1)
                s_st.inc(sy.dma_start(out=S["sbr_f32"][i * 128:(i + 1) * 128, :],
                                      in_=obb[i % 2][:]), 16)




def ph5a_qkv(ctx):
    nc, ar, S = ctx.nc, ctx.ar, ctx.S
    ar.reset()

    wq = ar.alloc([128, 4096], BF16)
    wk = ar.alloc([128, 4096], BF16)
    wv = ar.alloc([128, 4096], BF16)
    qb = ar.alloc([128, 4], F32)
    kb = ar.alloc([128, 4], F32)
    vbb = ar.alloc([128, 512], F32)
    ga = [ar.alloc([128, 512], BF16) for _ in range(16)]
    ob = [ar.alloc([128, 512], BF16) for _ in range(4)]

    s_w = ctx.sem("p5a_w")
    s_a = ctx.sem("p5a_a")
    s_mm = ctx.sem("p5a_mm")
    s_ev = ctx.sem("p5a_ev")
    s_st = ctx.sem("p5a_st")

    @ctx.sec("gpsimd")
    def _(g):
        load_wT(ctx, g, S["f_qwsl"], 1024, 512, wq, s_w)
        load_wT(ctx, g, S["f_kw"], 1024, 512, wk, s_w)
        load_wT(ctx, g, S["f_vw"], 1024, 512, wv, s_w)

    @ctx.sec("sync")
    def _(sy):
        s_w.inc(sy.dma_start(out=qb[:], in_=S["f_qbc"]), 16)
        s_w.inc(sy.dma_start(out=kb[:], in_=S["f_kbc"]), 16)
        s_w.inc(sy.dma_start(out=vbb[:], in_=S["f_vb_bc"]), 16)

    for tb in range(8):
        gbuf = [ga[(tb % 2) * 8 + dc] for dc in range(8)]

        @ctx.sec("sync")
        def _(sy, tb=tb, gbuf=gbuf):
            if tb >= 2:
                s_mm.wait(sy, (tb - 1) * 12)
            for dc in range(8):
                tload(ctx, sy, S["xn0_bf"], tb * 512, 512, dc * 128, 128, gbuf[dc], s_a)

        for ti, (wsb, scr, bcol) in enumerate([(wq, "fQT", qb), (wk, "fKT", kb)]):
            for oc in range(4):
                k = tb * 12 + ti * 4 + oc
                pb = ctx.pp[k % 2]
                obuf = ob[k % 4]

                @ctx.sec("tensor")
                def _(t, tb=tb, k=k, oc=oc, pb=pb, wsb=wsb):
                    if k == 0:
                        s_w.wait(t, 96)
                    s_a.wait(t, 16 * 8 * (tb + 1))
                    if k >= 2:
                        s_ev.wait(t, k - 1)
                    for dc in range(8):
                        inst = t.matmul(pb[:, :],
                                        wslice(wsb, 512, dc, oc * 128, oc * 128 + 128),
                                        gbuf[dc][:], start=(dc == 0), stop=(dc == 7))
                    s_mm.inc(inst)

                @ctx.sec("vector")
                def _(v, k=k, oc=oc, pb=pb, obuf=obuf, bcol=bcol):
                    s_mm.wait(v, k + 1)
                    if k >= 4:
                        s_st.wait(v, 16 * (k - 3))
                    s_ev.inc(v.tensor_scalar_add(obuf[:], pb[:, :], bcol[:, oc:oc + 1]))

                @ctx.sec("gpsimd")
                def _(g, tb=tb, k=k, oc=oc, obuf=obuf, scr=scr):
                    s_ev.wait(g, k + 1)
                    s_st.inc(g.dma_start(
                        out=S[scr][oc * 128:(oc + 1) * 128, tb * 512:(tb + 1) * 512],
                        in_=obuf[:]), 16)

        for tc in range(4):
            gt = tb * 4 + tc
            k = tb * 12 + 8 + tc
            pb = ctx.pp[k % 2]
            obuf = ob[k % 4]

            @ctx.sec("tensor")
            def _(t, tb=tb, tc=tc, k=k, pb=pb):
                s_a.wait(t, 16 * 8 * (tb + 1))
                if k >= 2:
                    s_ev.wait(t, k - 1)
                for dc in range(8):
                    inst = t.matmul(pb[:, :],
                                    gbuf[dc][:, tc * 128:tc * 128 + 128],
                                    wslice(wv, 512, dc, 0, 512),
                                    start=(dc == 0), stop=(dc == 7))
                s_mm.inc(inst)

            @ctx.sec("vector")
            def _(v, k=k, pb=pb, obuf=obuf):
                s_mm.wait(v, k + 1)
                if k >= 4:
                    s_st.wait(v, 16 * (k - 3))
                s_ev.inc(v.tensor_add(obuf[:], pb[:, :], vbb[:]))

            @ctx.sec("gpsimd")
            def _(g, gt=gt, k=k, obuf=obuf):
                s_ev.wait(g, k + 1)
                s_st.inc(g.dma_start(out=S["fVtm"][gt * 128:(gt + 1) * 128, :],
                                      in_=obuf[:]), 16)



def ph5b_attn(ctx):
    nc, ar, S = ctx.nc, ctx.ar, ctx.S
    ar.reset()
    import math as _m
    scl = 1.0 / _m.sqrt(32.0)

    QT = ar.alloc([128, 8192], BF16)
    KT = ar.alloc([128, 8192], BF16)
    VA = ar.alloc([128, 8448], BF16)
    vtmp = [ar.alloc([128, 512], BF16) for _ in range(2)]
    ex = [ar.alloc([128, 8192], BF16) for _ in range(2)]
    rec = ar.alloc([128, 512], F32)
    aos = ar.alloc([128, 512], F32)
    aob = [ar.alloc([128, 512], BF16) for _ in range(2)]
    onesr = ar.alloc([128, 128], F32)

    s_qk = ctx.sem("p5b_qk")
    s_va = ctx.sem("p5b_va")
    s_vc = ctx.sem("p5b_vc")
    s_sm = ctx.sem("p5b_sm")
    s_ex = ctx.sem("p5b_ex")
    s_av = ctx.sem("p5b_av")
    s_rc = ctx.sem("p5b_rc")
    s_mu = ctx.sem("p5b_mu")
    s_st = ctx.sem("p5b_st")

    @ctx.sec("vector")
    def _(v):
        v.memset(VA[:], 1.0)
        v.memset(onesr[:], 1.0)
        s_va.inc(v.engine_nop())

    for b in range(2):
        @ctx.sec("sync")
        def _(sy, b=b):
            if b == 1:
                s_sm.wait(sy, 1024)
                s_av.wait(sy, 64)
            src_q = S["fQT"].rearrange("(c p) t -> p c t", p=128)[:, :, b * 2048:(b + 1) * 2048]
            src_k = S["fKT"].rearrange("(c p) t -> p c t", p=128)[:, :, b * 2048:(b + 1) * 2048]
            s_qk.inc(sy.dma_start(out=QT.rearrange("p (c t) -> p c t", c=4), in_=src_q), 16)
            s_qk.inc(sy.dma_start(out=KT.rearrange("p (c t) -> p c t", c=4), in_=src_k), 16)

        for tc in range(16):
            @ctx.sec("sync")
            def _(sy, b=b, tc=tc):
                ci = b * 16 + tc
                if ci >= 2:
                    s_va.wait(sy, ci)
                s_vc.inc(sy.dma_start(out=vtmp[tc % 2][:],
                                      in_=S["fVtm"][ci * 128:(ci + 1) * 128, :]), 16)

            @ctx.sec("gpsimd")
            def _(g, b=b, tc=tc):
                ci = b * 16 + tc
                s_vc.wait(g, 16 * (ci + 1))
                if b == 1 and tc == 0:
                    s_av.wait(g, 64)
                for h in range(16):
                    inst = g.tensor_copy(VA[:, tc * 528 + h * 33:tc * 528 + h * 33 + 32],
                                         vtmp[tc % 2][:, h * 32:(h + 1) * 32])
                s_va.inc(inst)

        for h in range(16):
            p0 = 32 * (h % 4)
            c = h // 4
            for t1b in range(4):
                hh = b * 64 + h * 4 + t1b
                ebuf = ex[hh % 2]

                @ctx.sec("tensor")
                def _(t, b=b, h=h, t1b=t1b, hh=hh, p0=p0, c=c):
                    s_qk.wait(t, 32 * (b + 1))
                    if hh >= 1:
                        s_ex.wait(t, 16 * hh)
                    for t2c in range(16):
                        if t2c >= 4:
                            s_ex.wait(t, 16 * hh + t2c - 3)
                        s_sm.inc(t.matmul(
                            ctx.pp[t2c % 4][:, :],
                            KT[p0:p0 + 32, c * 2048 + t2c * 128:c * 2048 + (t2c + 1) * 128],
                            QT[p0:p0 + 32, c * 2048 + t1b * 512:c * 2048 + (t1b + 1) * 512],
                            start=True, stop=True,
                            tile_position=(p0, 0)))

                @ctx.sec("scalar")
                def _(sc, hh=hh, ebuf=ebuf):
                    if hh >= 2:
                        s_av.wait(sc, hh - 1)
                    for t2c in range(16):
                        s_sm.wait(sc, 16 * hh + t2c + 1)
                        s_ex.inc(sc.activation(ebuf[:, t2c * 512:(t2c + 1) * 512],
                                               ctx.pp[t2c % 4][:, :], AF.Exp, scale=scl))

                @ctx.sec("tensor")
                def _(t, b=b, h=h, hh=hh, ebuf=ebuf):
                    s_va.wait(t, 1 + 16 * (b + 1))
                    if hh >= 2:
                        s_mu.wait(t, hh - 1)
                    for t2c in range(16):
                        s_ex.wait(t, 16 * hh + t2c + 1)
                        inst = t.matmul(
                            ctx.pp[4 + hh % 2][0:33, :],
                            VA[:, t2c * 528 + h * 33:t2c * 528 + (h + 1) * 33],
                            ebuf[:, t2c * 512:(t2c + 1) * 512],
                            start=(t2c == 0), stop=(t2c == 15))
                    s_av.inc(inst)

                @ctx.sec("vector")
                def _(v, hh=hh):
                    s_av.wait(v, hh + 1)
                    if hh >= 1:
                        s_rc.wait(v, 2 * hh)
                    inst = v.reciprocal(rec[0:1, :], ctx.pp[4 + hh % 2][32:33, :])
                    ctx.fence(v, inst)
                    s_rc.inc(v.engine_nop())

                @ctx.sec("tensor")
                def _(t, hh=hh):
                    s_rc.wait(t, 2 * hh + 1)
                    if hh >= 1:
                        s_mu.wait(t, hh)
                    s_rc.inc(t.matmul(ctx.pp[6][0:32, :], onesr[0:1, 0:32], rec[0:1, :],
                                      start=True, stop=True))

                @ctx.sec("vector")
                def _(v, hh=hh):
                    s_rc.wait(v, 2 * (hh + 1))
                    if hh >= 2:
                        s_st.wait(v, 16 * (hh - 1))
                    v.tensor_copy(aos[0:32, :], ctx.pp[6][0:32, :])
                    s_mu.inc(v.tensor_mul(aob[hh % 2][0:32, :],
                                          ctx.pp[4 + hh % 2][0:32, :], aos[0:32, :]))

                @ctx.sec("sync")
                def _(sy, b=b, h=h, t1b=t1b, hh=hh):
                    s_mu.wait(sy, hh + 1)
                    s_st.inc(sy.dma_start(
                        out=S["aoTs"][h * 32:(h + 1) * 32,
                                      b * 2048 + t1b * 512:b * 2048 + (t1b + 1) * 512],
                        in_=aob[hh % 2][0:32, :]), 16)





def ph5c_mla(ctx):
    nc, ar, S = ctx.nc, ctx.ar, ctx.S
    ar.reset()

    ao = ar.alloc([128, 16384], BF16)
    ow = ar.alloc([128, 4096], BF16)
    obc = ar.alloc([128, 1024], F32)
    mt = [ar.alloc([128, 1024], BF16) for _ in range(2)]

    s_w = ctx.sem("p5c_w")
    s_mm = ctx.sem("p5c_mm")
    s_ev = ctx.sem("p5c_ev")
    s_st = ctx.sem("p5c_st")

    @ctx.sec("gpsimd")
    def _(g):
        load_wT(ctx, g, S["f_ow"], 512, 1024, ow, s_w)

    @ctx.sec("sync")
    def _(sy):
        s_w.inc(sy.dma_start(out=ao.rearrange("p (c t) -> p c t", c=4),
                             in_=S["aoTs"].rearrange("(c p) t -> p c t", p=128)), 16)
        s_w.inc(sy.dma_start(out=obc[:], in_=S["f_ob_bc"]), 16)

    for i in range(NTILE):
        buf = i % 2
        for oc in range(2):
            k = 2 * i + oc

            @ctx.sec("tensor")
            def _(t, i=i, oc=oc, k=k):
                if k == 0:
                    s_w.wait(t, 48)
                if k >= 4:
                    s_ev.wait(t, i - 1)
                for fc in range(4):
                    inst = t.matmul(ctx.pp[k % 4][:, :],
                                    ao[:, fc * 4096 + i * 128:fc * 4096 + i * 128 + 128],
                                    wslice(ow, 1024, fc, oc * 512, oc * 512 + 512),
                                    start=(fc == 0), stop=(fc == 3))
                s_mm.inc(inst)

        @ctx.sec("vector")
        def _(v, i=i, buf=buf):
            s_mm.wait(v, 2 * (i + 1))
            if i >= 2:
                s_st.wait(v, 16 * (i - 1))
            for oc in range(2):
                sl = slice(oc * 512, oc * 512 + 512)
                v.tensor_add(mt[buf][:, sl], ctx.pp[(2 * i + oc) % 4][:, :], obc[:, sl])
            s_ev.inc(v.engine_nop())

        @ctx.sec("sync")
        def _(sy, i=i, buf=buf):
            s_ev.wait(sy, i + 1)
            s_st.inc(sy.dma_start(out=S["mla_tm"][i * 128:(i + 1) * 128, :],
                                  in_=mt[buf][:]), 16)



def ph5d_router(ctx):
    nc, ar, S = ctx.nc, ctx.ar, ctx.S
    ar.reset()

    rw = ar.alloc([128, 64], BF16)
    rbbc = ar.alloc([128, 8], F32)
    ga = [ar.alloc([128, 512], BF16) for _ in range(16)]
    lg = ar.alloc([128, 256], F32)
    ee = ar.alloc([128, 256], F32)
    e2 = ar.alloc([128, 256], F32)
    m1 = ar.alloc([128, 32], F32)
    m2 = ar.alloc([128, 32], F32)
    den = ar.alloc([128, 32], F32)
    cw = ar.alloc([128, 256], F32)

    s_w = ctx.sem("p5d_w")
    s_a = ctx.sem("p5d_a")
    s_mm = ctx.sem("p5d_mm")
    s_sx = ctx.sem("p5d_sx")
    s_st = ctx.sem("p5d_st")

    @ctx.sec("gpsimd")
    def _(g):
        load_wT(ctx, g, S["f_rw"], 1024, 8, rw, s_w)

    @ctx.sec("sync")
    def _(sy):
        s_w.inc(sy.dma_start(out=rbbc[:], in_=S["f_rb_bc"]), 16)

    for tb in range(8):
        gbuf = [ga[(tb % 2) * 8 + dc] for dc in range(8)]

        @ctx.sec("sync")
        def _(sy, tb=tb, gbuf=gbuf):
            if tb >= 2:
                s_mm.wait(sy, 4 * (tb - 1))
            for dc in range(8):
                tload(ctx, sy, S["mla_tm"], tb * 512, 512, dc * 128, 128, gbuf[dc], s_a)

        for tc in range(4):
            i = tb * 4 + tc

            @ctx.sec("tensor")
            def _(t, tb=tb, tc=tc, i=i, gbuf=gbuf):
                if i == 0:
                    s_w.wait(t, 32)
                s_a.wait(t, 16 * 8 * (tb + 1))
                for dc in range(8):
                    inst = t.matmul(ctx.pp[4][:, i * 8:(i + 1) * 8],
                                    gbuf[dc][:, tc * 128:tc * 128 + 128],
                                    wslice(rw, 8, dc, 0, 8),
                                    start=(i == 0 and dc == 0), stop=(dc == 7),
                                    skip_group_check=True)
                s_mm.inc(inst)

    v3 = lambda ap: ap.rearrange("p (g c) -> p g c", c=8)
    bc = lambda ap: ap.rearrange("p (g u) -> p g u", u=1).broadcast_to([128, 32, 8])

    @ctx.sec("vector")
    def _(v):
        s_mm.wait(v, 32)
        inst = v.tensor_tensor(v3(lg[:]), v3(ctx.pp[4][:, 0:256]),
                        rbbc[:].rearrange("p (u c) -> p u c", u=1).broadcast_to([128, 32, 8]),
                        ALU.add)
        ctx.fence(v, inst)
        inst = v.tensor_reduce(m1[:], v3(lg[:]), axis=AX.X, op=ALU.max, negate=True)
        ctx.fence(v, inst)
        s_sx.inc(v.tensor_tensor(v3(ee[:]), v3(lg[:]), bc(m1[:]), ALU.add))

    @ctx.sec("scalar")
    def _(sc):
        s_sx.wait(sc, 1)
        s_sx.inc(sc.activation(ee[:], ee[:], AF.Exp))

    @ctx.sec("vector")
    def _(v):
        s_sx.wait(v, 2)
        inst = v.reduce_sum(den[:], v3(ee[:]), axis=AX.X)
        ctx.fence(v, inst)
        inst = v.tensor_scalar_mul(den[:], den[:], 1e-8)
        ctx.fence(v, inst)
        inst = v.reduce_max(m1[:], v3(ee[:]), axis=AX.X)
        ctx.fence(v, inst)
        v.tensor_tensor(v3(cw[:]), v3(ee[:]), bc(m1[:]), ALU.is_ge)
        inst = v.scalar_tensor_tensor(e2[:], cw[:], -1e6, ee[:], ALU.mult, ALU.add)
        ctx.fence(v, inst)
        inst = v.reduce_max(m2[:], v3(e2[:]), axis=AX.X)
        ctx.fence(v, inst)
        v.tensor_tensor(v3(e2[:]), v3(e2[:]), bc(m2[:]), ALU.is_ge)
        inst = v.tensor_add(den[:], den[:], m1[:])
        ctx.fence(v, inst)
        inst = v.tensor_add(den[:], den[:], m2[:])
        ctx.fence(v, inst)
        inst = v.reciprocal(den[:], den[:])
        ctx.fence(v, inst)
        v.tensor_add(cw[:], cw[:], e2[:])
        v.tensor_mul(cw[:], cw[:], ee[:])
        s_sx.inc(v.tensor_tensor(v3(cw[:]), v3(cw[:]), bc(den[:]), ALU.mult))

    @ctx.sec("sync")
    def _(sy):
        s_sx.wait(sy, 3)
        s_st.inc(sy.dma_start(out=S["cmb0"], in_=cw[:]), 16)



def ph5e_moe0(ctx):
    nc, ar, S = ctx.nc, ctx.ar, ctx.S
    ar.reset()

    w1sb = [ar.alloc([128, 16384], BF16) for _ in range(2)]
    w2sb = [ar.alloc([128, 16384], BF16)]
    b1c = [ar.alloc([128, 16], F32) for _ in range(2)]
    b2r = [ar.alloc([128, 1024], BF16) for _ in range(2)]
    onesb = ar.alloc([128, 128], BF16)
    cwsb = ar.alloc([128, 256], F32)
    ga = [ar.alloc([128, 512], BF16) for _ in range(24)]
    hT = [ar.alloc([128, 8192], BF16) for _ in range(2)]
    tmp = [ar.alloc([128, 512], F32) for _ in range(4)]
    zt = ar.alloc([128, 1024], F32)

    s_w = ctx.sem("p5e_w")
    s_z = ctx.sem("p5e_z")
    s_a = ctx.sem("p5e_a")
    s_a2 = ctx.sem("p5e_a2")
    s_m1 = ctx.sem("p5e_m1")
    s_h = ctx.sem("p5e_h")
    s_m2 = ctx.sem("p5e_m2")
    s_ev = ctx.sem("p5e_ev")
    s_ac = ctx.sem("p5e_ac")

    @ctx.sec("vector")
    def _(v):
        v.memset(zt[:], 0.0)
        v.memset(onesb[:], 1.0)
        s_z.inc(v.engine_nop())

    @ctx.sec("sync")
    def _(sy):
        s_z.wait(sy, 1)
        for i in range(NTILE):
            s_z.inc(sy.dma_start(out=S["moe0"][i * 128:(i + 1) * 128, :], in_=zt[:]), 16)
        s_z.inc(sy.dma_start(out=cwsb[:], in_=S["cmb0"]), 16)

    for e in range(8):
        eb = e % 2

        @ctx.sec("gpsimd")
        def _(g, e=e, eb=eb):
            if e >= 2:
                s_m1.wait(g, 128 * (e - 1))
            if e >= 1:
                s_m2.wait(g, 8 * 8 * e)
            load_wT(ctx, g, S["f_w1"][e], 1024, 2048, w1sb[eb], s_w)
            load_wT(ctx, g, S["f_w2"][e], 2048, 1024, w2sb[0], s_w)
            s_w.inc(g.dma_start(out=b2r[eb][0:1, :],
                                in_=S["f_b2"][e].rearrange("(u f) -> u f", u=1)), 16)

        @ctx.sec("sync")
        def _(sy, e=e, eb=eb):
            if e >= 2:
                s_h.wait(sy, 128 * (e - 1))
            s_w.inc(sy.dma_start(out=b1c[eb][:], in_=S["f_b1c"][e]), 16)

        for tb in range(8):
            gbuf = [ga[(tb % 2) * 8 + dc] for dc in range(8)]
            hbuf = hT[0]
            et = e * 8 + tb

            @ctx.sec("sync")
            def _(sy, e=e, tb=tb, et=et, gbuf=gbuf):
                if et >= 2:
                    s_m1.wait(sy, (et - 1) * 16)
                for dc in range(8):
                    tload(ctx, sy, S["mla_tm"], tb * 512, 512, dc * 128, 128, gbuf[dc], s_a)

            for oc in range(16):
                k = et * 16 + oc
                pb = ctx.pp[oc % 4]

                @ctx.sec("tensor")
                def _(t, e=e, eb=eb, tb=tb, et=et, oc=oc, k=k, pb=pb, gbuf=gbuf):
                    s_w.wait(t, 64 * (e + 1))
                    s_a.wait(t, 16 * 8 * (et + 1))
                    if k >= 4:
                        s_h.wait(t, k - 3)
                    for dc in range(8):
                        inst = t.matmul(pb[:, :],
                                        wslice(w1sb[eb], 2048, dc, oc * 128, oc * 128 + 128),
                                        gbuf[dc][:], start=(dc == 0), stop=(dc == 7))
                    s_m1.inc(inst)

                @ctx.sec("scalar")
                def _(sc, eb=eb, tb=tb, et=et, oc=oc, k=k, pb=pb, hbuf=hbuf):
                    s_m1.wait(sc, k + 1)
                    if et >= 2 and oc == 0:
                        s_m2.wait(sc, 8 * (et - 1))
                    s_h.inc(sc.activation(hbuf[:, oc * 512:(oc + 1) * 512], pb[:, :],
                                          AF.Gelu, bias=b1c[eb][:, oc:oc + 1]))

            for tc in range(4):
                i = tb * 4 + tc
                for oc2 in range(2):
                    kk = et * 8 + tc * 2 + oc2
                    pb2 = ctx.pp[4 + kk % 4]

                    @ctx.sec("tensor")
                    def _(t, eb=eb, tb=tb, tc=tc, oc2=oc2, et=et, kk=kk, pb2=pb2, hbuf=hbuf):
                        s_h.wait(t, 16 * (et + 1))
                        if kk >= 4:
                            s_ev.wait(t, kk - 3)
                        for hc in range(16):
                            t.matmul(pb2[:, :],
                                     hbuf[:, hc * 512 + tc * 128:hc * 512 + tc * 128 + 128],
                                     wslice(w2sb[0], 1024, hc, oc2 * 512, oc2 * 512 + 512),
                                     start=(hc == 0), stop=False, skip_group_check=True)
                        s_m2.inc(t.matmul(pb2[:, :], onesb[0:1, :],
                                          b2r[eb][0:1, oc2 * 512:(oc2 + 1) * 512],
                                          start=False, stop=True, skip_group_check=True))

                    @ctx.sec("vector")
                    def _(v, e=e, i=i, kk=kk, pb2=pb2):
                        s_m2.wait(v, kk + 1)
                        if kk == 0:
                            s_z.wait(v, 1 + 16 * NTILE + 16)
                        if kk >= 4:
                            s_ac.wait(v, 16 * (kk - 3))
                        s_ev.inc(v.tensor_scalar_mul(tmp[kk % 4][:], pb2[:, :],
                                                     cwsb[:, i * 8 + e:i * 8 + e + 1]))

                    @ctx.sec("gpsimd")
                    def _(g, i=i, oc2=oc2, kk=kk):
                        s_ev.wait(g, kk + 1)
                        if kk == 0:
                            s_z.wait(g, 1 + 16 * NTILE)
                        s_ac.inc(g.dma_start(
                            out=S["moe0"][i * 128:(i + 1) * 128, oc2 * 512:(oc2 + 1) * 512],
                            in_=tmp[kk % 4][:], accum_op=ALU.add), 16)



def ph5f_comb0(ctx):
    nc, ar, S = ctx.nc, ctx.ar, ctx.S
    ar.reset()

    xt = [ar.alloc([128, 1024], F32) for _ in range(4)]
    xnt = [ar.alloc([128, 1024], F32) for _ in range(4)]
    sbt = [ar.alloc([128, 1024], F32) for _ in range(4)]
    mot = [ar.alloc([128, 1024], F32) for _ in range(4)]
    gt = [ar.alloc([128, 1024], F32) for _ in range(4)]
    x1t = [ar.alloc([128, 1024], F32) for _ in range(4)]

    s_l = ctx.sem("p5f_l")
    s_lg = ctx.sem("p5f_lg")
    s_v1 = ctx.sem("p5f_v1")
    s_si = ctx.sem("p5f_si")
    s_v2 = ctx.sem("p5f_v2")
    s_st = ctx.sem("p5f_st")

    for i in range(NTILE):
        buf = i % 4
        rows = slice(i * 128, (i + 1) * 128)

        @ctx.sec("sync")
        def _(sy, i=i, buf=buf, rows=rows):
            if i >= 4:
                s_v2.wait(sy, i - 3)
                s_si.wait(sy, i - 3)
            s_l.inc(sy.dma_start(out=xt[buf][:], in_=S["x"][rows, :]), 16)
            s_l.inc(sy.dma_start(out=xnt[buf][:], in_=S["xn0_f32"][rows, :]), 16)

        @ctx.sec("gpsimd")
        def _(g, i=i, buf=buf, rows=rows):
            if i >= 4:
                s_v2.wait(g, i - 3)
                s_si.wait(g, i - 3)
            s_lg.inc(g.dma_start(out=sbt[buf][:], in_=S["sbr_f32"][rows, :]), 16)
            s_lg.inc(g.dma_start(out=mot[buf][:], in_=S["moe0"][rows, :]), 16)

        @ctx.sec("vector")
        def _(v, i=i, buf=buf):
            s_l.wait(v, 32 * (i + 1))
            s_lg.wait(v, 32 * (i + 1))
            if i >= 4:
                s_si.wait(v, i - 3)
            s_v1.inc(v.scalar_tensor_tensor(gt[buf][:], mot[buf][:], 0.3, xnt[buf][:],
                                            ALU.mult, ALU.add))

        @ctx.sec("scalar")
        def _(sc, i=i, buf=buf):
            s_v1.wait(sc, i + 1)
            s_si.inc(sc.activation(gt[buf][:], gt[buf][:], AF.Silu))

        @ctx.sec("vector")
        def _(v, i=i, buf=buf):
            s_si.wait(v, i + 1)
            if i >= 4:
                s_st.wait(v, 16 * (i - 3))
            v.tensor_mul(x1t[buf][:], sbt[buf][:], gt[buf][:])
            s_v2.inc(v.tensor_add(x1t[buf][:], x1t[buf][:], xt[buf][:]))

        @ctx.sec("sync")
        def _(sy, i=i, buf=buf, rows=rows):
            s_v2.wait(sy, i + 1)
            s_st.inc(sy.dma_start(out=S["x1_f32"][rows, :], in_=x1t[buf][:]), 16)



def ph6_ln1(ctx):
    nc, ar, S = ctx.nc, ctx.ar, ctx.S
    ar.reset()

    xt = [ar.alloc([128, 1024], F32) for _ in range(2)]
    dump = ar.alloc([128, 1024], F32)
    xnb = [ar.alloc([128, 1024], BF16) for _ in range(2)]
    stats = ar.alloc([128, 128], F32)
    statq = ar.alloc([128, 128], F32)
    vv = ar.alloc([128, 32], F32)
    rstd = ar.alloc([128, 32], F32)

    s_x = ctx.sem("p6_x")
    sA = ctx.sem("p6_A")
    sSB = ctx.sem("p6_SB")
    s_x2 = ctx.sem("p6_x2")
    sD = ctx.sem("p6_D")
    sST = ctx.sem("p6_ST")

    for i in range(NTILE):
        buf = i % 2

        @ctx.sec("sync")
        def _(sy, i=i, buf=buf):
            if i >= 2:
                sA.wait(sy, i - 1)
            s_x.inc(sy.dma_start(out=xt[buf][:], in_=S["x1_f32"][i * 128:(i + 1) * 128, :]), 16)

        @ctx.sec("scalar")
        def _(sc, i=i, buf=buf):
            s_x.wait(sc, 16 * (i + 1))
            sc.activation(dump[:], xt[buf][:], AF.Identity, scale=1.0 / 1024,
                          accum_out=stats[:, 4 * i:4 * i + 1])
            sA.inc(sc.activation(dump[:], xt[buf][:], AF.Square, scale=1.0 / 32,
                                 accum_out=statq[:, 4 * i:4 * i + 1]))

    @ctx.sec("vector")
    def _(v):
        sA.wait(v, NTILE)
        muv = view3(stats, 32, 4)[:, :, 0:1]
        msqv = view3(statq, 32, 4)[:, :, 0:1]
        inst = v.tensor_mul(vv[:].rearrange("p (g c) -> p g c", c=1), muv, muv)
        ctx.fence(v, inst)
        inst = v.scalar_tensor_tensor(vv[:].rearrange("p (g c) -> p g c", c=1),
                                      vv[:].rearrange("p (g c) -> p g c", c=1),
                                      -1.0, msqv, ALU.mult, ALU.add)
        ctx.fence(v, inst)
        sA.inc(v.tensor_scalar_add(vv[:], vv[:], EPS))

    @ctx.sec("scalar")
    def _(sc):
        sA.wait(sc, NTILE + 1)
        inst = sc.activation(rstd[:], vv[:], AF.Ln)
        ctx.fence(sc, inst)
        sSB.inc(sc.activation(rstd[:], rstd[:], AF.Exp, scale=-0.5))

    for i in range(NTILE):
        buf = i % 2

        @ctx.sec("sync")
        def _(sy, i=i, buf=buf):
            if i < 2:
                sA.wait(sy, NTILE)
            else:
                sD.wait(sy, i - 1)
            s_x2.inc(sy.dma_start(out=xt[buf][:], in_=S["x1_f32"][i * 128:(i + 1) * 128, :]), 16)

        @ctx.sec("vector")
        def _(v, i=i, buf=buf):
            s_x2.wait(v, 16 * (i + 1))
            sSB.wait(v, 1)
            if i >= 2:
                sST.wait(v, 16 * (i - 1))
            sD.inc(v.tensor_scalar(xnb[buf][:], xt[buf][:], stats[:, 4 * i:4 * i + 1],
                                   rstd[:, i:i + 1], ALU.subtract, ALU.mult))

        @ctx.sec("sync")
        def _(sy, i=i, buf=buf):
            sD.wait(sy, i + 1)
            sST.inc(sy.dma_start(out=S["xn1_bf"][i * 128:(i + 1) * 128, :], in_=xnb[buf][:]), 16)




def ph7a_qk1(ctx):
    nc, ar, S = ctx.nc, ctx.ar, ctx.S
    ar.reset()

    wq = ar.alloc([128, 8192], BF16)
    wk = ar.alloc([128, 8192], BF16)
    cosT = ar.alloc([128, 2048], F32)
    sinT = ar.alloc([128, 2048], F32)
    ga = [ar.alloc([128, 512], BF16) for _ in range(16)]
    qst = [ar.alloc([128, 1024], BF16) for _ in range(2)]
    rot = [ar.alloc([128, 1024], BF16) for _ in range(2)]
    tt = ar.alloc([128, 512], F32)

    s_w = ctx.sem("p7_w")
    s_a = ctx.sem("p7_a")
    s_mm = ctx.sem("p7_mm")
    s_ev = ctx.sem("p7_ev")
    s_ro = ctx.sem("p7_ro")
    s_st = ctx.sem("p7_st")

    @ctx.sec("gpsimd")
    def _(g):
        load_wT(ctx, g, S["WqT"], 1024, 1024, wq, s_w)
        load_wT(ctx, g, S["WkT"], 1024, 1024, wk, s_w)

    @ctx.sec("sync")
    def _(sy):
        s_w.inc(sy.dma_start(out=cosT[:], in_=S["cosT"]), 16)
        s_w.inc(sy.dma_start(out=sinT[:], in_=S["sinT"]), 16)

    for tb in range(8):
        toff = (tb % 4) * 512
        gbuf = [ga[(tb % 2) * 8 + dc] for dc in range(8)]

        @ctx.sec("sync")
        def _(sy, tb=tb, gbuf=gbuf):
            if tb >= 2:
                sy.wait_ge(s_mm.h, s_mm.base + (tb - 1) * 16)
            for dc in range(8):
                tload(ctx, sy, S["xn1_bf"], tb * 512, 512, dc * 128, 128, gbuf[dc], s_a)

        for ti, (wsb, scr, qscale) in enumerate([(wq, "qTr", 0.0625), (wk, "kTr", 1.0)]):
            for h in range(4):
                k = tb * 8 + ti * 4 + h
                kb = k % 2

                for half in range(2):
                    oc = 2 * h + half
                    pb = ctx.pp[(2 * k + half) % 4]

                    @ctx.sec("tensor")
                    def _(t, tb=tb, k=k, oc=oc, half=half, pb=pb, wsb=wsb):
                        if k == 0 and half == 0:
                            t.wait_ge(s_w.h, s_w.base + 64)
                        s_a.wait(t, 16 * 8 * (tb + 1))
                        if 2 * k + half >= 4:
                            s_ev.wait(t, 2 * k + half - 3)
                        for dc in range(8):
                            inst = t.matmul(pb[:, :],
                                            wslice(wsb, 1024, dc, oc * 128, oc * 128 + 128),
                                            gbuf[dc][:], start=(dc == 0), stop=(dc == 7))
                        s_mm.inc(inst)

                    @ctx.sec("vector")
                    def _(v, k=k, kb=kb, half=half, pb=pb, qscale=qscale):
                        s_mm.wait(v, 2 * k + half + 1)
                        if k >= 2:
                            s_ro.wait(v, k - 1)
                        s_ev.inc(v.tensor_scalar_mul(qst[kb][:, half * 512:half * 512 + 512],
                                                     pb[:, :], qscale))

                @ctx.sec("vector")
                def _(v, tb=tb, k=k, kb=kb, toff=toff):
                    s_ev.wait(v, 2 * (k + 1))
                    if k >= 2:
                        s_st.wait(v, 32 * (k - 1))
                    lo = qst[kb][:, 0:512]
                    hi = qst[kb][:, 512:1024]
                    cs = cosT[:, toff:toff + 512]
                    sn = sinT[:, toff:toff + 512]
                    rl = rot[kb][:, 0:512]
                    rh = rot[kb][:, 512:1024]
                    v.tensor_mul(rl[:, :], lo, cs)
                    v.tensor_mul(tt[:], hi, sn)
                    v.tensor_sub(rl[:, :], rl[:, :], tt[:])
                    v.tensor_mul(rh[:, :], hi, cs)
                    v.tensor_mul(tt[:], lo, sn)
                    s_ro.inc(v.tensor_add(rh[:, :], rh[:, :], tt[:]))

                @ctx.sec("gpsimd")
                def _(g, tb=tb, h=h, k=k, kb=kb, scr=scr):
                    s_ro.wait(g, k + 1)
                    s_st.inc(g.dma_start(
                        out=S[scr][(2 * h) * 128:(2 * h) * 128 + 128,
                                   tb * 512:(tb + 1) * 512],
                        in_=rot[kb][:, 0:512]), 16)
                    s_st.inc(g.dma_start(
                        out=S[scr][(2 * h + 1) * 128:(2 * h + 1) * 128 + 128,
                                   tb * 512:(tb + 1) * 512],
                        in_=rot[kb][:, 512:1024]), 16)



def ph7b_vg1(ctx):
    nc, ar, S = ctx.nc, ctx.ar, ctx.S
    ar.reset()

    wv = ar.alloc([128, 16384], BF16)
    wg = ar.alloc([128, 16384], BF16)
    ga = [ar.alloc([128, 512], BF16) for _ in range(16)]
    ob = [ar.alloc([128, 512], BF16) for _ in range(4)]

    s_w = ctx.sem("p7b_w")
    s_a = ctx.sem("p7b_a")
    s_mm = ctx.sem("p7b_mm")
    s_ev = ctx.sem("p7b_ev")
    s_st = ctx.sem("p7b_st")

    @ctx.sec("gpsimd")
    def _(g):
        load_wT(ctx, g, S["WvT"], 1024, 2048, wv, s_w)
        load_wT(ctx, g, S["WgT"], 1024, 2048, wg, s_w)

    for tb in range(8):
        gbuf = [ga[(tb % 2) * 8 + dc] for dc in range(8)]

        @ctx.sec("sync")
        def _(sy, tb=tb, gbuf=gbuf):
            if tb >= 2:
                s_mm.wait(sy, (tb - 1) * 32)
            for dc in range(8):
                tload(ctx, sy, S["xn1_bf"], tb * 512, 512, dc * 128, 128, gbuf[dc], s_a)

        for tc in range(4):
            gt = tb * 4 + tc
            for ti, (wsb, scr) in enumerate([(wv, "vtm1"), (wg, "gtm1")]):
                for oc in range(4):
                    k = gt * 8 + ti * 4 + oc
                    pb = ctx.pp[k % 2]
                    obuf = ob[k % 4]

                    @ctx.sec("tensor")
                    def _(t, tb=tb, tc=tc, oc=oc, k=k, pb=pb, wsb=wsb, gbuf=gbuf):
                        if k == 0:
                            t.wait_ge(s_w.h, s_w.base + 32)
                        s_a.wait(t, 16 * 8 * (tb + 1))
                        if k >= 2:
                            s_ev.wait(t, k - 1)
                        for dc in range(8):
                            inst = t.matmul(pb[:, :],
                                            gbuf[dc][:, tc * 128:tc * 128 + 128],
                                            wslice(wsb, 2048, dc, oc * 512, oc * 512 + 512),
                                            start=(dc == 0), stop=(dc == 7))
                        s_mm.inc(inst)

                    @ctx.sec("vector")
                    def _(v, k=k, pb=pb, obuf=obuf):
                        s_mm.wait(v, k + 1)
                        if k >= 4:
                            s_st.wait(v, 16 * (k - 3))
                        s_ev.inc(v.tensor_copy(obuf[:], pb[:, :]))

                    @ctx.sec("gpsimd")
                    def _(g, gt=gt, oc=oc, k=k, obuf=obuf, scr=scr):
                        s_ev.wait(g, k + 1)
                        s_st.inc(g.dma_start(
                            out=S[scr][gt * 128:(gt + 1) * 128, oc * 512:(oc + 1) * 512],
                            in_=obuf[:]), 16)



def ph8_ret1(ctx):
    nc, ar, S = ctx.nc, ctx.ar, ctx.S
    ar.reset()
    import math as _m
    gam = [1.0 - 2.0 ** (-5.0 - h) for h in range(4)]
    gC = [g ** 128 for g in gam]

    qT_sb = ar.alloc([128, 16384], BF16)
    kT_sb = ar.alloc([128, 16384], BF16)
    mask_sb = ar.alloc([128, 512], F32)
    cross_sb = ar.alloc([128, 512], F32)
    kdec_sb = ar.alloc([128, 1024], F32)
    gnw = ar.alloc([128, 512], F32)
    ktb = [ar.alloc([128, 1024], BF16) for _ in range(2)]
    vtb = [ar.alloc([128, 2048], BF16) for _ in range(2)]
    gtb = [ar.alloc([128, 2048], BF16) for _ in range(2)]
    gsi = ar.alloc([128, 2048], F32)
    ksc = [ar.alloc([128, 1024], BF16) for _ in range(2)]
    qc_bf = ar.alloc([128, 1024], BF16)
    simb = ar.alloc([128, 512], BF16)
    S_sb = ar.alloc([128, 4096], F32)
    S_bf = ar.alloc([128, 4096], BF16)
    ssq = ar.alloc([128, 16], F32)
    vv4 = ar.alloc([128, 4], F32)
    og = ar.alloc([128, 2048], F32)
    go = [ar.alloc([128, 2048], BF16) for _ in range(2)]

    s_c = ctx.sem("p8_c")
    s_qk = ctx.sem("p8_qk")
    s_kv = ctx.sem("p8_kv")
    s_pre = ctx.sem("p8_pre")
    s_simm = ctx.sem("p8_simm")
    s_sev = ctx.sem("p8_sev")
    s_tin = ctx.sem("p8_tin")
    s_och = ctx.sem("p8_och")
    s_sta = ctx.sem("p8_sta")
    s_sst = ctx.sem("p8_sst")
    s_sbf = ctx.sem("p8_sbf")
    s_rA = ctx.sem("p8_rA")
    s_rB = ctx.sem("p8_rB")
    s_rC = ctx.sem("p8_rC")
    s_oev = ctx.sem("p8_oev")
    s_ost = ctx.sem("p8_ost")

    @ctx.sec("sync")
    def _(sy):
        s_c.inc(sy.dma_start(out=mask_sb[:], in_=S["mask1"]), 16)
        s_c.inc(sy.dma_start(out=cross_sb[:], in_=S["cross1"]), 16)
        s_c.inc(sy.dma_start(out=kdec_sb[:], in_=S["kdec1f"]), 16)
        s_c.inc(sy.dma_start(out=gnw[:], in_=S["gnw_bc"]), 16)

    for b in range(2):
        @ctx.sec("sync")
        def _(sy, b=b):
            if b == 1:
                s_och.wait(sy, 16)
                s_pre.wait(sy, 16)
            src_q = S["qTr"].rearrange("(c p) t -> p c t", p=128)[:, :, b * 2048:(b + 1) * 2048]
            src_k = S["kTr"].rearrange("(c p) t -> p c t", p=128)[:, :, b * 2048:(b + 1) * 2048]
            s_qk.inc(sy.dma_start(out=qT_sb.rearrange("p (c t) -> p c t", c=8), in_=src_q), 16)
            s_qk.inc(sy.dma_start(out=kT_sb.rearrange("p (c t) -> p c t", c=8), in_=src_k), 16)

        @ctx.sec("vector")
        def _(v, b=b):
            if b == 1:
                s_och.wait(v, 16)
            v.memset(S_sb[:], 0.0)
            s_sbf.inc(v.memset(S_bf[:], 0.0))

        for j in range(16):
            jj = b * 16 + j

            sta_now = s_sta.rel
            sst_now = s_sst.rel

            @ctx.sec("sync")
            def _(sy, jj=jj, j=j, b=b, sta_now=sta_now):
                if jj >= 2:
                    s_och.wait(sy, jj - 1)
                    s_pre.wait(sy, jj - 1)
                    s_rA.wait(sy, jj - 1)
                    s_sta.wait(sy, sta_now)
                buf = jj % 2
                s_kv.inc(sy.dma_start(out=ktb[buf][:],
                                      in_=S["kTr"][:, jj * 128:(jj + 1) * 128],
                                      transpose=True), 16)
                s_kv.inc(sy.dma_start(out=vtb[buf][:],
                                      in_=S["vtm1"][jj * 128:(jj + 1) * 128, :]), 16)
                s_kv.inc(sy.dma_start(out=gtb[buf][:],
                                      in_=S["gtm1"][jj * 128:(jj + 1) * 128, :]), 16)

            @ctx.sec("vector")
            def _(v, b=b, j=j, jj=jj):
                buf = jj % 2
                s_kv.wait(v, 48 * (jj + 1))
                s_qk.wait(v, 32 * (b + 1))
                if jj == 0:
                    s_c.wait(v, 64)
                if jj >= 1:
                    s_och.wait(v, jj)
                v.tensor_mul(ksc[buf][:], ktb[buf][:], kdec_sb[:])
                for h in range(4):
                    for dkc in range(2):
                        c = 2 * h + dkc
                        v.tensor_mul(qc_bf[:, c * 128:(c + 1) * 128],
                                     qT_sb[:, c * 2048 + j * 128:c * 2048 + j * 128 + 128],
                                     cross_sb[:, h * 128:(h + 1) * 128])
                s_pre.inc(v.engine_nop())

            for h in range(4):
                @ctx.sec("tensor")
                def _(t, b=b, j=j, jj=jj, h=h, sst_now=sst_now):
                    if jj == 0 and h == 0:
                        t.wait_ge(s_c.h, s_c.base + 64)
                    s_qk.wait(t, 32 * (b + 1))
                    s_sst.wait(t, sst_now)
                    hh = jj * 4 + h
                    if hh >= 4:
                        s_sev.wait(t, hh - 3)
                    for dkc in range(2):
                        c = 2 * h + dkc
                        inst = t.matmul(
                            ctx.pp[4 + h][:, 0:128],
                            kT_sb[:, c * 2048 + j * 128:c * 2048 + j * 128 + 128],
                            qT_sb[:, c * 2048 + j * 128:c * 2048 + j * 128 + 128],
                            start=(dkc == 0), stop=(dkc == 1))
                    s_simm.inc(inst)

            for h in range(4):
                @ctx.sec("vector")
                def _(v, jj=jj, h=h):
                    hh = jj * 4 + h
                    s_simm.wait(v, hh + 1)
                    if jj >= 1:
                        s_tin.wait(v, (jj - 1) * 4 + h + 1)
                    s_sev.inc(v.tensor_tensor(simb[:, h * 128:(h + 1) * 128],
                                              ctx.pp[4 + h][:, 0:128],
                                              mask_sb[:, h * 128:(h + 1) * 128],
                                              ALU.mult))

            @ctx.sec("tensor")
            def _(t, b=b, j=j, jj=jj):
                buf = jj % 2
                s_pre.wait(t, jj + 1)
                s_oev.wait(t, 4 * jj)
                s_rA.wait(t, jj)
                last = None
                for h in range(4):
                    hh = jj * 4 + h
                    s_sev.wait(t, hh + 1)
                    last = t.matmul(ctx.pp[h][:, :], simb[:, h * 128:(h + 1) * 128],
                                    vtb[buf][:, h * 512:(h + 1) * 512],
                                    start=True, stop=False, skip_group_check=True)
                    s_tin.inc(last)
                if True:
                    s_sbf.wait(t, b * 17 + 1 + j)
                    for h in range(4):
                        for dkc in range(2):
                            c = 2 * h + dkc
                            last = t.matmul(ctx.pp[h][:, :],
                                            qc_bf[:, c * 128:(c + 1) * 128],
                                            S_bf[:, h * 1024 + dkc * 512:h * 1024 + (dkc + 1) * 512],
                                            start=False, stop=(dkc == 1),
                                            skip_group_check=True)
                s_och.inc(last)

            if j < 15:
                for wave in range(2):
                    sst_t = s_sst.rel
                    sta_t = s_sta.rel

                    @ctx.sec("tensor")
                    def _(t, jj=jj, wave=wave, sst_t=sst_t):
                        buf = jj % 2
                        s_sst.wait(t, sst_t)
                        s_sev.wait(t, 4 * (jj + 1))
                        last = None
                        for h in range(4):
                            last = t.matmul(
                                ctx.pp[4 + h][:, :],
                                ksc[buf][:, h * 256 + wave * 128:h * 256 + wave * 128 + 128],
                                vtb[buf][:, h * 512:(h + 1) * 512],
                                start=True, stop=True, skip_group_check=True)
                        s_sta.inc(last)

                    @ctx.sec("vector")
                    def _(v, b=b, j=j, jj=jj, wave=wave, sta_t=sta_t):
                        s_sta.wait(v, sta_t + 1)
                        if wave == 0:
                            s_och.wait(v, jj + 1)
                        for h in range(4):
                            sl = slice(h * 1024 + wave * 512, h * 1024 + (wave + 1) * 512)
                            s_sst.inc(v.scalar_tensor_tensor(
                                S_sb[:, sl], S_sb[:, sl], gC[h], ctx.pp[4 + h][:, :],
                                ALU.mult, ALU.add))

                @ctx.sec("vector")
                def _(v, b=b, j=j, jj=jj):
                    v.tensor_copy(S_bf[:], S_sb[:])
                    s_sbf.inc(v.engine_nop())
            else:
                @ctx.sec("vector")
                def _(v, jj=jj):
                    s_och.wait(v, jj + 1)
                    s_sbf.inc(v.engine_nop())

            @ctx.sec("scalar")
            def _(sc, jj=jj):
                buf = jj % 2
                s_och.wait(sc, jj + 1)
                s_kv.wait(sc, 48 * (jj + 1))
                if jj >= 2:
                    s_oev.wait(sc, 4 * (jj - 1))
                for h in range(4):
                    sc.activation(og[:, h * 512:(h + 1) * 512], ctx.pp[h][:, :],
                                  AF.Square, scale=1.0 / (512.0 ** 0.5),
                                  accum_out=ssq[:, 4 * h:4 * h + 1])
                s_rA.inc(sc.activation(gsi[:], gtb[buf][:], AF.Silu))

            @ctx.sec("vector")
            def _(v, jj=jj):
                s_rA.wait(v, jj + 1)
                ssq_v = ssq[:, 0:16].rearrange("p (g c) -> p g c", c=4)
                s_rB.inc(v.tensor_scalar_add(vv4[:], ssq_v[:, :, 0:1], EPS))

            @ctx.sec("scalar")
            def _(sc, jj=jj):
                s_rB.wait(sc, jj + 1)
                inst = sc.activation(vv4[:], vv4[:], AF.Ln)
                ctx.fence(sc, inst)
                s_rC.inc(sc.activation(vv4[:], vv4[:], AF.Exp, scale=-0.5))

            @ctx.sec("vector")
            def _(v, jj=jj):
                s_rC.wait(v, jj + 1)
                if jj >= 2:
                    s_ost.wait(v, 16 * (jj - 1))
                gob = go[jj % 2]
                for h in range(4):
                    sl = slice(h * 512, (h + 1) * 512)
                    v.scalar_tensor_tensor(og[:, sl], ctx.pp[h][:, :],
                                           vv4[:, h:h + 1], gnw[:],
                                           ALU.mult, ALU.mult)
                for h in range(4):
                    sl = slice(h * 512, (h + 1) * 512)
                    v.tensor_mul(gob[:, sl], og[:, sl], gsi[:, sl])
                s_oev.inc(v.engine_nop(), 4)

            @ctx.sec("sync")
            def _(sy, jj=jj):
                s_oev.wait(sy, 4 * (jj + 1))
                s_ost.inc(sy.dma_start(out=S["gotm1"][jj * 128:(jj + 1) * 128, :],
                                       in_=go[jj % 2][:]), 16)



def ph9_oproj1(ctx):
    nc, ar, S = ctx.nc, ctx.ar, ctx.S
    ar.reset()

    wo = ar.alloc([128, 16384], BF16)
    oT = [ar.alloc([128, 512], BF16) for _ in range(32)]
    s2f = [ar.alloc([128, 1024], F32) for _ in range(2)]

    s_w = ctx.sem("p9_w")
    s_a = ctx.sem("p9_a")
    s_mm = ctx.sem("p9_mm")
    s_ev = ctx.sem("p9_ev")
    s_st = ctx.sem("p9_st")

    @ctx.sec("gpsimd")
    def _(g):
        load_wT(ctx, g, S["WoT"], 2048, 1024, wo, s_w)

    for i in range(NTILE):
        b = i // 16
        buf = i % 2
        tb = i // 4

        if i % 4 == 0:
            @ctx.sec("sync")
            def _(sy, i=i, tb=tb):
                if tb >= 2:
                    s_mm.wait(sy, 2 * (tb - 1) * 4)
                for fc in range(16):
                    tload(ctx, sy, S["gotm1"], tb * 512, 512, fc * 128, 128,
                          oT[(tb % 2) * 16 + fc], s_a)

        for oc in range(2):
            @ctx.sec("tensor")
            def _(t, i=i, tb=tb, oc=oc):
                if i == 0 and oc == 0:
                    t.wait_ge(s_w.h, s_w.base + 16)
                s_a.wait(t, 256 * (tb + 1))
                k = 2 * i + oc
                if k >= 4:
                    s_ev.wait(t, i - 1)
                for fc in range(16):
                    inst = t.matmul(ctx.pp[k % 4][:, :],
                                    oT[(tb % 2) * 16 + fc][:, (i % 4) * 128:(i % 4) * 128 + 128],
                                    wslice(wo, 1024, fc, oc * 512, oc * 512 + 512),
                                    start=(fc == 0), stop=(fc == 15))
                s_mm.inc(inst)

        @ctx.sec("vector")
        def _(v, i=i, buf=buf):
            s_mm.wait(v, 2 * (i + 1))
            if i >= 2:
                s_st.wait(v, 16 * (i - 1))
            for oc in range(2):
                sl = slice(oc * 512, oc * 512 + 512)
                v.tensor_copy(s2f[buf][:, sl], ctx.pp[(2 * i + oc) % 4][:, :])
            s_ev.inc(v.engine_nop())

        @ctx.sec("gpsimd")
        def _(g, i=i, buf=buf):
            s_ev.wait(g, i + 1)
            s_st.inc(g.dma_start(out=S["s2_f32"][i * 128:(i + 1) * 128, :],
                                  in_=s2f[buf][:]), 16)



def ph10_rtr1(ctx):
    nc, ar, S = ctx.nc, ctx.ar, ctx.S
    ar.reset()

    rw = ar.alloc([128, 64], BF16)
    ga = [ar.alloc([128, 512], BF16) for _ in range(16)]
    lg = ar.alloc([128, 256], F32)
    ee = ar.alloc([128, 256], F32)
    e2 = ar.alloc([128, 256], F32)
    m1 = ar.alloc([128, 32], F32)
    m2 = ar.alloc([128, 32], F32)
    den = ar.alloc([128, 32], F32)
    cw = ar.alloc([128, 256], F32)

    s_w = ctx.sem("p10_w")
    s_a = ctx.sem("p10_a")
    s_mm = ctx.sem("p10_mm")
    s_sx = ctx.sem("p10_sx")
    s_st = ctx.sem("p10_st")

    @ctx.sec("gpsimd")
    def _(g):
        load_wT(ctx, g, S["gate_wT"], 1024, 8, rw, s_w)

    for tb in range(8):
        gbuf = [ga[(tb % 2) * 8 + dc] for dc in range(8)]

        @ctx.sec("sync")
        def _(sy, tb=tb, gbuf=gbuf):
            if tb >= 2:
                s_mm.wait(sy, 4 * (tb - 1))
            for dc in range(8):
                tload(ctx, sy, S["xn1_bf"], tb * 512, 512, dc * 128, 128, gbuf[dc], s_a)

        for tc in range(4):
            i = tb * 4 + tc

            @ctx.sec("tensor")
            def _(t, tb=tb, tc=tc, i=i, gbuf=gbuf):
                if i == 0:
                    t.wait_ge(s_w.h, s_w.base + 16)
                s_a.wait(t, 16 * 8 * (tb + 1))
                for dc in range(8):
                    inst = t.matmul(ctx.pp[4][:, i * 8:(i + 1) * 8],
                                    gbuf[dc][:, tc * 128:tc * 128 + 128],
                                    wslice(rw, 8, dc, 0, 8),
                                    start=(i == 0 and dc == 0), stop=(dc == 7),
                                    skip_group_check=True)
                s_mm.inc(inst)

    v3 = lambda ap: ap.rearrange("p (g c) -> p g c", c=8)
    bc = lambda ap: ap.rearrange("p (g u) -> p g u", u=1).broadcast_to([128, 32, 8])

    @ctx.sec("vector")
    def _(v):
        s_mm.wait(v, 32)
        inst = v.tensor_copy(lg[:], ctx.pp[4][:, 0:256])
        ctx.fence(v, inst)
        inst = v.tensor_reduce(m1[:], v3(lg[:]), axis=AX.X, op=ALU.max, negate=True)
        ctx.fence(v, inst)
        s_sx.inc(v.tensor_tensor(v3(ee[:]), v3(lg[:]), bc(m1[:]), ALU.add))

    @ctx.sec("scalar")
    def _(sc):
        s_sx.wait(sc, 1)
        s_sx.inc(sc.activation(ee[:], ee[:], AF.Exp))

    @ctx.sec("vector")
    def _(v):
        s_sx.wait(v, 2)
        inst = v.reduce_max(m1[:], v3(ee[:]), axis=AX.X)
        ctx.fence(v, inst)
        v.tensor_tensor(v3(cw[:]), v3(ee[:]), bc(m1[:]), ALU.is_ge)
        inst = v.scalar_tensor_tensor(e2[:], cw[:], -1e6, ee[:], ALU.mult, ALU.add)
        ctx.fence(v, inst)
        inst = v.reduce_max(m2[:], v3(e2[:]), axis=AX.X)
        ctx.fence(v, inst)
        v.tensor_tensor(v3(e2[:]), v3(e2[:]), bc(m2[:]), ALU.is_ge)
        inst = v.tensor_add(den[:], m1[:], m2[:])
        ctx.fence(v, inst)
        inst = v.reciprocal(den[:], den[:])
        ctx.fence(v, inst)
        v.tensor_add(cw[:], cw[:], e2[:])
        v.tensor_mul(cw[:], cw[:], ee[:])
        s_sx.inc(v.tensor_tensor(v3(cw[:]), v3(cw[:]), bc(den[:]), ALU.mult))

    @ctx.sec("sync")
    def _(sy):
        s_sx.wait(sy, 3)
        s_st.inc(sy.dma_start(out=S["cmb1"], in_=cw[:]), 16)



def ph11_moe1(ctx):
    nc, ar, S = ctx.nc, ctx.ar, ctx.S
    ar.reset()

    wgu = [ar.alloc([128, 8192], BF16) for _ in range(2)]
    wosb = ar.alloc([128, 32768], BF16)
    cwsb = ar.alloc([128, 256], F32)
    ga = [ar.alloc([128, 512], BF16) for _ in range(24)]
    hT = [ar.alloc([128, 16384], BF16)]
    sil = ar.alloc([128, 512], BF16)
    tmp = [ar.alloc([128, 512], F32) for _ in range(4)]
    zt = ar.alloc([128, 1024], F32)

    s_w = ctx.sem("p11_w")
    s_wo = ctx.sem("p11_wo")
    s_z = ctx.sem("p11_z")
    s_a = ctx.sem("p11_a")
    s_a2 = ctx.sem("p11_a2")
    s_m1 = ctx.sem("p11_m1")
    s_si = ctx.sem("p11_si")
    s_h = ctx.sem("p11_h")
    s_m2 = ctx.sem("p11_m2")
    s_ev = ctx.sem("p11_ev")
    s_ac = ctx.sem("p11_ac")

    @ctx.sec("vector")
    def _(v):
        v.memset(zt[:], 0.0)
        s_z.inc(v.engine_nop())

    @ctx.sec("sync")
    def _(sy):
        s_z.wait(sy, 1)
        for i in range(NTILE):
            s_z.inc(sy.dma_start(out=S["moe1"][i * 128:(i + 1) * 128, :], in_=zt[:]), 16)
        s_z.inc(sy.dma_start(out=cwsb[:], in_=S["cmb1"]), 16)

    for e in range(8):
        @ctx.sec("gpsimd")
        def _(g, e=e):
            if e >= 1:
                s_m2.wait(g, 64 * e)
            load_wT(ctx, g, S["woT_e"][e], 4096, 1024, wosb, s_wo)

        for tb in range(8):
            et = e * 8 + tb
            gbuf = [ga[(et % 3) * 8 + dc] for dc in range(8)]
            hbuf = hT[0]

            @ctx.sec("sync")
            def _(sy, et=et, tb=tb, gbuf=gbuf):
                if et >= 3:
                    s_m1.wait(sy, (et - 2) * 32)
                for dc in range(8):
                    tload(ctx, sy, S["xn1_bf"], tb * 512, 512, dc * 128, 128, gbuf[dc], s_a)

            for ob in range(8):
                ko = et * 8 + ob
                wb = wgu[ko % 2]

                @ctx.sec("gpsimd")
                def _(g, e=e, ob=ob, ko=ko, wb=wb):
                    if ko >= 2:
                        s_m1.wait(g, (ko - 1) * 4)
                    src_g = S["wgT_e"][e].rearrange("(c p) o -> p c o", p=128)[:, :, ob * 512:(ob + 1) * 512]
                    src_u = S["wuT_e"][e].rearrange("(c p) o -> p c o", p=128)[:, :, ob * 512:(ob + 1) * 512]
                    s_w.inc(g.dma_start(out=wb.rearrange("p (c o) -> p c o", c=8)[:, :, 0:512],
                                        in_=src_g), 16)
                    s_w.inc(g.dma_start(out=wb.rearrange("p (c o) -> p c o", c=8)[:, :, 512:1024],
                                        in_=src_u), 16)

                for fc in range(4):
                    kf = ko * 4 + fc
                    hc = ob * 4 + fc

                    @ctx.sec("tensor")
                    def _(t, tb=tb, et=et, ob=ob, fc=fc, ko=ko, kf=kf, wb=wb, gbuf=gbuf):
                        s_w.wait(t, 32 * (ko + 1))
                        s_a.wait(t, 128 * (et + 1))
                        if kf >= 2:
                            s_si.wait(t, kf - 1)
                        if kf >= 2:
                            s_h.wait(t, kf - 1)
                        for dc in range(8):
                            inst = t.matmul(ctx.pp[kf % 2][:, :],
                                            wb[:, dc * 1024 + fc * 128:dc * 1024 + fc * 128 + 128],
                                            gbuf[dc][:], start=(dc == 0), stop=(dc == 7))
                        for dc in range(8):
                            inst2 = t.matmul(ctx.pp[2 + kf % 2][:, :],
                                             wb[:, dc * 1024 + 512 + fc * 128:dc * 1024 + 512 + fc * 128 + 128],
                                             gbuf[dc][:], start=(dc == 0), stop=(dc == 7))
                        s_m1.inc(inst2)

                    @ctx.sec("scalar")
                    def _(sc, kf=kf):
                        s_m1.wait(sc, kf + 1)
                        if kf >= 1:
                            s_h.wait(sc, kf)
                        s_si.inc(sc.activation(sil[:], ctx.pp[kf % 2][:, :], AF.Silu))

                    @ctx.sec("vector")
                    def _(v, et=et, hc=hc, kf=kf, hbuf=hbuf):
                        s_si.wait(v, kf + 1)
                        if et >= 1 and hc == 0:
                            s_m2.wait(v, 8 * et)
                        s_h.inc(v.tensor_tensor(hbuf[:, hc * 512:(hc + 1) * 512],
                                                sil[:], ctx.pp[2 + kf % 2][:, :],
                                                ALU.mult))

            for tc in range(4):
                i = tb * 4 + tc
                for oc2 in range(2):
                    kk = et * 8 + tc * 2 + oc2
                    pb2 = ctx.pp[4 + kk % 4]

                    @ctx.sec("tensor")
                    def _(t, e=e, tb=tb, tc=tc, oc2=oc2, et=et, kk=kk, pb2=pb2, hbuf=hbuf):
                        s_h.wait(t, 32 * (et + 1))
                        s_wo.wait(t, 16 * (e + 1))
                        if kk >= 4:
                            s_ev.wait(t, kk - 3)
                        for hc in range(32):
                            inst = t.matmul(
                                pb2[:, :],
                                hbuf[:, hc * 512 + tc * 128:hc * 512 + tc * 128 + 128],
                                wslice(wosb, 1024, hc, oc2 * 512, oc2 * 512 + 512),
                                start=(hc == 0), stop=(hc == 31))
                        s_m2.inc(inst)

                    @ctx.sec("vector")
                    def _(v, e=e, i=i, kk=kk, pb2=pb2):
                        s_m2.wait(v, kk + 1)
                        if kk == 0:
                            s_z.wait(v, 1 + 16 * NTILE + 16)
                        if kk >= 4:
                            s_ac.wait(v, 16 * (kk - 3))
                        s_ev.inc(v.tensor_scalar_mul(tmp[kk % 4][:], pb2[:, :],
                                                     cwsb[:, i * 8 + e:i * 8 + e + 1]))

                    @ctx.sec("gpsimd")
                    def _(g, i=i, oc2=oc2, kk=kk):
                        s_ev.wait(g, kk + 1)
                        if kk == 0:
                            s_z.wait(g, 1 + 16 * NTILE)
                        s_ac.inc(g.dma_start(
                            out=S["moe1"][i * 128:(i + 1) * 128, oc2 * 512:(oc2 + 1) * 512],
                            in_=tmp[kk % 4][:], accum_op=ALU.add), 16)



def ph12_final(ctx):
    nc, ar, S = ctx.nc, ctx.ar, ctx.S
    ar.reset()

    x1t = [ar.alloc([128, 1024], F32) for _ in range(4)]
    s2t = [ar.alloc([128, 1024], F32) for _ in range(4)]
    mot = [ar.alloc([128, 1024], F32) for _ in range(4)]
    g1t = [ar.alloc([128, 1024], F32) for _ in range(4)]
    yt = [ar.alloc([128, 1024], F32) for _ in range(4)]

    s_l = ctx.sem("p12_l")
    s_lg = ctx.sem("p12_lg")
    s_si = ctx.sem("p12_si")
    s_v2 = ctx.sem("p12_v2")
    s_st = ctx.sem("p12_st")

    for i in range(NTILE):
        buf = i % 2
        rows = slice(i * 128, (i + 1) * 128)

        @ctx.sec("sync")
        def _(sy, i=i, buf=buf, rows=rows):
            if i >= 2:
                s_v2.wait(sy, i - 1)
                s_si.wait(sy, i - 1)
            s_l.inc(sy.dma_start(out=x1t[buf][:], in_=S["x1_f32"][rows, :]), 16)
            s_l.inc(sy.dma_start(out=s2t[buf][:], in_=S["s2_f32"][rows, :]), 16)

        @ctx.sec("gpsimd")
        def _(g, i=i, buf=buf, rows=rows):
            if i >= 2:
                s_v2.wait(g, i - 1)
                s_si.wait(g, i - 1)
            s_lg.inc(g.dma_start(out=mot[buf][:], in_=S["moe1"][rows, :]), 16)

        @ctx.sec("scalar")
        def _(sc, i=i, buf=buf):
            s_l.wait(sc, 32 * (i + 1))
            s_lg.wait(sc, 16 * (i + 1))
            if i >= 2:
                s_v2.wait(sc, i - 1)
            s_si.inc(sc.activation(g1t[buf][:], mot[buf][:], AF.Silu))

        @ctx.sec("vector")
        def _(v, i=i, buf=buf):
            s_si.wait(v, i + 1)
            s_l.wait(v, 32 * (i + 1))
            if i >= 2:
                s_st.wait(v, 16 * (i - 1))
            v.tensor_mul(yt[buf][:], s2t[buf][:], g1t[buf][:])
            s_v2.inc(v.tensor_add(yt[buf][:], yt[buf][:], x1t[buf][:]))

        @ctx.sec("sync")
        def _(sy, i=i, buf=buf, rows=rows):
            s_v2.wait(sy, i + 1)
            s_st.inc(sy.dma_start(out=S["y"][rows, :], in_=yt[buf][:]), 16)




def ph_rte(ctx, CMB, IDX, CWTM, XE, SRC, NSLOT, BOUNCE=True):
    nc, ar, S = ctx.nc, ctx.ar, ctx.S
    ar.reset()

    cw = ar.alloc([128, 256], F32)
    ltc = ar.alloc([128, 128], F32)
    tokf = ar.alloc([128, 32], F32)
    toki = ar.alloc([128, 32], I32)
    ones32 = ar.alloc([128, 32], F32)
    oobt = ar.alloc([128, 128], I32)
    mask = ar.alloc([128, 32], F32)
    incl = ar.alloc([128, 32], F32)
    excl = ar.alloc([128, 32], F32)
    nmask = ar.alloc([128, 32], F32)
    slotf = ar.alloc([128, 32], F32)
    sloti = [ar.alloc([128, 32], I32) for _ in range(2)]
    pfx = ar.alloc([128, 1], F32)
    idxc = [ar.alloc([128, 1], I32) for _ in range(4)]
    xg = [ar.alloc([128, 1024], BF16) for _ in range(4)]

    s_l = ctx.sem("rt_l")
    s_pf = ctx.sem("rt_pf")
    s_sc = ctx.sem("rt_sc")
    s_ix = ctx.sem("rt_ix")
    s_gx = ctx.sem("rt_gx")
    s_sx = ctx.sem("rt_sx")

    @ctx.sec("vector")
    def _(v):
        v.memset(ones32[:], 1.0)
        s_l.inc(v.memset(oobt[:], 1000000))

    @ctx.sec("sync")
    def _(sy):
        s_l.wait(sy, 1)
        s_l.inc(sy.dma_start(out=cw[:], in_=S[CMB]), 16)
        s_l.inc(sy.dma_start(out=ltc[:], in_=S["ltstrict"]), 16)
        s_l.inc(sy.dma_start(out=tokf[:], in_=S["tokid_f"]), 16)
        s_l.inc(sy.dma_start(out=toki[:], in_=S["tokid_i"]), 16)
        s_l.inc(sy.dma_start(out=S[IDX].rearrange("(a pf) -> a pf", a=128),
                             in_=oobt[:, 0:NSLOT * 8 // 128]), 16)
        s_l.inc(sy.dma_start(
            out=S[CWTM].rearrange("(t p) e -> p t e", p=128),
            in_=cw[:].rearrange("p (t e) -> p t e", e=8),
            allow_slow_non_contiguous=True), 16)

    for e in range(8):
        @ctx.sec("vector")
        def _(v, e=e):
            v.wait_ge(s_l.h, s_l.base + 97)
            if e >= 2:
                s_sc.wait(v, 512 * (e - 1))
            inst = v.tensor_single_scalar(
                mask[:].rearrange("p (t u) -> p t u", u=1),
                cw[:].rearrange("p (t c) -> p t c", c=8)[:, :, e:e + 1],
                0.0, ALU.is_gt)
            ctx.fence(v, inst)
            inst = v.tensor_tensor_scan(incl[:], ones32[:], mask[:], 0.0, ALU.mult, ALU.add)
            ctx.fence(v, inst)
            inst = v.tensor_sub(excl[:], incl[:], mask[:])
            ctx.fence(v, inst)
            inst = v.tensor_single_scalar(slotf[:], excl[:], float(NSLOT), ALU.is_lt)
            ctx.fence(v, inst)
            inst = v.tensor_mul(mask[:], mask[:], slotf[:])
            ctx.fence(v, inst)
            s_pf.inc(v.engine_nop())

        @ctx.sec("tensor")
        def _(t, e=e):
            s_pf.wait(t, 3 * e + 1)
            s_pf.inc(t.matmul(ctx.pp[0][:, 0:1], ltc[:], incl[:, 31:32],
                              start=True, stop=True))

        @ctx.sec("vector")
        def _(v, e=e):
            s_pf.wait(v, 3 * e + 2)
            inst = v.tensor_copy(pfx[:], ctx.pp[0][:, 0:1])
            ctx.fence(v, inst)
            inst = v.tensor_scalar(slotf[:], excl[:], pfx[:, 0:1], float(e * NSLOT),
                                   ALU.add, ALU.add)
            ctx.fence(v, inst)
            inst = v.tensor_scalar(nmask[:], mask[:], -1.0, 1.0, ALU.mult, ALU.add)
            ctx.fence(v, inst)
            inst = v.scalar_tensor_tensor(slotf[:], nmask[:], 1000000.0, slotf[:],
                                          ALU.mult, ALU.add)
            ctx.fence(v, inst)
            inst = v.tensor_copy(sloti[e % 2][:], slotf[:])
            ctx.fence(v, inst)
            s_pf.inc(v.engine_nop())

        @ctx.sec("gpsimd")
        def _(g, e=e):
            s_pf.wait(g, 3 * (e + 1))
            for tc in range(32):
                s_sc.inc(g.indirect_dma_start(
                    S[IDX].rearrange("(s u) -> s u", u=1),
                    IndirectOffsetOnAxis(sloti[e % 2][:, tc:tc + 1], 0),
                    toki[:, tc:tc + 1], None), 16)

    if not BOUNCE:
        return
    nst = NSLOT // 128
    for e in range(8):
        for st in range(nst):
            k = e * nst + st
            buf = k % 4

            @ctx.sec("sync")
            def _(sy, e=e, st=st, k=k, buf=buf):
                sy.wait_ge(s_sc.h, s_sc.base + 4096)
                if k >= 4:
                    s_gx.wait(sy, 16 * (k - 3))
                s_ix.inc(sy.dma_start(
                    out=idxc[buf][:],
                    in_=S[IDX].rearrange("(s u) -> s u", u=1)[e * NSLOT + st * 128:
                                                              e * NSLOT + (st + 1) * 128, :]), 16)

            @ctx.sec("gpsimd")
            def _(g, k=k, buf=buf):
                s_ix.wait(g, 16 * (k + 1))
                if k >= 4:
                    s_sx.wait(g, 16 * (k - 3))
                s_gx.inc(g.indirect_dma_start(
                    xg[buf][:], None, S[SRC],
                    IndirectOffsetOnAxis(idxc[buf][:, 0:1], 0)), 16)

            @ctx.sec("sync")
            def _(sy, e=e, st=st, k=k, buf=buf):
                s_gx.wait(sy, 16 * (k + 1))
                s_sx.inc(sy.dma_start(
                    out=S[XE][e * NSLOT + st * 128:e * NSLOT + (st + 1) * 128, :],
                    in_=xg[buf][:]), 16)


def ph_rte1(ctx):
    ph_rte(ctx, "cmb1", "idx1", "cw_tm", "xe_tm", "xn1_bf", 1536, BOUNCE=False)


def ph_rte0(ctx):
    ph_rte(ctx, "cmb0", "idx0", "cw0_tm", "me_tm", "mla_tm", 2048)



def ph11s_moe1(ctx):
    nc, ar, S = ctx.nc, ctx.ar, ctx.S
    ar.reset()

    wgu = [ar.alloc([128, 8192], BF16) for _ in range(2)]
    wosb = ar.alloc([128, 32768], BF16)
    ga = [ar.alloc([128, 512], BF16) for _ in range(24)]
    hT = [ar.alloc([128, 16384], BF16)]
    sil = ar.alloc([128, 512], BF16)
    tmp = [ar.alloc([128, 1024], F32) for _ in range(2)]
    zt = ar.alloc([128, 1024], F32)
    idxc = [ar.alloc([128, 4], I32) for _ in range(2)]
    cwg = [ar.alloc([128, 32], F32) for _ in range(2)]
    xgb = [ar.alloc([128, 1024], BF16) for _ in range(4)]
    idxb = [ar.alloc([128, 1], I32) for _ in range(4)]

    s_w = ctx.sem("p1s_w")
    s_wo = ctx.sem("p1s_wo")
    s_z = ctx.sem("p1s_z")
    s_a = ctx.sem("p1s_a")
    s_ix = ctx.sem("p1s_ix")
    s_cg = ctx.sem("p1s_cg")
    s_m1 = ctx.sem("p1s_m1")
    s_si = ctx.sem("p1s_si")
    s_h = ctx.sem("p1s_h")
    s_m2 = ctx.sem("p1s_m2")
    s_ev = ctx.sem("p1s_ev")
    s_ac = ctx.sem("p1s_ac")
    s_bix = ctx.sem("p1s_bix")
    s_bgx = ctx.sem("p1s_bgx")
    s_bsx = ctx.sem("p1s_bsx")

    def emit_bounce(e):
        for st in range(12):
            k = e * 12 + st
            buf = k % 4

            @ctx.sec("sync")
            def _(sy, e=e, st=st, k=k, buf=buf):
                if k >= 4:
                    s_bgx.wait(sy, 16 * (k - 3))
                s_bix.inc(sy.dma_start(
                    out=idxb[buf][:],
                    in_=S["idx1"].rearrange("(s u) -> s u", u=1)[e * 1536 + st * 128:
                                                                e * 1536 + (st + 1) * 128, :]), 16)

            @ctx.sec("gpsimd")
            def _(g, k=k, buf=buf):
                s_bix.wait(g, 16 * (k + 1))
                if k >= 4:
                    s_bsx.wait(g, 16 * (k - 3))
                s_bgx.inc(g.indirect_dma_start(
                    xgb[buf][:], None, S["xn1_bf"],
                    IndirectOffsetOnAxis(idxb[buf][:, 0:1], 0)), 16)

            @ctx.sec("sync")
            def _(sy, e=e, st=st, k=k, buf=buf):
                s_bgx.wait(sy, 16 * (k + 1))
                s_bsx.inc(sy.dma_start(
                    out=S["xe_tm"][e * 1536 + st * 128:e * 1536 + (st + 1) * 128, :],
                    in_=xgb[buf][:]), 16)

    @ctx.sec("vector")
    def _(v):
        v.memset(zt[:], 0.0)
        s_z.inc(v.engine_nop())

    @ctx.sec("sync")
    def _(sy):
        s_z.wait(sy, 1)
        for i in range(NTILE):
            s_z.inc(sy.dma_start(out=S["moe1"][i * 128:(i + 1) * 128, :], in_=zt[:]), 16)

    emit_bounce(0)
    for e in range(8):
        if e + 1 < 8:
            emit_bounce(e + 1)

        @ctx.sec("gpsimd")
        def _(g, e=e):
            if e >= 1:
                s_m2.wait(g, 24 * e)
            load_wT(ctx, g, S["woT_e"][e], 4096, 1024, wosb, s_wo)

        for tb in range(3):
            et = e * 3 + tb
            gbuf = [ga[(et % 2) * 8 + dc] for dc in range(8)]
            hbuf = hT[0]
            base = e * 1536 + tb * 512

            @ctx.sec("sync")
            def _(sy, e=e, et=et, tb=tb, base=base, gbuf=gbuf):
                s_bsx.wait(sy, 192 * (e + 1))
                if et >= 2:
                    s_m1.wait(sy, (et - 1) * 32)
                if et >= 2:
                    s_cg.wait(sy, 64 * (et - 1))
                    s_ac.wait(sy, 64 * (et - 1))
                s_ix.inc(sy.dma_start(
                    out=idxc[et % 2][:],
                    in_=S["idx1"].rearrange("(s u) -> u s", u=1)[:, base:base + 512]
                        .rearrange("u (t p) -> p (u t)", p=128),
                    allow_slow_non_contiguous=True), 16)
                for dc in range(8):
                    tload(ctx, sy, S["xe_tm"], base, 512, dc * 128, 128, gbuf[dc], s_a)

            @ctx.sec("gpsimd")
            def _(g, e=e, et=et, base=base):
                s_ix.wait(g, 16 * (et + 1))
                if et >= 2:
                    s_ev.wait(g, 8 * (et - 1))
                for tc in range(4):
                    s_cg.inc(g.indirect_dma_start(
                        cwg[et % 2][:, tc * 8:(tc + 1) * 8], None, S["cw_tm"],
                        IndirectOffsetOnAxis(idxc[et % 2][:, tc:tc + 1], 0)), 16)

            for ob in range(8):
                ko = et * 8 + ob
                wb = wgu[ko % 2]

                @ctx.sec("gpsimd")
                def _(g, e=e, ob=ob, ko=ko, wb=wb):
                    if ko >= 2:
                        s_m1.wait(g, (ko - 1) * 4)
                    src_g = S["wgT_e"][e].rearrange("(c p) o -> p c o", p=128)[:, :, ob * 512:(ob + 1) * 512]
                    src_u = S["wuT_e"][e].rearrange("(c p) o -> p c o", p=128)[:, :, ob * 512:(ob + 1) * 512]
                    s_w.inc(g.dma_start(out=wb.rearrange("p (c o) -> p c o", c=8)[:, :, 0:512],
                                        in_=src_g), 16)
                    s_w.inc(g.dma_start(out=wb.rearrange("p (c o) -> p c o", c=8)[:, :, 512:1024],
                                        in_=src_u), 16)

                for fc in range(4):
                    kf = ko * 4 + fc
                    hc = ob * 4 + fc

                    @ctx.sec("tensor")
                    def _(t, tb=tb, et=et, ob=ob, fc=fc, ko=ko, kf=kf, wb=wb, gbuf=gbuf):
                        s_w.wait(t, 32 * (ko + 1))
                        s_a.wait(t, 128 * (et + 1))
                        if kf >= 2:
                            s_si.wait(t, kf - 1)
                            s_h.wait(t, kf - 1)
                        for dc in range(8):
                            inst = t.matmul(ctx.pp[kf % 2][:, :],
                                            wb[:, dc * 1024 + fc * 128:dc * 1024 + fc * 128 + 128],
                                            gbuf[dc][:], start=(dc == 0), stop=(dc == 7))
                        for dc in range(8):
                            inst2 = t.matmul(ctx.pp[2 + kf % 2][:, :],
                                             wb[:, dc * 1024 + 512 + fc * 128:dc * 1024 + 512 + fc * 128 + 128],
                                             gbuf[dc][:], start=(dc == 0), stop=(dc == 7))
                        s_m1.inc(inst2)

                    @ctx.sec("scalar")
                    def _(sc, kf=kf):
                        s_m1.wait(sc, kf + 1)
                        if kf >= 1:
                            s_h.wait(sc, kf)
                        s_si.inc(sc.activation(sil[:], ctx.pp[kf % 2][:, :], AF.Silu))

                    @ctx.sec("vector")
                    def _(v, et=et, hc=hc, kf=kf, hbuf=hbuf):
                        s_si.wait(v, kf + 1)
                        if et >= 1 and hc == 0:
                            s_m2.wait(v, 8 * et)
                        s_h.inc(v.tensor_tensor(hbuf[:, hc * 512:(hc + 1) * 512],
                                                sil[:], ctx.pp[2 + kf % 2][:, :],
                                                ALU.mult))

            for tc in range(4):
                for oc2 in range(2):
                    kk = et * 8 + tc * 2 + oc2
                    pb2 = ctx.pp[4 + kk % 4]

                    @ctx.sec("tensor")
                    def _(t, e=e, tb=tb, tc=tc, oc2=oc2, et=et, kk=kk, pb2=pb2, hbuf=hbuf):
                        s_h.wait(t, 32 * (et + 1))
                        s_wo.wait(t, 16 * (e + 1))
                        if kk >= 4:
                            s_ev.wait(t, kk - 3)
                        for hc in range(32):
                            inst = t.matmul(
                                pb2[:, :],
                                hbuf[:, hc * 512 + tc * 128:hc * 512 + tc * 128 + 128],
                                wslice(wosb, 1024, hc, oc2 * 512, oc2 * 512 + 512),
                                start=(hc == 0), stop=(hc == 31))
                        s_m2.inc(inst)

                    @ctx.sec("vector")
                    def _(v, e=e, et=et, tc=tc, oc2=oc2, kk=kk, pb2=pb2):
                        s_m2.wait(v, kk + 1)
                        s_cg.wait(v, 64 * (et + 1))
                        kt = et * 4 + tc
                        if kt >= 2 and oc2 == 0:
                            s_ac.wait(v, 16 * (kt - 1))
                        s_ev.inc(v.tensor_scalar_mul(
                            tmp[kt % 2][:, oc2 * 512:(oc2 + 1) * 512], pb2[:, :],
                            cwg[et % 2][:, tc * 8 + e:tc * 8 + e + 1]))

                    if oc2 == 1:
                        @ctx.sec("gpsimd")
                        def _(g, et=et, tc=tc, kk=kk):
                            kt = et * 4 + tc
                            s_ev.wait(g, kk + 1)
                            if kt == 0:
                                s_z.wait(g, 1 + 16 * NTILE)
                            s_ac.inc(g.indirect_dma_start(
                                S["moe1"],
                                IndirectOffsetOnAxis(idxc[et % 2][:, tc:tc + 1], 0),
                                tmp[kt % 2][:], None, compute_op=ALU.add), 16)




def ph5es_moe0(ctx):
    nc, ar, S = ctx.nc, ctx.ar, ctx.S
    ar.reset()

    w1sb = [ar.alloc([128, 16384], BF16) for _ in range(2)]
    w2sb = [ar.alloc([128, 16384], BF16)]
    b1c = [ar.alloc([128, 16], F32) for _ in range(2)]
    b2r = [ar.alloc([128, 1024], BF16) for _ in range(2)]
    onesb = ar.alloc([128, 128], BF16)
    ga = [ar.alloc([128, 512], BF16) for _ in range(24)]
    hT = [ar.alloc([128, 8192], BF16) for _ in range(2)]
    tmp = [ar.alloc([128, 1024], F32) for _ in range(2)]
    zt = ar.alloc([128, 1024], F32)
    idxc = [ar.alloc([128, 4], I32) for _ in range(2)]
    cwg = [ar.alloc([128, 32], F32) for _ in range(2)]

    s_w = ctx.sem("p5s_w")
    s_z = ctx.sem("p5s_z")
    s_a = ctx.sem("p5s_a")
    s_ix = ctx.sem("p5s_ix")
    s_cg = ctx.sem("p5s_cg")
    s_m1 = ctx.sem("p5s_m1")
    s_h = ctx.sem("p5s_h")
    s_m2 = ctx.sem("p5s_m2")
    s_ev = ctx.sem("p5s_ev")
    s_ac = ctx.sem("p5s_ac")

    @ctx.sec("vector")
    def _(v):
        v.memset(zt[:], 0.0)
        v.memset(onesb[:], 1.0)
        s_z.inc(v.engine_nop())

    @ctx.sec("sync")
    def _(sy):
        s_z.wait(sy, 1)
        for i in range(NTILE):
            s_z.inc(sy.dma_start(out=S["moe0"][i * 128:(i + 1) * 128, :], in_=zt[:]), 16)

    for e in range(8):
        eb = e % 2

        @ctx.sec("gpsimd")
        def _(g, e=e, eb=eb):
            if e >= 2:
                g.wait_ge(s_m1.h, s_m1.base + 64 * (e - 1))
            if e >= 1:
                g.wait_ge(s_m2.h, s_m2.base + 32 * e)
            load_wT(ctx, g, S["f_w1"][e], 1024, 2048, w1sb[eb], s_w)
            load_wT(ctx, g, S["f_w2"][e], 2048, 1024, w2sb[0], s_w)
            s_w.inc(g.dma_start(out=b2r[eb][0:1, :],
                                in_=S["f_b2"][e].rearrange("(u f) -> u f", u=1)), 16)

        @ctx.sec("sync")
        def _(sy, e=e, eb=eb):
            if e >= 2:
                sy.wait_ge(s_h.h, s_h.base + 64 * (e - 1))
            s_w.inc(sy.dma_start(out=b1c[eb][:], in_=S["f_b1c"][e]), 16)

        for tb in range(4):
            et = e * 4 + tb
            gbuf = [ga[(et % 3) * 8 + dc] for dc in range(8)]
            hbuf = hT[et % 2]
            base = e * 2048 + tb * 512

            @ctx.sec("sync")
            def _(sy, et=et, base=base, gbuf=gbuf):
                if et >= 3:
                    s_m1.wait(sy, (et - 2) * 16)
                if et >= 2:
                    s_cg.wait(sy, 64 * (et - 1))
                    s_ac.wait(sy, 64 * (et - 1))
                s_ix.inc(sy.dma_start(
                    out=idxc[et % 2][:],
                    in_=S["idx0"].rearrange("(s u) -> u s", u=1)[:, base:base + 512]
                        .rearrange("u (t p) -> p (u t)", p=128),
                    allow_slow_non_contiguous=True), 16)
                for dc in range(8):
                    tload(ctx, sy, S["me_tm"], base, 512, dc * 128, 128, gbuf[dc], s_a)

            @ctx.sec("gpsimd")
            def _(g, et=et):
                s_ix.wait(g, 16 * (et + 1))
                if et >= 2:
                    s_ev.wait(g, 8 * (et - 1))
                for tc in range(4):
                    s_cg.inc(g.indirect_dma_start(
                        cwg[et % 2][:, tc * 8:(tc + 1) * 8], None, S["cw0_tm"],
                        IndirectOffsetOnAxis(idxc[et % 2][:, tc:tc + 1], 0)), 16)

            for oc in range(16):
                k = et * 16 + oc
                pb = ctx.pp[oc % 4]

                @ctx.sec("tensor")
                def _(t, e=e, eb=eb, et=et, oc=oc, k=k, pb=pb, gbuf=gbuf):
                    t.wait_ge(s_w.h, s_w.base + 64 * (e + 1))
                    s_a.wait(t, 128 * (et + 1))
                    if k >= 4:
                        s_h.wait(t, k - 3)
                    for dc in range(8):
                        inst = t.matmul(pb[:, :],
                                        wslice(w1sb[eb], 2048, dc, oc * 128, oc * 128 + 128),
                                        gbuf[dc][:], start=(dc == 0), stop=(dc == 7))
                    s_m1.inc(inst)

                @ctx.sec("scalar")
                def _(sc, eb=eb, et=et, oc=oc, k=k, pb=pb, hbuf=hbuf):
                    s_m1.wait(sc, k + 1)
                    if et >= 2 and oc == 0:
                        s_m2.wait(sc, 8 * (et - 1))
                    s_h.inc(sc.activation(hbuf[:, oc * 512:(oc + 1) * 512], pb[:, :],
                                          AF.Gelu, bias=b1c[eb][:, oc:oc + 1]))

            for tc in range(4):
                for oc2 in range(2):
                    kk = et * 8 + tc * 2 + oc2
                    pb2 = ctx.pp[4 + kk % 4]

                    @ctx.sec("tensor")
                    def _(t, eb=eb, et=et, tc=tc, oc2=oc2, kk=kk, pb2=pb2, hbuf=hbuf):
                        s_h.wait(t, 16 * (et + 1))
                        if kk >= 4:
                            s_ev.wait(t, kk - 3)
                        for hc in range(16):
                            t.matmul(pb2[:, :],
                                     hbuf[:, hc * 512 + tc * 128:hc * 512 + tc * 128 + 128],
                                     wslice(w2sb[0], 1024, hc, oc2 * 512, oc2 * 512 + 512),
                                     start=(hc == 0), stop=False, skip_group_check=True)
                        s_m2.inc(t.matmul(pb2[:, :], onesb[0:1, :],
                                          b2r[eb][0:1, oc2 * 512:(oc2 + 1) * 512],
                                          start=False, stop=True, skip_group_check=True))

                    @ctx.sec("vector")
                    def _(v, e=e, et=et, tc=tc, oc2=oc2, kk=kk, pb2=pb2):
                        s_m2.wait(v, kk + 1)
                        s_cg.wait(v, 64 * (et + 1))
                        kt = et * 4 + tc
                        if kt >= 2 and oc2 == 0:
                            s_ac.wait(v, 16 * (kt - 1))
                        s_ev.inc(v.tensor_scalar_mul(
                            tmp[kt % 2][:, oc2 * 512:(oc2 + 1) * 512], pb2[:, :],
                            cwg[et % 2][:, tc * 8 + e:tc * 8 + e + 1]))

                    if oc2 == 1:
                        @ctx.sec("gpsimd")
                        def _(g, et=et, tc=tc, kk=kk):
                            kt = et * 4 + tc
                            s_ev.wait(g, kk + 1)
                            if kt == 0:
                                s_z.wait(g, 1 + 16 * NTILE)
                            s_ac.inc(g.indirect_dma_start(
                                S["moe0"],
                                IndirectOffsetOnAxis(idxc[et % 2][:, tc:tc + 1], 0),
                                tmp[kt % 2][:], None, compute_op=ALU.add), 16)


def load_wT(ctx, sy, dram_wT, D_in, O, dst_sb, sem):
    src = dram_wT.rearrange("(c p) o -> p c o", p=128)
    sem.inc(sy.dma_start(out=dst_sb[:, :], in_=src), 16)


def wslice(wsb, O, dc, o0, o1):
    return wsb[:, dc * O + o0: dc * O + o1]


def tload(ctx, sy, scratch_tm, t0, nt, f0, nf, dst_sb, sem):
    sem.inc(sy.dma_start(out=dst_sb[:, :], in_=scratch_tm[t0:t0 + nt, f0:f0 + nf],
                         transpose=True), 16)



def stage_host(inputs):
    H = {}
    H["ones"] = np.ones((128, 1), np.float32)
    H["gn1_g"] = np.ascontiguousarray(
        np.asarray(inputs["L0_attn_norm_g"], np.float32).reshape(8, 128).T)
    H["gn1_b"] = np.ascontiguousarray(
        np.asarray(inputs["L0_attn_norm_b"], np.float32).reshape(8, 128).T)
    H["qwT"] = np.ascontiguousarray(np.asarray(inputs["L0_attn_qw"], np.float32).T)
    H["kwT"] = np.ascontiguousarray(np.asarray(inputs["L0_attn_kw"], np.float32).T)
    H["vwT"] = np.ascontiguousarray(np.asarray(inputs["L0_attn_vw"], np.float32).T)
    H["owT"] = np.ascontiguousarray(np.asarray(inputs["L0_attn_ow"], np.float32).T)
    H["ob_bc"] = np.ascontiguousarray(np.broadcast_to(
        np.asarray(inputs["L0_attn_ob"], np.float32)[None, :], (128, 1024)))
    H["gn2_g"] = np.ascontiguousarray(
        np.asarray(inputs["L0_attn_ffn_norm_g"], np.float32).reshape(8, 128).T)
    H["gn2_b"] = np.ascontiguousarray(
        np.asarray(inputs["L0_attn_ffn_norm_b"], np.float32).reshape(8, 128).T)
    gam = 1.0 - np.exp(np.linspace(np.log(1 / 32.), np.log(1 / 512.), 16))
    i_ = np.arange(128, dtype=np.float64)
    mask0 = np.zeros((128, 2048), np.float64)
    for h in range(16):
        dif = i_[None, :] - i_[:, None]
        mask0[:, h * 128:(h + 1) * 128] = np.where(dif >= 0, gam[h] ** dif, 0.0) / 8.0
    H["mask0"] = mask0.astype(np.float32)
    cross0 = np.zeros((128, 1024), np.float64)
    for g in range(8):
        for par in range(2):
            hh = 2 * g + par
            cross0[par * 64:(par + 1) * 64, g * 128:(g + 1) * 128] = \
                (gam[hh] ** (i_ + 1.0))[None, :] / 8.0
    H["cross0"] = cross0.astype(np.float32)
    kdec0f = np.zeros((128, 1024), np.float64)
    for h in range(16):
        kdec0f[:, h * 64:(h + 1) * 64] = (gam[h] ** (127.0 - i_))[:, None]
    H["kdec0f"] = kdec0f.astype(np.float32)
    H["w1T"] = np.ascontiguousarray(np.asarray(inputs["L0_attn_w1"], np.float32).T)
    H["w2T"] = np.ascontiguousarray(np.asarray(inputs["L0_attn_w2"], np.float32).T)
    H["b1c"] = np.ascontiguousarray(
        np.asarray(inputs["L0_attn_b1"], np.float32).reshape(16, 128).T)
    H["b2_bc"] = np.ascontiguousarray(np.broadcast_to(
        np.asarray(inputs["L0_attn_b2"], np.float32)[None, :], (128, 1024)))
    cols = np.array([h * 64 + j for h in range(16) for j in range(32)])
    H["f_qwsl"] = np.ascontiguousarray(np.asarray(inputs["L0_ffn_q_w"], np.float32)[:, cols])
    H["f_kw"] = np.ascontiguousarray(np.asarray(inputs["L0_ffn_k_w"], np.float32))
    H["f_vw"] = np.ascontiguousarray(np.asarray(inputs["L0_ffn_v_w"], np.float32))
    H["f_qbc"] = np.ascontiguousarray(
        np.asarray(inputs["L0_ffn_q_b"], np.float32)[cols].reshape(4, 128).T)
    H["f_kbc"] = np.ascontiguousarray(
        np.asarray(inputs["L0_ffn_k_b"], np.float32).reshape(4, 128).T)
    H["f_vb_bc"] = np.ascontiguousarray(np.broadcast_to(
        np.asarray(inputs["L0_ffn_v_b"], np.float32)[None, :], (128, 512)))
    H["f_ow"] = np.ascontiguousarray(np.asarray(inputs["L0_ffn_o_w"], np.float32)[:512])
    H["f_ob_bc"] = np.ascontiguousarray(np.broadcast_to(
        np.asarray(inputs["L0_ffn_o_b"], np.float32)[None, :], (128, 1024)))
    H["f_rw"] = np.ascontiguousarray(np.asarray(inputs["L0_ffn_router_w"], np.float32))
    H["f_rb_bc"] = np.ascontiguousarray(np.broadcast_to(
        np.asarray(inputs["L0_ffn_router_b"], np.float32)[None, :], (128, 8)))
    H["f_w1"] = np.ascontiguousarray(np.asarray(inputs["L0_ffn_w1"], np.float32))
    H["f_w2"] = np.ascontiguousarray(np.asarray(inputs["L0_ffn_w2"], np.float32))
    H["f_b1c"] = np.ascontiguousarray(
        np.asarray(inputs["L0_ffn_b1"], np.float32).reshape(8, 16, 128).transpose(0, 2, 1))
    H["f_b2"] = np.ascontiguousarray(np.asarray(inputs["L0_ffn_b2"], np.float32))
    H["WqT"] = np.ascontiguousarray(np.asarray(inputs["L1_attn_Wq"], np.float32).T)
    H["WkT"] = np.ascontiguousarray(np.asarray(inputs["L1_attn_Wk"], np.float32).T)
    H["WvT"] = np.ascontiguousarray(np.asarray(inputs["L1_attn_Wv"], np.float32).T)
    H["WgT"] = np.ascontiguousarray(np.asarray(inputs["L1_attn_Wg"], np.float32).T)
    H["WoT"] = np.ascontiguousarray(np.asarray(inputs["L1_attn_Wo"], np.float32).T)
    t_ = np.arange(2048, dtype=np.float64)
    invf = 1.0 / 10000.0 ** (np.arange(0, 256, 2, dtype=np.float64) / 256.0)
    fr = t_[None, :] * invf[:, None]
    H["cosT"] = np.cos(fr).astype(np.float32)
    H["sinT"] = np.sin(fr).astype(np.float32)
    gam1 = 1.0 - np.exp2(-5.0 - np.arange(4, dtype=np.float64))
    i_ = np.arange(128, dtype=np.float64)
    mask1 = np.zeros((128, 512), np.float64)
    for h in range(4):
        dif = i_[None, :] - i_[:, None]
        mask1[:, h * 128:(h + 1) * 128] = np.where(dif >= 0, gam1[h] ** dif, 0.0)
    H["mask1"] = mask1.astype(np.float32)
    cross1 = np.zeros((128, 512), np.float64)
    for h in range(4):
        cross1[:, h * 128:(h + 1) * 128] = (gam1[h] ** (i_ + 1.0))[None, :]
    H["cross1"] = cross1.astype(np.float32)
    kdec1f = np.zeros((128, 1024), np.float64)
    for h in range(4):
        kdec1f[:, h * 256:(h + 1) * 256] = (gam1[h] ** (127.0 - i_))[:, None]
    H["kdec1f"] = kdec1f.astype(np.float32)
    H["gnw_bc"] = np.ascontiguousarray(np.broadcast_to(
        np.asarray(inputs["L1_attn_gnorm_w"], np.float32)[None, :], (128, 512)))
    H["gate_wT"] = np.ascontiguousarray(np.asarray(inputs["L1_ffn_gate_w"], np.float32).T)
    H["ltstrict"] = np.ascontiguousarray(np.tril(np.ones((128, 128), np.float32), -1).T)
    tok = (np.arange(32)[None, :] * 128 + np.arange(128)[:, None])
    H["tokid_f"] = tok.astype(np.float32)
    H["tokid_i"] = tok.astype(np.int32)
    H["wgT_e"] = np.ascontiguousarray(np.asarray(inputs["L1_ffn_wi_gate"], np.float32).transpose(0, 2, 1))
    H["wuT_e"] = np.ascontiguousarray(np.asarray(inputs["L1_ffn_wi_up"], np.float32).transpose(0, 2, 1))
    H["woT_e"] = np.ascontiguousarray(np.asarray(inputs["L1_ffn_wo"], np.float32).transpose(0, 2, 1))
    return H


SCRATCH = {
    "p1dbg": ((128, 128), F32),
    "qT0": ((D, NT), BF16),
    "kT0": ((D, NT), BF16),
    "vtm0": ((NT, D), BF16),
    "ktm0": ((NT, D), BF16),
    "fq0": ((128, 32), F32),
    "fk0": ((128, 32), F32),
    "otm0": ((NT, D), BF16),
    "s1_f32": ((NT, D), F32),
    "s1_bf": ((NT, D), BF16),
    "gn2ab": ((2, 2, 128, 8), F32),
    "ph3_stat": ((2, 2, 16), F32),
    "sbr_f32": ((NT, D), F32),
    "fQT": ((512, NT), BF16),
    "fKT": ((512, NT), BF16),
    "fVtm": ((NT, 512), BF16),
    "aoTs": ((512, NT), BF16),
    "mla_tm": ((NT, D), BF16),
    "cmb0": ((128, 256), F32),
    "moe0": ((NT, D), F32),
    "x1_f32": ((NT, D), F32),
    "xn1_bf": ((NT, D), BF16),
    "qTr": ((D, NT), BF16),
    "kTr": ((D, NT), BF16),
    "vtm1": ((NT, 2048), BF16),
    "gtm1": ((NT, 2048), BF16),
    "gotm1": ((NT, 2048), BF16),
    "s2_f32": ((NT, D), F32),
    "cmb1": ((128, 256), F32),
    "moe1": ((NT, D), F32),
    "idx1": ((12288,), I32),
    "idx0": ((16384,), I32),
    "cw0_tm": ((NT, 8), F32),
    "me_tm": ((16384, D), BF16),
    "cw_tm": ((NT, 8), F32),
    "xe_tm": ((12288, D), BF16),
    "y": ((NT, D), F32),
    "xn0_f32": ((NT, D), F32),
    "xn0_bf": ((NT, D), BF16),
    "gn1ab": ((2, 2, 128, 8), F32),
    "ph1_stat": ((2, 2, 16), F32),
}
CONST_SB = {"ones", "gn1_g", "gn1_b", "gn2_g", "gn2_b"}
EXT = {
    "x": ((NT, D), F32),
    "qwT": ((D, D), F32),
    "kwT": ((D, D), F32),
    "vwT": ((D, D), F32),
    "owT": ((D, D), F32),
    "mask0": ((128, 2048), F32),
    "cross0": ((128, 1024), F32),
    "kdec0f": ((128, 1024), F32),
    "ob_bc": ((128, 1024), F32),
    "gn2_g": ((128, 8), F32),
    "gn2_b": ((128, 8), F32),
    "w1T": ((D, 2048), F32),
    "w2T": ((2048, D), F32),
    "b1c": ((128, 16), F32),
    "b2_bc": ((128, 1024), F32),
    "f_qwsl": ((D, 512), F32),
    "f_kw": ((D, 512), F32),
    "f_vw": ((D, 512), F32),
    "f_qbc": ((128, 4), F32),
    "f_kbc": ((128, 4), F32),
    "f_vb_bc": ((128, 512), F32),
    "f_ow": ((512, D), F32),
    "f_ob_bc": ((128, 1024), F32),
    "f_rw": ((D, 8), F32),
    "f_rb_bc": ((128, 8), F32),
    "f_w1": ((8, D, 2048), F32),
    "f_w2": ((8, 2048, D), F32),
    "f_b1c": ((8, 128, 16), F32),
    "f_b2": ((8, 1024), F32),
    "WqT": ((D, D), F32),
    "WkT": ((D, D), F32),
    "WvT": ((D, 2048), F32),
    "WgT": ((D, 2048), F32),
    "WoT": ((2048, D), F32),
    "cosT": ((128, 2048), F32),
    "sinT": ((128, 2048), F32),
    "mask1": ((128, 512), F32),
    "cross1": ((128, 512), F32),
    "kdec1f": ((128, 1024), F32),
    "gnw_bc": ((128, 512), F32),
    "gate_wT": ((D, 8), F32),
    "ltstrict": ((128, 128), F32),
    "tokid_f": ((128, 32), F32),
    "tokid_i": ((128, 32), I32),
    "wgT_e": ((8, D, 4096), F32),
    "wuT_e": ((8, D, 4096), F32),
    "woT_e": ((8, 4096, D), F32),
    "ones": ((128, 1), F32),
    "gn1_g": ((128, 8), F32),
    "gn1_b": ((128, 8), F32),
}

PHASES = [
    ("ln0", ph1_ln0, ["x", "ones", "gn1_g", "gn1_b"],
     ["xn0_f32", "xn0_bf", "gn1ab", "p1dbg"], ["ph1_stat"]),
    ("qk0", ph2a_qk0, ["xn0_bf", "gn1ab", "qwT", "kwT", "ones"],
     ["qT0", "kT0", "fq0", "fk0"], []),
    ("kv0", ph2b_kv0, ["xn0_bf", "gn1ab", "vwT", "kwT", "fk0"],
     ["vtm0", "ktm0"], []),
    ("ret0", ph3a_ret0, ["qT0", "kT0", "fq0", "fk0", "ktm0", "vtm0",
                         "mask0", "cross0", "kdec0f", "ones"],
     ["otm0"], []),
    ("opr0", ph3b_oproj0, ["otm0", "xn0_f32", "owT", "ob_bc", "ones", "gn2_g", "gn2_b"],
     ["s1_f32", "s1_bf", "gn2ab"], ["ph3_stat"]),
    ("ffn0", ph4_ffn0, ["s1_bf", "s1_f32", "gn2ab", "w1T", "w2T", "b1c", "b2_bc"],
     ["sbr_f32"], []),
    ("fqkv", ph5a_qkv, ["xn0_bf", "f_qwsl", "f_kw", "f_vw", "f_qbc", "f_kbc", "f_vb_bc"],
     ["fQT", "fKT", "fVtm"], []),
    ("fatt", ph5b_attn, ["fQT", "fKT", "fVtm"], ["aoTs"], []),
    ("fmla", ph5c_mla, ["aoTs", "f_ow", "f_ob_bc"], ["mla_tm"], []),
    ("frtr", ph5d_router, ["mla_tm", "f_rw", "f_rb_bc"], ["cmb0"], []),
    ("rte0", ph_rte0, ["cmb0", "mla_tm", "ltstrict", "tokid_f", "tokid_i"],
     ["idx0", "cw0_tm", "me_tm"], []),
    ("moe0s", ph5es_moe0, ["idx0", "cw0_tm", "me_tm", "f_w1", "f_w2", "f_b1c", "f_b2"],
     ["moe0"], []),
    ("fmoe", ph5e_moe0, ["mla_tm", "cmb0", "f_w1", "f_w2", "f_b1c", "f_b2"], ["moe0"], []),
    ("cmb0f", ph5f_comb0, ["x", "xn0_f32", "sbr_f32", "moe0"], ["x1_f32"], []),
    ("ln1", ph6_ln1, ["x1_f32"], ["xn1_bf"], []),
    ("qk1", ph7a_qk1, ["xn1_bf", "WqT", "WkT", "cosT", "sinT"], ["qTr", "kTr"], []),
    ("vg1", ph7b_vg1, ["xn1_bf", "WvT", "WgT"], ["vtm1", "gtm1"], []),
    ("ret1", ph8_ret1, ["qTr", "kTr", "vtm1", "gtm1", "mask1", "cross1", "kdec1f", "gnw_bc"],
     ["gotm1"], []),
    ("opr1", ph9_oproj1, ["gotm1", "WoT"], ["s2_f32"], []),
    ("rtr1", ph10_rtr1, ["xn1_bf", "gate_wT"], ["cmb1"], []),
    ("moe1p", ph11_moe1, ["xn1_bf", "cmb1", "wgT_e", "wuT_e", "woT_e"], ["moe1"], []),
    ("rte1", ph_rte1, ["cmb1", "xn1_bf", "ltstrict", "tokid_f", "tokid_i"],
     ["idx1", "cw_tm", "xe_tm"], []),

    ("moe1s", ph11s_moe1, ["idx1", "cw_tm", "xe_tm", "wgT_e", "wuT_e", "woT_e"],
     ["moe1"], []),
    ("fin", ph12_final, ["x1_f32", "s2_f32", "moe1"], ["y"], []),
]


def build_group(phase_names, final_outputs=None):
    todo = [p for p in PHASES if p[0] in phase_names]
    assert len(todo) == len(phase_names)
    produced, consumed, internal = set(), set(), set()
    for _, _, ins, outs, tmps in todo:
        for n in ins:
            if n not in produced:
                consumed.add(n)
        produced.update(outs)
        internal.update(tmps)

    nc = bass.Bass("TRN2", target_bir_lowering=False, debug=False)
    ctx = Ctx(nc)
    for n in sorted(consumed):
        shape, dt = SCRATCH[n] if n in SCRATCH else EXT[n]
        ctx.S[n] = nc.dram_tensor(n, list(shape), dt, kind="ExternalInput").ap()
    for n in sorted(produced):
        shape, dt = SCRATCH[n]
        kind = "ExternalOutput" if (final_outputs is None or n in final_outputs) \
            else "Internal"
        ctx.S[n] = nc.dram_tensor(n, list(shape), dt, kind=kind).ap()
    for n in sorted(internal):
        shape, dt = SCRATCH[n]
        ctx.S[n] = nc.dram_tensor(n, list(shape), dt, kind="Internal").ap()
    block = nc.Block().__enter__()
    ctx.block = block

    const_names = [n for n in sorted(consumed) if n in CONST_SB]
    ctx.csb = {}
    cld = ctx.sem("c_ld")
    for n in const_names:
        shape, dt = EXT[n]
        ctx.csb[n] = nc.alloc_sbuf_tensor("c_" + n, list(shape), dt).ap()

    @ctx.sec("sync")
    def _(sy):
        for n in const_names:
            cld.inc(sy.dma_start(out=ctx.csb[n][:, :], in_=ctx.S[n]), 16)
    ctx.cld = cld
    ctx.consts = ctx.csb

    for pi, (name, emit, ins, outs, tmps) in enumerate(todo):
        if pi > 0:
            snap = [(s, s.n) for s in ctx.sems.values() if s.n > 0]
            for eng in ["sync", "scalar", "vector", "tensor", "gpsimd"]:
                @ctx.sec(eng)
                def _(e, snap=snap):
                    for s, n in snap:
                        e.wait_ge(s.h, n)
        ctx.new_phase()
        emit(ctx)

    @ctx.sec("sync")
    def _(sy):
        for s in ctx.sems.values():
            sy.wait_ge(s.h, s.n)
    block.__exit__(None, None, None)
    outs = sorted(produced) if final_outputs is None else sorted(
        n for n in produced if n in final_outputs)
    return nc, sorted(consumed), outs


ALL_PHASES = [p[0] for p in PHASES if p[0] not in ("moe1p", "rte0", "moe0s")]


def kernel(**inputs):
    H = stage_host(inputs)
    x = np.asarray(inputs["L0_attn_x"], np.float32)
    shards = [np.ascontiguousarray(x[2 * c:2 * c + 2].reshape(NT, D)) for c in range(8)]
    nc, ins, outs = build_group(ALL_PHASES, final_outputs={"y"})
    in_maps = []
    for c in range(8):
        im = {}
        for n in ins:
            im[n] = shards[c] if n == "x" else H[n]
        in_maps.append(im)
    res = run_bass_kernel_spmd(nc, in_maps, core_ids=list(range(8)))
    y = np.concatenate([np.asarray(res.results[c]["y"]).reshape(2, T, D)
                        for c in range(8)], axis=0)
    return y


# version 12 (final)
# speedup vs baseline: 1.0100x; 1.0100x over previous
import math
import numpy as np
import ml_dtypes
import concourse.bass as bass
import concourse.mybir as mybir
from concourse.bass_utils import run_bass_kernel_spmd
from concourse.bass_core_rust import IndirectOffsetOnAxis

F32 = mybir.dt.float32
BF16 = mybir.dt.bfloat16
I32 = mybir.dt.int32
AF = mybir.ActivationFunctionType
ALU = mybir.AluOpType
AX = mybir.AxisListType

NT = 4096
T = 2048
D = 1024
NTILE = NT // 128
EPS = 1e-5


class SemCtr:
    def __init__(self, h):
        self.h = h
        self.n = 0
        self.base = 0

    def inc(self, inst, k=1):
        inst.then_inc(self.h, k)
        self.n += k
        return inst

    def wait(self, eng, val):
        eng.wait_ge(self.h, self.base + val)

    @property
    def rel(self):
        return self.n - self.base


class Arena:
    def __init__(self, nc, kb_per_part=184):
        self.bytes = kb_per_part * 1024
        self.t = nc.alloc_sbuf_tensor("arena", [128, self.bytes // 4], F32)
        self.off = 0

    def reset(self):
        self.off = 0

    def alloc(self, shape, dtype):
        assert len(shape) == 2 and shape[0] <= 128
        isz = 2 if dtype == BF16 else 4
        need = shape[1] * isz
        off = (self.off + 511) & ~511
        assert off + need <= self.bytes, f"arena overflow {off}+{need}"
        self.off = off + need
        ap = self.t.ap()[0:shape[0], :]
        if dtype == BF16:
            return ap.bitcast(BF16)[:, off // 2: off // 2 + shape[1]]
        if dtype == I32:
            return ap.bitcast(I32)[:, off // 4: off // 4 + shape[1]]
        return ap[:, off // 4: off // 4 + shape[1]]


class Ctx:
    def __init__(self, nc):
        self.nc = nc
        self.block = None
        self.sems = {}
        self.local_map = {}
        self.ar = Arena(nc)
        self.pp = [nc.alloc_psum_tensor(f"pp{i}", [128, 512], F32) for i in range(8)]
        self.S = {}
        self.consts = None

    def sem(self, name):
        if name.startswith("selfsync") or name == "c_ld":
            if name not in self.sems:
                self.sems[name] = SemCtr(self.nc.semaphore(name).__enter__())
            return self.sems[name]
        if name in self.local_map:
            return self.local_map[name]
        idx = len(self.local_map)
        key = f"pool{idx}"
        if key not in self.sems:
            self.sems[key] = SemCtr(self.nc.semaphore(key).__enter__())
        s = self.sems[key]
        s.base = s.n
        self.local_map[name] = s
        return s

    def new_phase(self):
        self.local_map = {}

    def sec(self, eng):
        return getattr(self.block, eng)

    def fence(self, eng_obj, inst):
        s = self.sem("selfsync_" + type(eng_obj).__name__)
        s.inc(inst)
        eng_obj.wait_ge(s.h, s.n)


def view3(ap, g, c):
    return ap.rearrange("p (g c) -> p g c", g=g, c=c)



def ph1_ln0(ctx):
    nc, ar, S = ctx.nc, ctx.ar, ctx.S
    ar.reset()
    x = S["x"]
    xn_f = S["xn0_f32"]
    xn_b = S["xn0_bf"]
    gn1ab = S["gn1ab"]
    stat_tmp = S["ph1_stat"]
    dbg = S["p1dbg"]
    C = ctx.consts

    xt = [ar.alloc([128, 1024], F32) for _ in range(2)]
    sq = [ar.alloc([128, 1024], F32) for _ in range(2)]
    dump = ar.alloc([128, 1024], F32)
    xnf = [ar.alloc([128, 1024], F32) for _ in range(2)]
    xnb = [ar.alloc([128, 1024], BF16) for _ in range(2)]
    gcat = [ar.alloc([128, 32], F32) for _ in range(2)]
    stats = ar.alloc([128, 128], F32)
    statq = ar.alloc([128, 128], F32)
    vv = ar.alloc([128, 32], F32)
    rstd = ar.alloc([128, 32], F32)
    stat = ar.alloc([128, 8], F32)
    rfm = ar.alloc([128, 8], F32)
    mfm = ar.alloc([128, 8], F32)
    afm = ar.alloc([128, 8], F32)
    bfm = ar.alloc([128, 8], F32)

    s_x = ctx.sem("p1_x")
    sA = ctx.sem("p1_A")
    sSB = ctx.sem("p1_SB")
    s_x2 = ctx.sem("p1_x2")
    sD = ctx.sem("p1_D")
    sE = ctx.sem("p1_E")
    sF = ctx.sem("p1_F")
    sG = ctx.sem("p1_G")
    sST = ctx.sem("p1_ST")
    sV2 = ctx.sem("p1_V2")
    sC2 = ctx.sem("p1_C2")
    sPM = ctx.sem("p1_PM")
    sV3 = ctx.sem("p1_V3")
    sAB = ctx.sem("p1_AB")

    pp0, pp1 = ctx.pp[0], ctx.pp[1]

    for i in range(NTILE):
        buf = i % 2

        @ctx.sec("sync")
        def _(sy, i=i, buf=buf):
            if i >= 2:
                sA.wait(sy, i - 1)
            s_x.inc(sy.dma_start(out=xt[buf][:], in_=x[i * 128:(i + 1) * 128, :]), 16)

        @ctx.sec("scalar")
        def _(sc, i=i, buf=buf):
            s_x.wait(sc, 16 * (i + 1))
            sc.activation(dump[:], xt[buf][:], AF.Identity, scale=1.0 / 1024,
                          accum_out=stats[:, 4 * i:4 * i + 1])
            sA.inc(sc.activation(dump[:], xt[buf][:], AF.Square, scale=1.0 / 32,
                                 accum_out=statq[:, 4 * i:4 * i + 1]))

    @ctx.sec("vector")
    def _(v):
        sA.wait(v, NTILE)
        muv = view3(stats, 32, 4)[:, :, 0:1]
        msqv = view3(statq, 32, 4)[:, :, 0:1]
        inst = v.tensor_mul(vv[:].rearrange("p (g c) -> p g c", c=1), muv, muv)
        ctx.fence(v, inst)
        inst = v.scalar_tensor_tensor(vv[:].rearrange("p (g c) -> p g c", c=1),
                                      vv[:].rearrange("p (g c) -> p g c", c=1),
                                      -1.0, msqv, ALU.mult, ALU.add)
        ctx.fence(v, inst)
        sA.inc(v.tensor_scalar_add(vv[:], vv[:], EPS))

    @ctx.sec("scalar")
    def _(sc):
        sA.wait(sc, NTILE + 1)
        inst = sc.activation(rstd[:], vv[:], AF.Ln)
        ctx.fence(sc, inst)
        sSB.inc(sc.activation(rstd[:], rstd[:], AF.Exp, scale=-0.5))

    for i in range(NTILE):
        b = i // 16
        buf = i % 2
        first, last = i % 16 == 0, i % 16 == 15

        @ctx.sec("sync")
        def _(sy, i=i, buf=buf):
            if i < 2:
                sA.wait(sy, NTILE)
            else:
                sD.wait(sy, i - 1)
                sE.wait(sy, i - 1)
            s_x2.inc(sy.dma_start(out=xt[buf][:], in_=x[i * 128:(i + 1) * 128, :]), 16)

        @ctx.sec("vector")
        def _(v, i=i, buf=buf):
            s_x2.wait(v, 16 * (i + 1))
            sSB.wait(v, 1)
            if i >= 2:
                sE.wait(v, i - 1)
                sST.wait(v, 32 * (i - 1))
                sG.wait(v, i - 1)
            v.tensor_scalar(xnf[buf][:], xt[buf][:], stats[:, 4 * i:4 * i + 1],
                            rstd[:, i:i + 1], ALU.subtract, ALU.mult)
            v.tensor_copy(xnb[buf][:], xnf[buf][:])
            sD.inc(v.reduce_sum(gcat[buf][:, 0:16], view3(xnf[buf], 16, 64), axis=AX.X))

        @ctx.sec("scalar")
        def _(sc, i=i, buf=buf):
            sD.wait(sc, i + 1)
            sE.inc(sc.activation(sq[buf][:], xnf[buf][:], AF.Square))

        @ctx.sec("vector")
        def _(v, i=i, buf=buf):
            sE.wait(v, i + 1)
            sF.inc(v.reduce_sum(gcat[buf][:, 16:32], view3(sq[buf], 16, 64), axis=AX.X))

        @ctx.sec("tensor")
        def _(t, i=i, buf=buf, first=first, last=last):
            if i == 0:
                t.wait_ge(ctx.cld.h, ctx.cld.n)
            sF.wait(t, i + 1)
            if i == 16:
                sV2.wait(t, 4)
            t.matmul(pp0[0:16, 0:1], gcat[buf][:, 0:16], C["ones"][:, 0:1],
                     start=first, stop=last)
            sG.inc(t.matmul(pp1[0:16, 0:1], gcat[buf][:, 16:32], C["ones"][:, 0:1],
                            start=first, stop=last))

        @ctx.sec("sync")
        def _(sy, i=i, buf=buf):
            sD.wait(sy, i + 1)
            sST.inc(sy.dma_start(out=xn_f[i * 128:(i + 1) * 128, :], in_=xnf[buf][:]), 16)
            sST.inc(sy.dma_start(out=xn_b[i * 128:(i + 1) * 128, :], in_=xnb[buf][:]), 16)

        if last:
            @ctx.sec("vector")
            def _(v, b=b):
                if b == 0:
                    v.wait_ge(ctx.cld.h, ctx.cld.n)
                sG.wait(v, 16 * (b + 1))
                if b == 1:
                    sPM.wait(v, 64)
                v.tensor_scalar_mul(stat[0:16, 0:1], pp0[0:16, 0:1], 1.0 / (64 * 2048))
                inst = v.tensor_scalar_mul(stat[0:16, 1:2], pp1[0:16, 0:1], 1.0 / (64 * 2048))
                ctx.fence(v, inst)
                inst = v.tensor_mul(stat[0:16, 2:3], stat[0:16, 0:1], stat[0:16, 0:1])
                ctx.fence(v, inst)
                inst = v.scalar_tensor_tensor(stat[0:16, 2:3], stat[0:16, 2:3], -1.0,
                                              stat[0:16, 1:2], ALU.mult, ALU.add)
                ctx.fence(v, inst)
                sV2.inc(v.tensor_scalar_add(stat[0:16, 2:3], stat[0:16, 2:3], EPS), 4)

            @ctx.sec("scalar")
            def _(sc, b=b):
                sV2.wait(sc, 4 * (b + 1))
                inst = sc.activation(stat[0:16, 3:4], stat[0:16, 2:3], AF.Ln)
                ctx.fence(sc, inst)
                sC2.inc(sc.activation(stat[0:16, 3:4], stat[0:16, 3:4], AF.Exp, scale=-0.5))

            @ctx.sec("sync")
            def _(sy, b=b):
                sC2.wait(sy, b + 1)
                sPM.inc(sy.dma_start(
                    out=stat_tmp[b, 0].rearrange("(h c) -> c h", h=2, c=8),
                    in_=stat[0:16, 3:4], allow_slow_non_contiguous=True), 16)
                sPM.inc(sy.dma_start(
                    out=stat_tmp[b, 1].rearrange("(h c) -> c h", h=2, c=8),
                    in_=stat[0:16, 0:1], allow_slow_non_contiguous=True), 16)
                sPM.wait(sy, 64 * b + 32)
                if b == 1:
                    sV3.wait(sy, 3)
                src_r = stat_tmp[b, 0].rearrange("(h c) -> h c", h=2) \
                    .rearrange("h (c u) -> h u c", u=1).broadcast_to([2, 64, 8])
                src_m = stat_tmp[b, 1].rearrange("(h c) -> h c", h=2) \
                    .rearrange("h (c u) -> h u c", u=1).broadcast_to([2, 64, 8])
                sPM.inc(sy.dma_start(out=rfm[:], in_=src_r), 16)
                sPM.inc(sy.dma_start(out=mfm[:], in_=src_m), 16)

            @ctx.sec("vector")
            def _(v, b=b):
                sPM.wait(v, 64 * (b + 1))
                if b == 1:
                    sAB.wait(v, 32)
                inst = v.tensor_mul(afm[:], rfm[:], C["gn1_g"][:])
                ctx.fence(v, inst)
                inst = v.tensor_mul(bfm[:], mfm[:], afm[:])
                ctx.fence(v, inst)
                sV3.inc(v.scalar_tensor_tensor(bfm[:], bfm[:], -1.0, C["gn1_b"][:],
                                               ALU.mult, ALU.add), 3)

            @ctx.sec("sync")
            def _(sy, b=b):
                sV3.wait(sy, 3 * (b + 1))
                sAB.inc(sy.dma_start(out=gn1ab[b, 0], in_=afm[:]), 16)
                sAB.inc(sy.dma_start(out=gn1ab[b, 1], in_=bfm[:]), 16)
                if b == 1:
                    sDBG = ctx.sem("p1_dbg")
                    sDBG.inc(sy.dma_start(out=dbg[:, 0:32], in_=view3(stats, 32, 4)[:, :, 0:1], allow_slow_non_contiguous=True), 16)
                    sDBG.inc(sy.dma_start(out=dbg[:, 32:64], in_=view3(statq, 32, 4)[:, :, 0:1], allow_slow_non_contiguous=True), 16)
                    sDBG.inc(sy.dma_start(out=dbg[:, 64:96], in_=vv[:]), 16)
                    sDBG.inc(sy.dma_start(out=dbg[:, 96:128], in_=rstd[:]), 16)






def ph2a_qk0(ctx):
    nc, ar, S = ctx.nc, ctx.ar, ctx.S
    ar.reset()
    xn_b = S["xn0_bf"]
    gn1ab = S["gn1ab"]

    wq = ar.alloc([128, 8192], BF16)
    wk = ar.alloc([128, 8192], BF16)
    ab = ar.alloc([128, 32], F32)
    ga = [ar.alloc([128, 512], BF16) for _ in range(16)]
    qk_bf = [ar.alloc([128, 512], BF16) for _ in range(4)]
    sqv = [ar.alloc([128, 512], F32) for _ in range(2)]
    ssm = ar.alloc([128, 32], F32)
    fq = ar.alloc([128, 32], F32)
    fk = ar.alloc([128, 32], F32)

    s_w = ctx.sem("p2_w")
    s_a = ctx.sem("p2_a")
    s_ab = ctx.sem("p2_ab")
    s_mm = ctx.sem("p2_mm")
    s_ev = ctx.sem("p2_ev")
    s_sq = ctx.sem("p2_sq")
    s_qm = ctx.sem("p2_qm")
    s_st = ctx.sem("p2_st")
    s_f = ctx.sem("p2_f")

    @ctx.sec("gpsimd")
    def _(g):
        load_wT(ctx, g, S["qwT"], 1024, 1024, wq, s_w)
        load_wT(ctx, g, S["kwT"], 1024, 1024, wk, s_w)

    @ctx.sec("sync")
    def _(sy):
        for b in range(2):
            s_ab.inc(sy.dma_start(out=ab[:, 16 * b:16 * b + 8], in_=gn1ab[b, 0]), 16)
            s_ab.inc(sy.dma_start(out=ab[:, 16 * b + 8:16 * b + 16], in_=gn1ab[b, 1]), 16)

    ps_ss = ctx.pp[4]
    first_ss = [True]

    NTB = 8
    for tb in range(NTB):
        bt = tb // 4
        gbuf = [ga[(tb % 2) * 8 + dc] for dc in range(8)]

        @ctx.sec("sync")
        def _(sy, tb=tb, gbuf=gbuf):
            if tb >= 2:
                s_mm.wait(sy, (tb - 1) * 16)
            for dc in range(8):
                tload(ctx, sy, xn_b, tb * 512, 512, dc * 128, 128, gbuf[dc], s_a)

        @ctx.sec("vector")
        def _(v, tb=tb, bt=bt, gbuf=gbuf):
            s_a.wait(v, 16 * 8 * (tb + 1))
            if tb == 0:
                s_ab.wait(v, 64)
            for dc in range(8):
                v.tensor_scalar(gbuf[dc][:], gbuf[dc][:],
                                ab[:, 16 * bt + dc:16 * bt + dc + 1],
                                ab[:, 16 * bt + 8 + dc:16 * bt + 9 + dc],
                                ALU.mult, ALU.add)
            s_a.inc(v.engine_nop())

        for ti, (wsb, scr, colbase) in enumerate([(wq, "qT0", 0), (wk, "kT0", 16)]):
            for oc in range(8):
                pb = ctx.pp[(oc + ti * 8) % 2]
                obuf = qk_bf[(oc + ti * 8) % 4]
                sqb = sqv[(oc + ti * 8) % 2]

                @ctx.sec("tensor")
                def _(t, tb=tb, ti=ti, oc=oc, pb=pb, wsb=wsb, gbuf=gbuf):
                    if tb == 0 and ti == 0 and oc == 0:
                        s_w.wait(t, 32)
                    s_a.wait(t, (16 * 8 + 1) * (tb + 1))
                    k = (tb * 16 + ti * 8 + oc)
                    if k >= 2:
                        s_ev.wait(t, k - 1)
                        s_sq.wait(t, k - 1)
                    for dc in range(8):
                        inst = t.matmul(pb[:, :],
                                        wslice(wsb, 1024, dc, oc * 128, oc * 128 + 128),
                                        gbuf[dc][:], start=(dc == 0), stop=(dc == 7))
                    s_mm.inc(inst)

                @ctx.sec("vector")
                def _(v, tb=tb, ti=ti, oc=oc, pb=pb, obuf=obuf):
                    k = (tb * 16 + ti * 8 + oc)
                    s_mm.wait(v, k + 1)
                    if k >= 4:
                        s_st.wait(v, 16 * (k - 3))
                    s_ev.inc(v.tensor_copy(obuf[:], pb[:, :]))

                @ctx.sec("scalar")
                def _(sc, tb=tb, ti=ti, oc=oc, pb=pb, sqb=sqb):
                    k = (tb * 16 + ti * 8 + oc)
                    s_mm.wait(sc, k + 1)
                    if k >= 2:
                        s_qm.wait(sc, 4 * (k - 1))
                    s_sq.inc(sc.activation(sqb[:], pb[:, :], AF.Square))

                @ctx.sec("tensor")
                def _(t, tb=tb, ti=ti, oc=oc, sqb=sqb, colbase=colbase):
                    k = (tb * 16 + ti * 8 + oc)
                    s_sq.wait(t, k + 1)
                    for tc in range(4):
                        gtc = tb * 4 + tc
                        col = colbase * 2 + gtc
                        st = first_ss[0]
                        first_ss[0] = False
                        s_qm.inc(t.matmul(ps_ss[:, col:col + 1],
                                          sqb[:, tc * 128:tc * 128 + 128],
                                          ctx.consts["ones"][:, 0:1],
                                          start=st, stop=True, skip_group_check=True))

                @ctx.sec("gpsimd")
                def _(g, tb=tb, ti=ti, oc=oc, obuf=obuf, scr=scr):
                    k = (tb * 16 + ti * 8 + oc)
                    s_ev.wait(g, k + 1)
                    s_st.inc(g.dma_start(
                        out=S[scr][oc * 128:(oc + 1) * 128, tb * 512:(tb + 1) * 512],
                        in_=obuf[:]), 16)

    @ctx.sec("vector")
    def _(v):
        s_qm.wait(v, 4 * 16 * NTB)
        inst = v.tensor_scalar_max(ssm[:], ps_ss[:, 0:32], 1e-24)
        ctx.fence(v, inst)
        s_f.inc(v.engine_nop())

    @ctx.sec("scalar")
    def _(sc):
        s_f.wait(sc, 1)
        inst = sc.activation(fq[:], ssm[:], AF.Ln)
        ctx.fence(sc, inst)
        s_f.inc(sc.activation(fq[:], fq[:], AF.Exp, scale=-0.5))

    @ctx.sec("vector")
    def _(v):
        s_f.wait(v, 2)
        inst = v.tensor_scalar_max(ssm[:], ps_ss[:, 32:64], 1e-24)
        ctx.fence(v, inst)
        s_f.inc(v.engine_nop())

    @ctx.sec("scalar")
    def _(sc):
        s_f.wait(sc, 3)
        inst = sc.activation(fk[:], ssm[:], AF.Ln)
        ctx.fence(sc, inst)
        s_f.inc(sc.activation(fk[:], fk[:], AF.Exp, scale=-0.5))

    @ctx.sec("sync")
    def _(sy):
        s_f.wait(sy, 4)
        s_f.inc(sy.dma_start(out=S["fq0"], in_=fq[:]), 16)
        s_f.inc(sy.dma_start(out=S["fk0"], in_=fk[:]), 16)



def ph2b_kv0(ctx):
    nc, ar, S = ctx.nc, ctx.ar, ctx.S
    ar.reset()
    xn_b = S["xn0_bf"]
    gn1ab = S["gn1ab"]

    wv = ar.alloc([128, 8192], BF16)
    wk = ar.alloc([128, 8192], BF16)
    ab = ar.alloc([128, 32], F32)
    fkc = ar.alloc([128, 32], F32)
    ga = [ar.alloc([128, 512], BF16) for _ in range(16)]
    ob = [ar.alloc([128, 512], BF16) for _ in range(4)]

    s_w = ctx.sem("p2b_w")
    s_a = ctx.sem("p2b_a")
    s_mm = ctx.sem("p2b_mm")
    s_ev = ctx.sem("p2b_ev")
    s_st = ctx.sem("p2b_st")

    @ctx.sec("gpsimd")
    def _(g):
        load_wT(ctx, g, S["vwT"], 1024, 1024, wv, s_w)
        load_wT(ctx, g, S["kwT"], 1024, 1024, wk, s_w)

    @ctx.sec("sync")
    def _(sy):
        for b in range(2):
            s_w.inc(sy.dma_start(out=ab[:, 16 * b:16 * b + 8], in_=gn1ab[b, 0]), 16)
            s_w.inc(sy.dma_start(out=ab[:, 16 * b + 8:16 * b + 16], in_=gn1ab[b, 1]), 16)
        s_w.inc(sy.dma_start(out=fkc[:], in_=S["fk0"]), 16)

    for tb in range(8):
        bt = tb // 4
        gbuf = [ga[(tb % 2) * 8 + dc] for dc in range(8)]

        @ctx.sec("sync")
        def _(sy, tb=tb, gbuf=gbuf):
            if tb >= 2:
                s_mm.wait(sy, (tb - 1) * 16)
            for dc in range(8):
                tload(ctx, sy, xn_b, tb * 512, 512, dc * 128, 128, gbuf[dc], s_a)

        @ctx.sec("vector")
        def _(v, tb=tb, bt=bt, gbuf=gbuf):
            s_a.wait(v, 16 * 8 * (tb + 1))
            if tb == 0:
                s_w.wait(v, 32 + 80)
            for dc in range(8):
                v.tensor_scalar(gbuf[dc][:], gbuf[dc][:],
                                ab[:, 16 * bt + dc:16 * bt + dc + 1],
                                ab[:, 16 * bt + 8 + dc:16 * bt + 9 + dc],
                                ALU.mult, ALU.add)
            s_a.inc(v.engine_nop())

        for tc in range(4):
            gt = tb * 4 + tc
            for ti, (wsb, scr) in enumerate([(wv, "vtm0"), (wk, "ktm0")]):
                for oc in range(2):
                    k = gt * 4 + ti * 2 + oc
                    pb = ctx.pp[k % 2]
                    obuf = ob[k % 4]

                    @ctx.sec("tensor")
                    def _(t, tb=tb, tc=tc, ti=ti, oc=oc, k=k, pb=pb, wsb=wsb, gbuf=gbuf):
                        if k == 0:
                            s_w.wait(t, 112)
                        s_a.wait(t, (16 * 8 + 1) * (tb + 1))
                        if k >= 2:
                            s_ev.wait(t, k - 1)
                        for dc in range(8):
                            inst = t.matmul(pb[:, :],
                                            gbuf[dc][:, tc * 128:tc * 128 + 128],
                                            wslice(wsb, 1024, dc, oc * 512, oc * 512 + 512),
                                            start=(dc == 0), stop=(dc == 7))
                        s_mm.inc(inst)

                    @ctx.sec("vector")
                    def _(v, ti=ti, k=k, gt=gt, pb=pb, obuf=obuf):
                        s_mm.wait(v, k + 1)
                        if k >= 4:
                            s_st.wait(v, 16 * (k - 3))
                        if ti == 0:
                            s_ev.inc(v.tensor_copy(obuf[:], pb[:, :]))
                        else:
                            s_ev.inc(v.tensor_scalar_mul(obuf[:], pb[:, :],
                                                         fkc[:, gt:gt + 1]))

                    @ctx.sec("sync")
                    def _(sy, k=k, gt=gt, oc=oc, obuf=obuf, scr=scr):
                        s_ev.wait(sy, k + 1)
                        s_st.inc(sy.dma_start(
                            out=S[scr][gt * 128:(gt + 1) * 128, oc * 512:(oc + 1) * 512],
                            in_=obuf[:]), 16)




def ph3a_ret0(ctx):
    nc, ar, S = ctx.nc, ctx.ar, ctx.S
    ar.reset()

    qT_sb = ar.alloc([128, 16384], BF16)
    kT_sb = ar.alloc([128, 16384], BF16)
    mask_sb = ar.alloc([128, 2048], F32)
    cross_sb = ar.alloc([128, 1024], F32)
    kdec_sb = ar.alloc([128, 1024], F32)
    fq_sb = ar.alloc([128, 32], F32)
    fk_sb = ar.alloc([128, 32], F32)
    kbuf = [ar.alloc([128, 1024], BF16) for _ in range(2)]
    vbuf = [ar.alloc([128, 1024], BF16) for _ in range(2)]
    ksc = [ar.alloc([128, 1024], BF16) for _ in range(2)]
    qc_bf = ar.alloc([128, 1024], BF16)
    simb = ar.alloc([128, 2048], BF16)
    S_sb = ar.alloc([128, 512], F32)
    S_bf = ar.alloc([128, 1024], BF16)
    o_sb = [ar.alloc([128, 1024], BF16) for _ in range(2)]

    s_c = ctx.sem("p3_c")
    s_qk = ctx.sem("p3_qk")
    s_kv = ctx.sem("p3_kv")
    s_pre = ctx.sem("p3_pre")
    s_simm = ctx.sem("p3_simm")
    s_tin = ctx.sem("p3_tin")
    s_sev = ctx.sem("p3_sev")
    s_tch = ctx.sem("p3_tch")
    s_sst = ctx.sem("p3_sst")
    s_sbf = ctx.sem("p3_sbf")
    s_oev = ctx.sem("p3_oev")
    s_ost = ctx.sem("p3_ost")

    import math as _m
    gam = [1.0 - _m.exp(lo) for lo in
           [(_m.log(1 / 32.) + (_m.log(1 / 512.) - _m.log(1 / 32.)) * h / 15.0)
            for h in range(16)]]
    gC = [g ** 128 for g in gam]

    @ctx.sec("sync")
    def _(sy):
        s_c.inc(sy.dma_start(out=mask_sb[:], in_=S["mask0"]), 16)
        s_c.inc(sy.dma_start(out=cross_sb[:], in_=S["cross0"]), 16)
        s_c.inc(sy.dma_start(out=kdec_sb[:], in_=S["kdec0f"]), 16)
        s_c.inc(sy.dma_start(out=fq_sb[:], in_=S["fq0"]), 16)
        s_c.inc(sy.dma_start(out=fk_sb[:], in_=S["fk0"]), 16)

    for b in range(2):
        @ctx.sec("sync")
        def _(sy, b=b):
            if b == 1:
                s_tch.wait(sy, 16)
                s_pre.wait(sy, 16)
            src_q = S["qT0"].rearrange("(c p) t -> p c t", p=128)[:, :, b * 2048:(b + 1) * 2048]
            src_k = S["kT0"].rearrange("(c p) t -> p c t", p=128)[:, :, b * 2048:(b + 1) * 2048]
            s_qk.inc(sy.dma_start(out=qT_sb.rearrange("p (c t) -> p c t", c=8), in_=src_q), 16)
            s_qk.inc(sy.dma_start(out=kT_sb.rearrange("p (c t) -> p c t", c=8), in_=src_k), 16)

        @ctx.sec("vector")
        def _(v, b=b):
            if b == 1:
                s_tch.wait(v, 16)
            v.memset(S_sb[:], 0.0)
            s_sbf.inc(v.memset(S_bf[:], 0.0))

        for j in range(16):
            jj = b * 16 + j

            @ctx.sec("sync")
            def _(sy, jj=jj, j=j):
                if jj >= 2:
                    s_tch.wait(sy, jj - 1)
                buf = jj % 2
                s_kv.inc(sy.dma_start(out=kbuf[buf][:],
                                      in_=S["ktm0"][jj * 128:(jj + 1) * 128, :]), 16)
                s_kv.inc(sy.dma_start(out=vbuf[buf][:],
                                      in_=S["vtm0"][jj * 128:(jj + 1) * 128, :]), 16)

            @ctx.sec("vector")
            def _(v, b=b, j=j, jj=jj):
                buf = jj % 2
                s_kv.wait(v, 32 * (jj + 1))
                s_qk.wait(v, 32 * (b + 1))
                if jj == 0:
                    s_c.wait(v, 80)
                if jj >= 1:
                    s_tch.wait(v, jj)
                v.tensor_mul(ksc[buf][:], kbuf[buf][:], kdec_sb[:])
                for g in range(8):
                    v.tensor_mul(qc_bf[:, g * 128:(g + 1) * 128],
                                 qT_sb[:, g * 2048 + j * 128:g * 2048 + j * 128 + 128],
                                 cross_sb[:, g * 128:(g + 1) * 128])
                s_pre.inc(v.engine_nop())

            for h in range(16):
                @ctx.sec("tensor")
                def _(t, b=b, j=j, jj=jj, h=h):
                    if jj == 0 and h == 0:
                        s_c.wait(t, 80)
                    s_qk.wait(t, 32 * (b + 1))
                    hh = jj * 16 + h
                    if hh >= 4:
                        s_sev.wait(t, hh - 3)
                    p0 = 64 * (h % 2)
                    c = h // 2
                    col = c * 2048 + j * 128
                    s_simm.inc(t.matmul(
                        ctx.pp[4 + h % 4][:, 0:128],
                        kT_sb[p0:p0 + 64, col:col + 128],
                        qT_sb[p0:p0 + 64, col:col + 128],
                        start=True, stop=True))

            for h in range(16):
                @ctx.sec("vector")
                def _(v, jj=jj, h=h):
                    hh = jj * 16 + h
                    s_simm.wait(v, hh + 1)
                    if jj >= 1:
                        s_tin.wait(v, (jj - 1) * 16 + h + 1)
                    s_sev.inc(v.scalar_tensor_tensor(
                        simb[:, h * 128:(h + 1) * 128], ctx.pp[4 + h % 4][:, 0:128],
                        fk_sb[:, jj:jj + 1], mask_sb[:, h * 128:(h + 1) * 128],
                        ALU.mult, ALU.mult))

            sst_target = s_sst.rel

            @ctx.sec("tensor")
            def _(t, b=b, j=j, jj=jj, sst_target=sst_target):
                buf = jj % 2
                s_pre.wait(t, jj + 1)
                s_oev.wait(t, 2 * jj)
                last = None
                for h in range(16):
                    hh = jj * 16 + h
                    s_sev.wait(t, hh + 1)
                    pb = ctx.pp[0] if h < 8 else ctx.pp[1]
                    last = t.matmul(pb[:, (h % 8) * 64:(h % 8) * 64 + 64],
                                    simb[:, h * 128:(h + 1) * 128],
                                    vbuf[buf][:, h * 64:h * 64 + 64],
                                    start=(h % 8 == 0), stop=False, skip_group_check=True)
                    s_tin.inc(last)
                if j > 0:
                    s_sbf.wait(t, b * 17 + 1 + j)
                    for g in range(8):
                        pb = ctx.pp[0] if g < 4 else ctx.pp[1]
                        last = t.matmul(pb[:, (g % 4) * 128:(g % 4) * 128 + 128],
                                        qc_bf[:, g * 128:(g + 1) * 128],
                                        S_bf[:, g * 128:(g + 1) * 128],
                                        start=False, stop=(g % 4 == 3),
                                        skip_group_check=True)
                if j < 15:
                    s_sst.wait(t, sst_target)
                    for g in range(8):
                        t.matmul(ctx.pp[2][:, g * 64:g * 64 + 64],
                                 ksc[buf][:, g * 128:(g + 1) * 128],
                                 vbuf[buf][:, (2 * g) * 64:(2 * g) * 64 + 64],
                                 start=(g == 0), stop=True, skip_group_check=True)
                        last = t.matmul(
                            ctx.pp[3][:, g * 64:g * 64 + 64],
                            ksc[buf][:, g * 128:(g + 1) * 128],
                            vbuf[buf][:, (2 * g + 1) * 64:(2 * g + 1) * 64 + 64],
                            start=(g == 0), stop=True, skip_group_check=True)
                s_tch.inc(last)

            if j < 15:
                @ctx.sec("vector")
                def _(v, b=b, j=j, jj=jj):
                    s_tch.wait(v, jj + 1)
                    for g in range(8):
                        s_sst.inc(v.scalar_tensor_tensor(
                            S_sb[0:64, g * 64:g * 64 + 64], S_sb[0:64, g * 64:g * 64 + 64],
                            gC[2 * g], ctx.pp[2][0:64, g * 64:g * 64 + 64],
                            ALU.mult, ALU.add))
                        s_sst.inc(v.scalar_tensor_tensor(
                            S_sb[64:128, g * 64:g * 64 + 64], S_sb[64:128, g * 64:g * 64 + 64],
                            gC[2 * g + 1], ctx.pp[3][64:128, g * 64:g * 64 + 64],
                            ALU.mult, ALU.add))
                    for g in range(8):
                        v.tensor_copy(S_bf[0:64, g * 128:g * 128 + 64],
                                      S_sb[0:64, g * 64:g * 64 + 64])
                        v.tensor_copy(S_bf[64:128, g * 128 + 64:g * 128 + 128],
                                      S_sb[64:128, g * 64:g * 64 + 64])
                    s_sbf.inc(v.engine_nop())
            else:
                @ctx.sec("vector")
                def _(v, jj=jj):
                    s_tch.wait(v, jj + 1)
                    s_sbf.inc(v.engine_nop())

            @ctx.sec("vector")
            def _(v, jj=jj):
                s_tch.wait(v, jj + 1)
                if jj >= 2:
                    s_ost.wait(v, 16 * (jj - 1))
                ob = o_sb[jj % 2]
                v.tensor_scalar_mul(ob[:, 0:512], ctx.pp[0][:, :], fq_sb[:, jj:jj + 1])
                s_oev.inc(v.tensor_scalar_mul(ob[:, 512:1024], ctx.pp[1][:, :],
                                              fq_sb[:, jj:jj + 1]), 2)

            @ctx.sec("sync")
            def _(sy, jj=jj):
                s_oev.wait(sy, 2 * (jj + 1))
                s_ost.inc(sy.dma_start(out=S["otm0"][jj * 128:(jj + 1) * 128, :],
                                       in_=o_sb[jj % 2][:]), 16)



def ph3b_oproj0(ctx):
    nc, ar, S = ctx.nc, ctx.ar, ctx.S
    ar.reset()

    wo = ar.alloc([128, 8192], BF16)
    obbc = ar.alloc([128, 1024], F32)
    oT = [ar.alloc([128, 512], BF16) for _ in range(16)]
    xnt = [ar.alloc([128, 1024], F32) for _ in range(2)]
    s1f = [ar.alloc([128, 1024], F32) for _ in range(2)]
    s1b = [ar.alloc([128, 1024], BF16) for _ in range(2)]
    sq = [ar.alloc([128, 1024], F32) for _ in range(2)]
    gcat = [ar.alloc([128, 32], F32) for _ in range(2)]
    stat = ar.alloc([128, 8], F32)
    rfm = ar.alloc([128, 8], F32)
    mfm = ar.alloc([128, 8], F32)
    afm = ar.alloc([128, 8], F32)
    bfm = ar.alloc([128, 8], F32)

    s_w = ctx.sem("p3b_w")
    s_a = ctx.sem("p3b_a")
    s_x = ctx.sem("p3b_x")
    s_mm = ctx.sem("p3b_mm")
    s_ev = ctx.sem("p3b_ev")
    s_e2 = ctx.sem("p3b_e2")
    s_f2 = ctx.sem("p3b_f2")
    s_g2 = ctx.sem("p3b_g2")
    s_st = ctx.sem("p3b_st")
    sV2 = ctx.sem("p3b_V2")
    sC2 = ctx.sem("p3b_C2")
    sPM = ctx.sem("p3b_PM")
    sV3 = ctx.sem("p3b_V3")
    sAB = ctx.sem("p3b_AB")

    @ctx.sec("gpsimd")
    def _(g):
        load_wT(ctx, g, S["owT"], 1024, 1024, wo, s_w)

    @ctx.sec("sync")
    def _(sy):
        s_w.inc(sy.dma_start(out=obbc[:], in_=S["ob_bc"]), 16)

    pp6, pp7 = ctx.pp[6], ctx.pp[7]

    for i in range(NTILE):
        b = i // 16
        buf = i % 2
        first, last = i % 16 == 0, i % 16 == 15
        tb = i // 4

        if i % 4 == 0:
            @ctx.sec("sync")
            def _(sy, i=i, tb=tb):
                if tb >= 2:
                    s_mm.wait(sy, 2 * (tb - 1) * 4)
                for dc in range(8):
                    tload(ctx, sy, S["otm0"], tb * 512, 512, dc * 128, 128,
                          oT[(tb % 2) * 8 + dc], s_a)

        @ctx.sec("sync")
        def _(sy, i=i, buf=buf):
            if i >= 2:
                s_ev.wait(sy, i - 1)
            s_x.inc(sy.dma_start(out=xnt[buf][:],
                                 in_=S["xn0_f32"][i * 128:(i + 1) * 128, :]), 16)

        for oc in range(2):
            @ctx.sec("tensor")
            def _(t, i=i, tb=tb, oc=oc):
                if i == 0 and oc == 0:
                    s_w.wait(t, 32)
                s_a.wait(t, 128 * (tb + 1))
                k = 2 * i + oc
                if k >= 4:
                    s_ev.wait(t, i - 1)
                for dc in range(8):
                    inst = t.matmul(ctx.pp[k % 4][:, :],
                                    oT[(tb % 2) * 8 + dc][:, (i % 4) * 128:(i % 4) * 128 + 128],
                                    wslice(wo, 1024, dc, oc * 512, oc * 512 + 512),
                                    start=(dc == 0), stop=(dc == 7))
                s_mm.inc(inst)

        @ctx.sec("vector")
        def _(v, i=i, buf=buf):
            s_mm.wait(v, 2 * (i + 1))
            s_x.wait(v, 16 * (i + 1))
            if i == 0:
                s_w.wait(v, 32)
            if i >= 2:
                s_e2.wait(v, i - 1)
                s_st.wait(v, 32 * (i - 1))
                s_g2.wait(v, i - 1)
            for oc in range(2):
                sl = slice(oc * 512, oc * 512 + 512)
                v.tensor_add(s1f[buf][:, sl], ctx.pp[(2 * i + oc) % 4][:, :], obbc[:, sl])
                v.tensor_add(s1f[buf][:, sl], s1f[buf][:, sl], xnt[buf][:, sl])
            v.tensor_copy(s1b[buf][:], s1f[buf][:])
            s_ev.inc(v.reduce_sum(gcat[buf][:, 0:16], view3(s1f[buf], 16, 64), axis=AX.X))

        @ctx.sec("scalar")
        def _(sc, i=i, buf=buf):
            s_ev.wait(sc, i + 1)
            s_e2.inc(sc.activation(sq[buf][:], s1f[buf][:], AF.Square))

        @ctx.sec("vector")
        def _(v, i=i, buf=buf):
            s_e2.wait(v, i + 1)
            s_f2.inc(v.reduce_sum(gcat[buf][:, 16:32], view3(sq[buf], 16, 64), axis=AX.X))

        @ctx.sec("tensor")
        def _(t, i=i, buf=buf, first=first, last=last):
            s_f2.wait(t, i + 1)
            if i == 16:
                sV2.wait(t, 4)
            t.matmul(pp6[0:16, 0:1], gcat[buf][:, 0:16], ctx.consts["ones"][:, 0:1],
                     start=first, stop=last)
            s_g2.inc(t.matmul(pp7[0:16, 0:1], gcat[buf][:, 16:32], ctx.consts["ones"][:, 0:1],
                              start=first, stop=last))

        @ctx.sec("sync")
        def _(sy, i=i, buf=buf):
            s_ev.wait(sy, i + 1)
            s_st.inc(sy.dma_start(out=S["s1_f32"][i * 128:(i + 1) * 128, :], in_=s1f[buf][:]), 16)
            s_st.inc(sy.dma_start(out=S["s1_bf"][i * 128:(i + 1) * 128, :], in_=s1b[buf][:]), 16)

        if last:
            @ctx.sec("vector")
            def _(v, b=b):
                if b == 0:
                    v.wait_ge(ctx.cld.h, ctx.cld.n)
                s_g2.wait(v, 16 * (b + 1))
                if b == 1:
                    sPM.wait(v, 64)
                v.tensor_scalar_mul(stat[0:16, 0:1], pp6[0:16, 0:1], 1.0 / (64 * 2048))
                inst = v.tensor_scalar_mul(stat[0:16, 1:2], pp7[0:16, 0:1], 1.0 / (64 * 2048))
                ctx.fence(v, inst)
                inst = v.tensor_mul(stat[0:16, 2:3], stat[0:16, 0:1], stat[0:16, 0:1])
                ctx.fence(v, inst)
                inst = v.scalar_tensor_tensor(stat[0:16, 2:3], stat[0:16, 2:3], -1.0,
                                              stat[0:16, 1:2], ALU.mult, ALU.add)
                ctx.fence(v, inst)
                sV2.inc(v.tensor_scalar_add(stat[0:16, 2:3], stat[0:16, 2:3], EPS), 4)

            @ctx.sec("scalar")
            def _(sc, b=b):
                sV2.wait(sc, 4 * (b + 1))
                inst = sc.activation(stat[0:16, 3:4], stat[0:16, 2:3], AF.Ln)
                ctx.fence(sc, inst)
                sC2.inc(sc.activation(stat[0:16, 3:4], stat[0:16, 3:4], AF.Exp, scale=-0.5))

            @ctx.sec("sync")
            def _(sy, b=b):
                sC2.wait(sy, b + 1)
                sPM.inc(sy.dma_start(
                    out=S["ph3_stat"][b, 0].rearrange("(h c) -> c h", h=2, c=8),
                    in_=stat[0:16, 3:4], allow_slow_non_contiguous=True), 16)
                sPM.inc(sy.dma_start(
                    out=S["ph3_stat"][b, 1].rearrange("(h c) -> c h", h=2, c=8),
                    in_=stat[0:16, 0:1], allow_slow_non_contiguous=True), 16)
                sPM.wait(sy, 64 * b + 32)
                if b == 1:
                    sV3.wait(sy, 3)
                src_r = S["ph3_stat"][b, 0].rearrange("(h c) -> h c", h=2) \
                    .rearrange("h (c u) -> h u c", u=1).broadcast_to([2, 64, 8])
                src_m = S["ph3_stat"][b, 1].rearrange("(h c) -> h c", h=2) \
                    .rearrange("h (c u) -> h u c", u=1).broadcast_to([2, 64, 8])
                sPM.inc(sy.dma_start(out=rfm[:], in_=src_r), 16)
                sPM.inc(sy.dma_start(out=mfm[:], in_=src_m), 16)

            @ctx.sec("vector")
            def _(v, b=b):
                sPM.wait(v, 64 * (b + 1))
                if b == 1:
                    sAB.wait(v, 32)
                inst = v.tensor_mul(afm[:], rfm[:], ctx.consts["gn2_g"][:])
                ctx.fence(v, inst)
                inst = v.tensor_mul(bfm[:], mfm[:], afm[:])
                ctx.fence(v, inst)
                sV3.inc(v.scalar_tensor_tensor(bfm[:], bfm[:], -1.0, ctx.consts["gn2_b"][:],
                                               ALU.mult, ALU.add), 3)

            @ctx.sec("sync")
            def _(sy, b=b):
                sV3.wait(sy, 3 * (b + 1))
                sAB.inc(sy.dma_start(out=S["gn2ab"][b, 0], in_=afm[:]), 16)
                sAB.inc(sy.dma_start(out=S["gn2ab"][b, 1], in_=bfm[:]), 16)




def ph4_ffn0(ctx):
    nc, ar, S = ctx.nc, ctx.ar, ctx.S
    ar.reset()

    w1 = ar.alloc([128, 16384], BF16)
    w2 = ar.alloc([128, 16384], BF16)
    b1c = ar.alloc([128, 16], F32)
    b2bc = ar.alloc([128, 1024], F32)
    ab = ar.alloc([128, 32], F32)
    ga = [ar.alloc([128, 512], BF16) for _ in range(16)]
    hT = [ar.alloc([128, 8192], BF16) for _ in range(2)]
    s1t = [ar.alloc([128, 1024], F32) for _ in range(2)]
    ob = [ar.alloc([128, 512], F32) for _ in range(4)]
    obb = [ar.alloc([128, 1024], F32) for _ in range(2)]

    s_w = ctx.sem("p4_w")
    s_a = ctx.sem("p4_a")
    s_m1 = ctx.sem("p4_m1")
    s_h = ctx.sem("p4_h")
    s_m2 = ctx.sem("p4_m2")
    s_x = ctx.sem("p4_x")
    s_ev = ctx.sem("p4_ev")
    s_st = ctx.sem("p4_st")

    @ctx.sec("gpsimd")
    def _(g):
        load_wT(ctx, g, S["w1T"], 1024, 2048, w1, s_w)
        load_wT(ctx, g, S["w2T"], 2048, 1024, w2, s_w)

    @ctx.sec("sync")
    def _(sy):
        s_w.inc(sy.dma_start(out=b1c[:], in_=S["b1c"]), 16)
        s_w.inc(sy.dma_start(out=b2bc[:], in_=S["b2_bc"]), 16)
        for b in range(2):
            s_w.inc(sy.dma_start(out=ab[:, 16 * b:16 * b + 8], in_=S["gn2ab"][b, 0]), 16)
            s_w.inc(sy.dma_start(out=ab[:, 16 * b + 8:16 * b + 16], in_=S["gn2ab"][b, 1]), 16)

    for tb in range(8):
        bt = tb // 4
        gbuf = [ga[(tb % 2) * 8 + dc] for dc in range(8)]
        hbuf = hT[tb % 2]

        @ctx.sec("sync")
        def _(sy, tb=tb, gbuf=gbuf):
            if tb >= 2:
                s_m1.wait(sy, (tb - 1) * 16)
            for dc in range(8):
                tload(ctx, sy, S["s1_bf"], tb * 512, 512, dc * 128, 128, gbuf[dc], s_a)

        @ctx.sec("vector")
        def _(v, tb=tb, bt=bt, gbuf=gbuf):
            s_a.wait(v, (16 * 8 + 1) * tb + 16 * 8)
            if tb == 0:
                s_w.wait(v, 128)
            for dc in range(8):
                v.tensor_scalar(gbuf[dc][:], gbuf[dc][:],
                                ab[:, 16 * bt + dc:16 * bt + dc + 1],
                                ab[:, 16 * bt + 8 + dc:16 * bt + 9 + dc],
                                ALU.mult, ALU.add)
            s_a.inc(v.engine_nop())

        for oc in range(16):
            k = tb * 16 + oc
            pb = ctx.pp[oc % 4]

            @ctx.sec("tensor")
            def _(t, tb=tb, oc=oc, k=k, pb=pb, gbuf=gbuf):
                if k == 0:
                    s_w.wait(t, 32)
                s_a.wait(t, (16 * 8 + 1) * (tb + 1))
                if k >= 4:
                    s_h.wait(t, k - 3)
                for dc in range(8):
                    inst = t.matmul(pb[:, :],
                                    wslice(w1, 2048, dc, oc * 128, oc * 128 + 128),
                                    gbuf[dc][:], start=(dc == 0), stop=(dc == 7))
                s_m1.inc(inst)

            @ctx.sec("scalar")
            def _(sc, tb=tb, oc=oc, k=k, pb=pb, hbuf=hbuf):
                s_m1.wait(sc, k + 1)
                if tb >= 2 and oc == 0:
                    s_m2.wait(sc, 8 * (tb - 1))
                s_h.inc(sc.activation(hbuf[:, oc * 512:(oc + 1) * 512], pb[:, :],
                                      AF.Silu, bias=b1c[:, oc:oc + 1]))

        for tc in range(4):
            i = tb * 4 + tc
            sbuf = s1t[tc % 2]

            @ctx.sec("sync")
            def _(sy, i=i, tc=tc, sbuf=sbuf):
                if i >= 2:
                    s_ev.wait(sy, i - 1)
                s_x.inc(sy.dma_start(out=sbuf[:],
                                     in_=S["s1_f32"][i * 128:(i + 1) * 128, :]), 16)

            for oc2 in range(2):
                kk = i * 2 + oc2
                pb2 = ctx.pp[4 + kk % 4]

                @ctx.sec("tensor")
                def _(t, tb=tb, tc=tc, oc2=oc2, kk=kk, pb2=pb2, hbuf=hbuf):
                    s_h.wait(t, 16 * (tb + 1))
                    if kk >= 4:
                        s_ev.wait(t, (kk - 4) // 2 + 1)
                    for hc in range(16):
                        inst = t.matmul(pb2[:, :],
                                        hbuf[:, hc * 512 + tc * 128:hc * 512 + tc * 128 + 128],
                                        wslice(w2, 1024, hc, oc2 * 512, oc2 * 512 + 512),
                                        start=(hc == 0), stop=(hc == 15))
                    s_m2.inc(inst)

            @ctx.sec("vector")
            def _(v, i=i, tc=tc, sbuf=sbuf):
                s_m2.wait(v, 2 * (i + 1))
                s_x.wait(v, 16 * (i + 1))
                if i >= 2:
                    s_st.wait(v, 16 * (i - 1))
                o = obb[i % 2]
                for oc2 in range(2):
                    kk = i * 2 + oc2
                    sl = slice(oc2 * 512, oc2 * 512 + 512)
                    v.tensor_add(o[:, sl], ctx.pp[4 + kk % 4][:, :], b2bc[:, sl])
                    v.tensor_add(o[:, sl], o[:, sl], sbuf[:, sl])
                s_ev.inc(v.engine_nop())

            @ctx.sec("sync")
            def _(sy, i=i):
                s_ev.wait(sy, i + 1)
                s_st.inc(sy.dma_start(out=S["sbr_f32"][i * 128:(i + 1) * 128, :],
                                      in_=obb[i % 2][:]), 16)




def ph5a_qkv(ctx):
    nc, ar, S = ctx.nc, ctx.ar, ctx.S
    ar.reset()

    wq = ar.alloc([128, 4096], BF16)
    wk = ar.alloc([128, 4096], BF16)
    wv = ar.alloc([128, 4096], BF16)
    qb = ar.alloc([128, 4], F32)
    kb = ar.alloc([128, 4], F32)
    vbb = ar.alloc([128, 512], F32)
    ga = [ar.alloc([128, 512], BF16) for _ in range(16)]
    ob = [ar.alloc([128, 512], BF16) for _ in range(4)]

    s_w = ctx.sem("p5a_w")
    s_a = ctx.sem("p5a_a")
    s_mm = ctx.sem("p5a_mm")
    s_ev = ctx.sem("p5a_ev")
    s_st = ctx.sem("p5a_st")

    @ctx.sec("gpsimd")
    def _(g):
        load_wT(ctx, g, S["f_qwsl"], 1024, 512, wq, s_w)
        load_wT(ctx, g, S["f_kw"], 1024, 512, wk, s_w)
        load_wT(ctx, g, S["f_vw"], 1024, 512, wv, s_w)

    @ctx.sec("sync")
    def _(sy):
        s_w.inc(sy.dma_start(out=qb[:], in_=S["f_qbc"]), 16)
        s_w.inc(sy.dma_start(out=kb[:], in_=S["f_kbc"]), 16)
        s_w.inc(sy.dma_start(out=vbb[:], in_=S["f_vb_bc"]), 16)

    for tb in range(8):
        gbuf = [ga[(tb % 2) * 8 + dc] for dc in range(8)]

        @ctx.sec("sync")
        def _(sy, tb=tb, gbuf=gbuf):
            if tb >= 2:
                s_mm.wait(sy, (tb - 1) * 12)
            for dc in range(8):
                tload(ctx, sy, S["xn0_bf"], tb * 512, 512, dc * 128, 128, gbuf[dc], s_a)

        for ti, (wsb, scr, bcol) in enumerate([(wq, "fQT", qb), (wk, "fKT", kb)]):
            for oc in range(4):
                k = tb * 12 + ti * 4 + oc
                pb = ctx.pp[k % 2]
                obuf = ob[k % 4]

                @ctx.sec("tensor")
                def _(t, tb=tb, k=k, oc=oc, pb=pb, wsb=wsb):
                    if k == 0:
                        s_w.wait(t, 96)
                    s_a.wait(t, 16 * 8 * (tb + 1))
                    if k >= 2:
                        s_ev.wait(t, k - 1)
                    for dc in range(8):
                        inst = t.matmul(pb[:, :],
                                        wslice(wsb, 512, dc, oc * 128, oc * 128 + 128),
                                        gbuf[dc][:], start=(dc == 0), stop=(dc == 7))
                    s_mm.inc(inst)

                @ctx.sec("vector")
                def _(v, k=k, oc=oc, pb=pb, obuf=obuf, bcol=bcol):
                    s_mm.wait(v, k + 1)
                    if k >= 4:
                        s_st.wait(v, 16 * (k - 3))
                    s_ev.inc(v.tensor_scalar_add(obuf[:], pb[:, :], bcol[:, oc:oc + 1]))

                @ctx.sec("gpsimd")
                def _(g, tb=tb, k=k, oc=oc, obuf=obuf, scr=scr):
                    s_ev.wait(g, k + 1)
                    s_st.inc(g.dma_start(
                        out=S[scr][oc * 128:(oc + 1) * 128, tb * 512:(tb + 1) * 512],
                        in_=obuf[:]), 16)

        for tc in range(4):
            gt = tb * 4 + tc
            k = tb * 12 + 8 + tc
            pb = ctx.pp[k % 2]
            obuf = ob[k % 4]

            @ctx.sec("tensor")
            def _(t, tb=tb, tc=tc, k=k, pb=pb):
                s_a.wait(t, 16 * 8 * (tb + 1))
                if k >= 2:
                    s_ev.wait(t, k - 1)
                for dc in range(8):
                    inst = t.matmul(pb[:, :],
                                    gbuf[dc][:, tc * 128:tc * 128 + 128],
                                    wslice(wv, 512, dc, 0, 512),
                                    start=(dc == 0), stop=(dc == 7))
                s_mm.inc(inst)

            @ctx.sec("vector")
            def _(v, k=k, pb=pb, obuf=obuf):
                s_mm.wait(v, k + 1)
                if k >= 4:
                    s_st.wait(v, 16 * (k - 3))
                s_ev.inc(v.tensor_add(obuf[:], pb[:, :], vbb[:]))

            @ctx.sec("gpsimd")
            def _(g, gt=gt, k=k, obuf=obuf):
                s_ev.wait(g, k + 1)
                s_st.inc(g.dma_start(out=S["fVtm"][gt * 128:(gt + 1) * 128, :],
                                      in_=obuf[:]), 16)



def ph5b_attn(ctx):
    nc, ar, S = ctx.nc, ctx.ar, ctx.S
    ar.reset()
    import math as _m
    scl = 1.0 / _m.sqrt(32.0)

    QT = ar.alloc([128, 8192], BF16)
    KT = ar.alloc([128, 8192], BF16)
    VA = ar.alloc([128, 8448], BF16)
    vtmp = [ar.alloc([128, 512], BF16) for _ in range(2)]
    ex = [ar.alloc([128, 8192], BF16) for _ in range(2)]
    rec = ar.alloc([128, 512], F32)
    aos = ar.alloc([128, 512], F32)
    aob = [ar.alloc([128, 512], BF16) for _ in range(2)]
    onesr = ar.alloc([128, 128], F32)

    s_qk = ctx.sem("p5b_qk")
    s_va = ctx.sem("p5b_va")
    s_vc = ctx.sem("p5b_vc")
    s_sm = ctx.sem("p5b_sm")
    s_ex = ctx.sem("p5b_ex")
    s_av = ctx.sem("p5b_av")
    s_rc = ctx.sem("p5b_rc")
    s_mu = ctx.sem("p5b_mu")
    s_st = ctx.sem("p5b_st")

    @ctx.sec("vector")
    def _(v):
        v.memset(VA[:], 1.0)
        v.memset(onesr[:], 1.0)
        s_va.inc(v.engine_nop())

    for b in range(2):
        @ctx.sec("sync")
        def _(sy, b=b):
            if b == 1:
                s_sm.wait(sy, 1024)
                s_av.wait(sy, 64)
            src_q = S["fQT"].rearrange("(c p) t -> p c t", p=128)[:, :, b * 2048:(b + 1) * 2048]
            src_k = S["fKT"].rearrange("(c p) t -> p c t", p=128)[:, :, b * 2048:(b + 1) * 2048]
            s_qk.inc(sy.dma_start(out=QT.rearrange("p (c t) -> p c t", c=4), in_=src_q), 16)
            s_qk.inc(sy.dma_start(out=KT.rearrange("p (c t) -> p c t", c=4), in_=src_k), 16)

        for tc in range(16):
            @ctx.sec("sync")
            def _(sy, b=b, tc=tc):
                ci = b * 16 + tc
                if ci >= 2:
                    s_va.wait(sy, ci)
                s_vc.inc(sy.dma_start(out=vtmp[tc % 2][:],
                                      in_=S["fVtm"][ci * 128:(ci + 1) * 128, :]), 16)

            @ctx.sec("gpsimd")
            def _(g, b=b, tc=tc):
                ci = b * 16 + tc
                s_vc.wait(g, 16 * (ci + 1))
                if b == 1 and tc == 0:
                    s_av.wait(g, 64)
                for h in range(16):
                    inst = g.tensor_copy(VA[:, tc * 528 + h * 33:tc * 528 + h * 33 + 32],
                                         vtmp[tc % 2][:, h * 32:(h + 1) * 32])
                s_va.inc(inst)

        for h in range(16):
            p0 = 32 * (h % 4)
            c = h // 4
            for t1b in range(4):
                hh = b * 64 + h * 4 + t1b
                ebuf = ex[hh % 2]

                @ctx.sec("tensor")
                def _(t, b=b, h=h, t1b=t1b, hh=hh, p0=p0, c=c):
                    s_qk.wait(t, 32 * (b + 1))
                    if hh >= 1:
                        s_ex.wait(t, 16 * hh)
                    for t2c in range(16):
                        if t2c >= 4:
                            s_ex.wait(t, 16 * hh + t2c - 3)
                        s_sm.inc(t.matmul(
                            ctx.pp[t2c % 4][:, :],
                            KT[p0:p0 + 32, c * 2048 + t2c * 128:c * 2048 + (t2c + 1) * 128],
                            QT[p0:p0 + 32, c * 2048 + t1b * 512:c * 2048 + (t1b + 1) * 512],
                            start=True, stop=True,
                            tile_position=(p0, 0)))

                @ctx.sec("scalar")
                def _(sc, hh=hh, ebuf=ebuf):
                    if hh >= 2:
                        s_av.wait(sc, hh - 1)
                    for t2c in range(16):
                        s_sm.wait(sc, 16 * hh + t2c + 1)
                        s_ex.inc(sc.activation(ebuf[:, t2c * 512:(t2c + 1) * 512],
                                               ctx.pp[t2c % 4][:, :], AF.Exp, scale=scl))

                @ctx.sec("tensor")
                def _(t, b=b, h=h, hh=hh, ebuf=ebuf):
                    s_va.wait(t, 1 + 16 * (b + 1))
                    if hh >= 2:
                        s_mu.wait(t, hh - 1)
                    for t2c in range(16):
                        s_ex.wait(t, 16 * hh + t2c + 1)
                        inst = t.matmul(
                            ctx.pp[4 + hh % 2][0:33, :],
                            VA[:, t2c * 528 + h * 33:t2c * 528 + (h + 1) * 33],
                            ebuf[:, t2c * 512:(t2c + 1) * 512],
                            start=(t2c == 0), stop=(t2c == 15))
                    s_av.inc(inst)

                @ctx.sec("vector")
                def _(v, hh=hh):
                    s_av.wait(v, hh + 1)
                    if hh >= 1:
                        s_rc.wait(v, 2 * hh)
                    inst = v.reciprocal(rec[0:1, :], ctx.pp[4 + hh % 2][32:33, :])
                    ctx.fence(v, inst)
                    s_rc.inc(v.engine_nop())

                @ctx.sec("tensor")
                def _(t, hh=hh):
                    s_rc.wait(t, 2 * hh + 1)
                    if hh >= 1:
                        s_mu.wait(t, hh)
                    s_rc.inc(t.matmul(ctx.pp[6][0:32, :], onesr[0:1, 0:32], rec[0:1, :],
                                      start=True, stop=True))

                @ctx.sec("vector")
                def _(v, hh=hh):
                    s_rc.wait(v, 2 * (hh + 1))
                    if hh >= 2:
                        s_st.wait(v, 16 * (hh - 1))
                    v.tensor_copy(aos[0:32, :], ctx.pp[6][0:32, :])
                    s_mu.inc(v.tensor_mul(aob[hh % 2][0:32, :],
                                          ctx.pp[4 + hh % 2][0:32, :], aos[0:32, :]))

                @ctx.sec("sync")
                def _(sy, b=b, h=h, t1b=t1b, hh=hh):
                    s_mu.wait(sy, hh + 1)
                    s_st.inc(sy.dma_start(
                        out=S["aoTs"][h * 32:(h + 1) * 32,
                                      b * 2048 + t1b * 512:b * 2048 + (t1b + 1) * 512],
                        in_=aob[hh % 2][0:32, :]), 16)





def ph5c_mla(ctx):
    nc, ar, S = ctx.nc, ctx.ar, ctx.S
    ar.reset()

    ao = ar.alloc([128, 16384], BF16)
    ow = ar.alloc([128, 4096], BF16)
    obc = ar.alloc([128, 1024], F32)
    mt = [ar.alloc([128, 1024], BF16) for _ in range(2)]

    s_w = ctx.sem("p5c_w")
    s_mm = ctx.sem("p5c_mm")
    s_ev = ctx.sem("p5c_ev")
    s_st = ctx.sem("p5c_st")

    @ctx.sec("gpsimd")
    def _(g):
        load_wT(ctx, g, S["f_ow"], 512, 1024, ow, s_w)

    @ctx.sec("sync")
    def _(sy):
        s_w.inc(sy.dma_start(out=ao.rearrange("p (c t) -> p c t", c=4),
                             in_=S["aoTs"].rearrange("(c p) t -> p c t", p=128)), 16)
        s_w.inc(sy.dma_start(out=obc[:], in_=S["f_ob_bc"]), 16)

    for i in range(NTILE):
        buf = i % 2
        for oc in range(2):
            k = 2 * i + oc

            @ctx.sec("tensor")
            def _(t, i=i, oc=oc, k=k):
                if k == 0:
                    s_w.wait(t, 48)
                if k >= 4:
                    s_ev.wait(t, i - 1)
                for fc in range(4):
                    inst = t.matmul(ctx.pp[k % 4][:, :],
                                    ao[:, fc * 4096 + i * 128:fc * 4096 + i * 128 + 128],
                                    wslice(ow, 1024, fc, oc * 512, oc * 512 + 512),
                                    start=(fc == 0), stop=(fc == 3))
                s_mm.inc(inst)

        @ctx.sec("vector")
        def _(v, i=i, buf=buf):
            s_mm.wait(v, 2 * (i + 1))
            if i >= 2:
                s_st.wait(v, 16 * (i - 1))
            for oc in range(2):
                sl = slice(oc * 512, oc * 512 + 512)
                v.tensor_add(mt[buf][:, sl], ctx.pp[(2 * i + oc) % 4][:, :], obc[:, sl])
            s_ev.inc(v.engine_nop())

        @ctx.sec("sync")
        def _(sy, i=i, buf=buf):
            s_ev.wait(sy, i + 1)
            s_st.inc(sy.dma_start(out=S["mla_tm"][i * 128:(i + 1) * 128, :],
                                  in_=mt[buf][:]), 16)



def ph5d_router(ctx):
    nc, ar, S = ctx.nc, ctx.ar, ctx.S
    ar.reset()

    rw = ar.alloc([128, 64], BF16)
    rbbc = ar.alloc([128, 8], F32)
    ga = [ar.alloc([128, 512], BF16) for _ in range(16)]
    lg = ar.alloc([128, 256], F32)
    ee = ar.alloc([128, 256], F32)
    e2 = ar.alloc([128, 256], F32)
    m1 = ar.alloc([128, 32], F32)
    m2 = ar.alloc([128, 32], F32)
    den = ar.alloc([128, 32], F32)
    cw = ar.alloc([128, 256], F32)

    s_w = ctx.sem("p5d_w")
    s_a = ctx.sem("p5d_a")
    s_mm = ctx.sem("p5d_mm")
    s_sx = ctx.sem("p5d_sx")
    s_st = ctx.sem("p5d_st")

    @ctx.sec("gpsimd")
    def _(g):
        load_wT(ctx, g, S["f_rw"], 1024, 8, rw, s_w)

    @ctx.sec("sync")
    def _(sy):
        s_w.inc(sy.dma_start(out=rbbc[:], in_=S["f_rb_bc"]), 16)

    for tb in range(8):
        gbuf = [ga[(tb % 2) * 8 + dc] for dc in range(8)]

        @ctx.sec("sync")
        def _(sy, tb=tb, gbuf=gbuf):
            if tb >= 2:
                s_mm.wait(sy, 4 * (tb - 1))
            for dc in range(8):
                tload(ctx, sy, S["mla_tm"], tb * 512, 512, dc * 128, 128, gbuf[dc], s_a)

        for tc in range(4):
            i = tb * 4 + tc

            @ctx.sec("tensor")
            def _(t, tb=tb, tc=tc, i=i, gbuf=gbuf):
                if i == 0:
                    s_w.wait(t, 32)
                s_a.wait(t, 16 * 8 * (tb + 1))
                for dc in range(8):
                    inst = t.matmul(ctx.pp[4][:, i * 8:(i + 1) * 8],
                                    gbuf[dc][:, tc * 128:tc * 128 + 128],
                                    wslice(rw, 8, dc, 0, 8),
                                    start=(i == 0 and dc == 0), stop=(dc == 7),
                                    skip_group_check=True)
                s_mm.inc(inst)

    v3 = lambda ap: ap.rearrange("p (g c) -> p g c", c=8)
    bc = lambda ap: ap.rearrange("p (g u) -> p g u", u=1).broadcast_to([128, 32, 8])

    @ctx.sec("vector")
    def _(v):
        s_mm.wait(v, 32)
        inst = v.tensor_tensor(v3(lg[:]), v3(ctx.pp[4][:, 0:256]),
                        rbbc[:].rearrange("p (u c) -> p u c", u=1).broadcast_to([128, 32, 8]),
                        ALU.add)
        ctx.fence(v, inst)
        inst = v.tensor_reduce(m1[:], v3(lg[:]), axis=AX.X, op=ALU.max, negate=True)
        ctx.fence(v, inst)
        s_sx.inc(v.tensor_tensor(v3(ee[:]), v3(lg[:]), bc(m1[:]), ALU.add))

    @ctx.sec("scalar")
    def _(sc):
        s_sx.wait(sc, 1)
        s_sx.inc(sc.activation(ee[:], ee[:], AF.Exp))

    @ctx.sec("vector")
    def _(v):
        s_sx.wait(v, 2)
        inst = v.reduce_sum(den[:], v3(ee[:]), axis=AX.X)
        ctx.fence(v, inst)
        inst = v.tensor_scalar_mul(den[:], den[:], 1e-8)
        ctx.fence(v, inst)
        inst = v.reduce_max(m1[:], v3(ee[:]), axis=AX.X)
        ctx.fence(v, inst)
        v.tensor_tensor(v3(cw[:]), v3(ee[:]), bc(m1[:]), ALU.is_ge)
        inst = v.scalar_tensor_tensor(e2[:], cw[:], -1e6, ee[:], ALU.mult, ALU.add)
        ctx.fence(v, inst)
        inst = v.reduce_max(m2[:], v3(e2[:]), axis=AX.X)
        ctx.fence(v, inst)
        v.tensor_tensor(v3(e2[:]), v3(e2[:]), bc(m2[:]), ALU.is_ge)
        inst = v.tensor_add(den[:], den[:], m1[:])
        ctx.fence(v, inst)
        inst = v.tensor_add(den[:], den[:], m2[:])
        ctx.fence(v, inst)
        inst = v.reciprocal(den[:], den[:])
        ctx.fence(v, inst)
        v.tensor_add(cw[:], cw[:], e2[:])
        v.tensor_mul(cw[:], cw[:], ee[:])
        s_sx.inc(v.tensor_tensor(v3(cw[:]), v3(cw[:]), bc(den[:]), ALU.mult))

    @ctx.sec("sync")
    def _(sy):
        s_sx.wait(sy, 3)
        s_st.inc(sy.dma_start(out=S["cmb0"], in_=cw[:]), 16)



def ph5e_moe0(ctx):
    nc, ar, S = ctx.nc, ctx.ar, ctx.S
    ar.reset()

    w1sb = [ar.alloc([128, 16384], BF16) for _ in range(2)]
    w2sb = [ar.alloc([128, 16384], BF16)]
    b1c = [ar.alloc([128, 16], F32) for _ in range(2)]
    b2r = [ar.alloc([128, 1024], BF16) for _ in range(2)]
    onesb = ar.alloc([128, 128], BF16)
    cwsb = ar.alloc([128, 256], F32)
    ga = [ar.alloc([128, 512], BF16) for _ in range(24)]
    hT = [ar.alloc([128, 8192], BF16) for _ in range(2)]
    tmp = [ar.alloc([128, 512], F32) for _ in range(4)]
    zt = ar.alloc([128, 1024], F32)

    s_w = ctx.sem("p5e_w")
    s_z = ctx.sem("p5e_z")
    s_a = ctx.sem("p5e_a")
    s_a2 = ctx.sem("p5e_a2")
    s_m1 = ctx.sem("p5e_m1")
    s_h = ctx.sem("p5e_h")
    s_m2 = ctx.sem("p5e_m2")
    s_ev = ctx.sem("p5e_ev")
    s_ac = ctx.sem("p5e_ac")

    @ctx.sec("vector")
    def _(v):
        v.memset(zt[:], 0.0)
        v.memset(onesb[:], 1.0)
        s_z.inc(v.engine_nop())

    @ctx.sec("sync")
    def _(sy):
        s_z.wait(sy, 1)
        for i in range(NTILE):
            s_z.inc(sy.dma_start(out=S["moe0"][i * 128:(i + 1) * 128, :], in_=zt[:]), 16)
        s_z.inc(sy.dma_start(out=cwsb[:], in_=S["cmb0"]), 16)

    for e in range(8):
        eb = e % 2

        @ctx.sec("gpsimd")
        def _(g, e=e, eb=eb):
            if e >= 2:
                s_m1.wait(g, 128 * (e - 1))
            if e >= 1:
                s_m2.wait(g, 8 * 8 * e)
            load_wT(ctx, g, S["f_w1"][e], 1024, 2048, w1sb[eb], s_w)
            load_wT(ctx, g, S["f_w2"][e], 2048, 1024, w2sb[0], s_w)
            s_w.inc(g.dma_start(out=b2r[eb][0:1, :],
                                in_=S["f_b2"][e].rearrange("(u f) -> u f", u=1)), 16)

        @ctx.sec("sync")
        def _(sy, e=e, eb=eb):
            if e >= 2:
                s_h.wait(sy, 128 * (e - 1))
            s_w.inc(sy.dma_start(out=b1c[eb][:], in_=S["f_b1c"][e]), 16)

        for tb in range(8):
            gbuf = [ga[(tb % 2) * 8 + dc] for dc in range(8)]
            hbuf = hT[0]
            et = e * 8 + tb

            @ctx.sec("sync")
            def _(sy, e=e, tb=tb, et=et, gbuf=gbuf):
                if et >= 2:
                    s_m1.wait(sy, (et - 1) * 16)
                for dc in range(8):
                    tload(ctx, sy, S["mla_tm"], tb * 512, 512, dc * 128, 128, gbuf[dc], s_a)

            for oc in range(16):
                k = et * 16 + oc
                pb = ctx.pp[oc % 4]

                @ctx.sec("tensor")
                def _(t, e=e, eb=eb, tb=tb, et=et, oc=oc, k=k, pb=pb, gbuf=gbuf):
                    s_w.wait(t, 64 * (e + 1))
                    s_a.wait(t, 16 * 8 * (et + 1))
                    if k >= 4:
                        s_h.wait(t, k - 3)
                    for dc in range(8):
                        inst = t.matmul(pb[:, :],
                                        wslice(w1sb[eb], 2048, dc, oc * 128, oc * 128 + 128),
                                        gbuf[dc][:], start=(dc == 0), stop=(dc == 7))
                    s_m1.inc(inst)

                @ctx.sec("scalar")
                def _(sc, eb=eb, tb=tb, et=et, oc=oc, k=k, pb=pb, hbuf=hbuf):
                    s_m1.wait(sc, k + 1)
                    if et >= 2 and oc == 0:
                        s_m2.wait(sc, 8 * (et - 1))
                    s_h.inc(sc.activation(hbuf[:, oc * 512:(oc + 1) * 512], pb[:, :],
                                          AF.Gelu, bias=b1c[eb][:, oc:oc + 1]))

            for tc in range(4):
                i = tb * 4 + tc
                for oc2 in range(2):
                    kk = et * 8 + tc * 2 + oc2
                    pb2 = ctx.pp[4 + kk % 4]

                    @ctx.sec("tensor")
                    def _(t, eb=eb, tb=tb, tc=tc, oc2=oc2, et=et, kk=kk, pb2=pb2, hbuf=hbuf):
                        s_h.wait(t, 16 * (et + 1))
                        if kk >= 4:
                            s_ev.wait(t, kk - 3)
                        for hc in range(16):
                            t.matmul(pb2[:, :],
                                     hbuf[:, hc * 512 + tc * 128:hc * 512 + tc * 128 + 128],
                                     wslice(w2sb[0], 1024, hc, oc2 * 512, oc2 * 512 + 512),
                                     start=(hc == 0), stop=False, skip_group_check=True)
                        s_m2.inc(t.matmul(pb2[:, :], onesb[0:1, :],
                                          b2r[eb][0:1, oc2 * 512:(oc2 + 1) * 512],
                                          start=False, stop=True, skip_group_check=True))

                    @ctx.sec("vector")
                    def _(v, e=e, i=i, kk=kk, pb2=pb2):
                        s_m2.wait(v, kk + 1)
                        if kk == 0:
                            s_z.wait(v, 1 + 16 * NTILE + 16)
                        if kk >= 4:
                            s_ac.wait(v, 16 * (kk - 3))
                        s_ev.inc(v.tensor_scalar_mul(tmp[kk % 4][:], pb2[:, :],
                                                     cwsb[:, i * 8 + e:i * 8 + e + 1]))

                    @ctx.sec("gpsimd")
                    def _(g, i=i, oc2=oc2, kk=kk):
                        s_ev.wait(g, kk + 1)
                        if kk == 0:
                            s_z.wait(g, 1 + 16 * NTILE)
                        s_ac.inc(g.dma_start(
                            out=S["moe0"][i * 128:(i + 1) * 128, oc2 * 512:(oc2 + 1) * 512],
                            in_=tmp[kk % 4][:], accum_op=ALU.add), 16)



def ph5f_comb0(ctx):
    nc, ar, S = ctx.nc, ctx.ar, ctx.S
    ar.reset()

    xt = [ar.alloc([128, 1024], F32) for _ in range(2)]
    xnt = [ar.alloc([128, 1024], F32) for _ in range(2)]
    sbt = [ar.alloc([128, 1024], F32) for _ in range(2)]
    mot = [ar.alloc([128, 1024], F32) for _ in range(2)]
    gt = [ar.alloc([128, 1024], F32) for _ in range(2)]
    x1t = [ar.alloc([128, 1024], F32) for _ in range(2)]

    s_l = ctx.sem("p5f_l")
    s_lg = ctx.sem("p5f_lg")
    s_v1 = ctx.sem("p5f_v1")
    s_si = ctx.sem("p5f_si")
    s_v2 = ctx.sem("p5f_v2")
    s_st = ctx.sem("p5f_st")

    for i in range(NTILE):
        buf = i % 2
        rows = slice(i * 128, (i + 1) * 128)

        @ctx.sec("sync")
        def _(sy, i=i, buf=buf, rows=rows):
            if i >= 2:
                s_v2.wait(sy, i - 1)
                s_si.wait(sy, i - 1)
            s_l.inc(sy.dma_start(out=xt[buf][:], in_=S["x"][rows, :]), 16)
            s_l.inc(sy.dma_start(out=xnt[buf][:], in_=S["xn0_f32"][rows, :]), 16)

        @ctx.sec("gpsimd")
        def _(g, i=i, buf=buf, rows=rows):
            if i >= 2:
                s_v2.wait(g, i - 1)
                s_si.wait(g, i - 1)
            s_lg.inc(g.dma_start(out=sbt[buf][:], in_=S["sbr_f32"][rows, :]), 16)
            s_lg.inc(g.dma_start(out=mot[buf][:], in_=S["moe0"][rows, :]), 16)

        @ctx.sec("vector")
        def _(v, i=i, buf=buf):
            s_l.wait(v, 32 * (i + 1))
            s_lg.wait(v, 32 * (i + 1))
            if i >= 2:
                s_si.wait(v, i - 1)
            s_v1.inc(v.scalar_tensor_tensor(gt[buf][:], mot[buf][:], 0.3, xnt[buf][:],
                                            ALU.mult, ALU.add))

        @ctx.sec("scalar")
        def _(sc, i=i, buf=buf):
            s_v1.wait(sc, i + 1)
            s_si.inc(sc.activation(gt[buf][:], gt[buf][:], AF.Silu))

        @ctx.sec("vector")
        def _(v, i=i, buf=buf):
            s_si.wait(v, i + 1)
            if i >= 2:
                s_st.wait(v, 16 * (i - 1))
            v.tensor_mul(x1t[buf][:], sbt[buf][:], gt[buf][:])
            s_v2.inc(v.tensor_add(x1t[buf][:], x1t[buf][:], xt[buf][:]))

        @ctx.sec("sync")
        def _(sy, i=i, buf=buf, rows=rows):
            s_v2.wait(sy, i + 1)
            s_st.inc(sy.dma_start(out=S["x1_f32"][rows, :], in_=x1t[buf][:]), 16)



def ph6_ln1(ctx):
    nc, ar, S = ctx.nc, ctx.ar, ctx.S
    ar.reset()

    xt = [ar.alloc([128, 1024], F32) for _ in range(2)]
    dump = ar.alloc([128, 1024], F32)
    xnb = [ar.alloc([128, 1024], BF16) for _ in range(2)]
    stats = ar.alloc([128, 128], F32)
    statq = ar.alloc([128, 128], F32)
    vv = ar.alloc([128, 32], F32)
    rstd = ar.alloc([128, 32], F32)

    s_x = ctx.sem("p6_x")
    sA = ctx.sem("p6_A")
    sSB = ctx.sem("p6_SB")
    s_x2 = ctx.sem("p6_x2")
    sD = ctx.sem("p6_D")
    sST = ctx.sem("p6_ST")

    for i in range(NTILE):
        buf = i % 2

        @ctx.sec("sync")
        def _(sy, i=i, buf=buf):
            if i >= 2:
                sA.wait(sy, i - 1)
            s_x.inc(sy.dma_start(out=xt[buf][:], in_=S["x1_f32"][i * 128:(i + 1) * 128, :]), 16)

        @ctx.sec("scalar")
        def _(sc, i=i, buf=buf):
            s_x.wait(sc, 16 * (i + 1))
            sc.activation(dump[:], xt[buf][:], AF.Identity, scale=1.0 / 1024,
                          accum_out=stats[:, 4 * i:4 * i + 1])
            sA.inc(sc.activation(dump[:], xt[buf][:], AF.Square, scale=1.0 / 32,
                                 accum_out=statq[:, 4 * i:4 * i + 1]))

    @ctx.sec("vector")
    def _(v):
        sA.wait(v, NTILE)
        muv = view3(stats, 32, 4)[:, :, 0:1]
        msqv = view3(statq, 32, 4)[:, :, 0:1]
        inst = v.tensor_mul(vv[:].rearrange("p (g c) -> p g c", c=1), muv, muv)
        ctx.fence(v, inst)
        inst = v.scalar_tensor_tensor(vv[:].rearrange("p (g c) -> p g c", c=1),
                                      vv[:].rearrange("p (g c) -> p g c", c=1),
                                      -1.0, msqv, ALU.mult, ALU.add)
        ctx.fence(v, inst)
        sA.inc(v.tensor_scalar_add(vv[:], vv[:], EPS))

    @ctx.sec("scalar")
    def _(sc):
        sA.wait(sc, NTILE + 1)
        inst = sc.activation(rstd[:], vv[:], AF.Ln)
        ctx.fence(sc, inst)
        sSB.inc(sc.activation(rstd[:], rstd[:], AF.Exp, scale=-0.5))

    for i in range(NTILE):
        buf = i % 2

        @ctx.sec("sync")
        def _(sy, i=i, buf=buf):
            if i < 2:
                sA.wait(sy, NTILE)
            else:
                sD.wait(sy, i - 1)
            s_x2.inc(sy.dma_start(out=xt[buf][:], in_=S["x1_f32"][i * 128:(i + 1) * 128, :]), 16)

        @ctx.sec("vector")
        def _(v, i=i, buf=buf):
            s_x2.wait(v, 16 * (i + 1))
            sSB.wait(v, 1)
            if i >= 2:
                sST.wait(v, 16 * (i - 1))
            sD.inc(v.tensor_scalar(xnb[buf][:], xt[buf][:], stats[:, 4 * i:4 * i + 1],
                                   rstd[:, i:i + 1], ALU.subtract, ALU.mult))

        @ctx.sec("sync")
        def _(sy, i=i, buf=buf):
            sD.wait(sy, i + 1)
            sST.inc(sy.dma_start(out=S["xn1_bf"][i * 128:(i + 1) * 128, :], in_=xnb[buf][:]), 16)




def ph7a_qk1(ctx):
    nc, ar, S = ctx.nc, ctx.ar, ctx.S
    ar.reset()

    wq = ar.alloc([128, 8192], BF16)
    wk = ar.alloc([128, 8192], BF16)
    cosT = ar.alloc([128, 2048], F32)
    sinT = ar.alloc([128, 2048], F32)
    ga = [ar.alloc([128, 512], BF16) for _ in range(16)]
    qst = [ar.alloc([128, 1024], BF16) for _ in range(2)]
    rot = [ar.alloc([128, 1024], BF16) for _ in range(2)]
    tt = ar.alloc([128, 512], F32)

    s_w = ctx.sem("p7_w")
    s_a = ctx.sem("p7_a")
    s_mm = ctx.sem("p7_mm")
    s_ev = ctx.sem("p7_ev")
    s_ro = ctx.sem("p7_ro")
    s_st = ctx.sem("p7_st")

    @ctx.sec("gpsimd")
    def _(g):
        load_wT(ctx, g, S["WqT"], 1024, 1024, wq, s_w)
        load_wT(ctx, g, S["WkT"], 1024, 1024, wk, s_w)

    @ctx.sec("sync")
    def _(sy):
        s_w.inc(sy.dma_start(out=cosT[:], in_=S["cosT"]), 16)
        s_w.inc(sy.dma_start(out=sinT[:], in_=S["sinT"]), 16)

    for tb in range(8):
        toff = (tb % 4) * 512
        gbuf = [ga[(tb % 2) * 8 + dc] for dc in range(8)]

        @ctx.sec("sync")
        def _(sy, tb=tb, gbuf=gbuf):
            if tb >= 2:
                sy.wait_ge(s_mm.h, s_mm.base + (tb - 1) * 16)
            for dc in range(8):
                tload(ctx, sy, S["xn1_bf"], tb * 512, 512, dc * 128, 128, gbuf[dc], s_a)

        for ti, (wsb, scr, qscale) in enumerate([(wq, "qTr", 0.0625), (wk, "kTr", 1.0)]):
            for h in range(4):
                k = tb * 8 + ti * 4 + h
                kb = k % 2

                for half in range(2):
                    oc = 2 * h + half
                    pb = ctx.pp[(2 * k + half) % 4]

                    @ctx.sec("tensor")
                    def _(t, tb=tb, k=k, oc=oc, half=half, pb=pb, wsb=wsb):
                        if k == 0 and half == 0:
                            t.wait_ge(s_w.h, s_w.base + 64)
                        s_a.wait(t, 16 * 8 * (tb + 1))
                        if 2 * k + half >= 4:
                            s_ev.wait(t, 2 * k + half - 3)
                        for dc in range(8):
                            inst = t.matmul(pb[:, :],
                                            wslice(wsb, 1024, dc, oc * 128, oc * 128 + 128),
                                            gbuf[dc][:], start=(dc == 0), stop=(dc == 7))
                        s_mm.inc(inst)

                    @ctx.sec("vector")
                    def _(v, k=k, kb=kb, half=half, pb=pb, qscale=qscale):
                        s_mm.wait(v, 2 * k + half + 1)
                        if k >= 2:
                            s_ro.wait(v, k - 1)
                        s_ev.inc(v.tensor_scalar_mul(qst[kb][:, half * 512:half * 512 + 512],
                                                     pb[:, :], qscale))

                @ctx.sec("vector")
                def _(v, tb=tb, k=k, kb=kb, toff=toff):
                    s_ev.wait(v, 2 * (k + 1))
                    if k >= 2:
                        s_st.wait(v, 32 * (k - 1))
                    lo = qst[kb][:, 0:512]
                    hi = qst[kb][:, 512:1024]
                    cs = cosT[:, toff:toff + 512]
                    sn = sinT[:, toff:toff + 512]
                    rl = rot[kb][:, 0:512]
                    rh = rot[kb][:, 512:1024]
                    v.tensor_mul(rl[:, :], lo, cs)
                    v.tensor_mul(tt[:], hi, sn)
                    v.tensor_sub(rl[:, :], rl[:, :], tt[:])
                    v.tensor_mul(rh[:, :], hi, cs)
                    v.tensor_mul(tt[:], lo, sn)
                    s_ro.inc(v.tensor_add(rh[:, :], rh[:, :], tt[:]))

                @ctx.sec("gpsimd")
                def _(g, tb=tb, h=h, k=k, kb=kb, scr=scr):
                    s_ro.wait(g, k + 1)
                    s_st.inc(g.dma_start(
                        out=S[scr][(2 * h) * 128:(2 * h) * 128 + 128,
                                   tb * 512:(tb + 1) * 512],
                        in_=rot[kb][:, 0:512]), 16)
                    s_st.inc(g.dma_start(
                        out=S[scr][(2 * h + 1) * 128:(2 * h + 1) * 128 + 128,
                                   tb * 512:(tb + 1) * 512],
                        in_=rot[kb][:, 512:1024]), 16)



def ph7b_vg1(ctx):
    nc, ar, S = ctx.nc, ctx.ar, ctx.S
    ar.reset()

    wv = ar.alloc([128, 16384], BF16)
    wg = ar.alloc([128, 16384], BF16)
    ga = [ar.alloc([128, 512], BF16) for _ in range(16)]
    ob = [ar.alloc([128, 512], BF16) for _ in range(4)]

    s_w = ctx.sem("p7b_w")
    s_a = ctx.sem("p7b_a")
    s_mm = ctx.sem("p7b_mm")
    s_ev = ctx.sem("p7b_ev")
    s_st = ctx.sem("p7b_st")

    @ctx.sec("gpsimd")
    def _(g):
        load_wT(ctx, g, S["WvT"], 1024, 2048, wv, s_w)
        load_wT(ctx, g, S["WgT"], 1024, 2048, wg, s_w)

    for tb in range(8):
        gbuf = [ga[(tb % 2) * 8 + dc] for dc in range(8)]

        @ctx.sec("sync")
        def _(sy, tb=tb, gbuf=gbuf):
            if tb >= 2:
                s_mm.wait(sy, (tb - 1) * 32)
            for dc in range(8):
                tload(ctx, sy, S["xn1_bf"], tb * 512, 512, dc * 128, 128, gbuf[dc], s_a)

        for tc in range(4):
            gt = tb * 4 + tc
            for ti, (wsb, scr) in enumerate([(wv, "vtm1"), (wg, "gtm1")]):
                for oc in range(4):
                    k = gt * 8 + ti * 4 + oc
                    pb = ctx.pp[k % 2]
                    obuf = ob[k % 4]

                    @ctx.sec("tensor")
                    def _(t, tb=tb, tc=tc, oc=oc, k=k, pb=pb, wsb=wsb, gbuf=gbuf):
                        if k == 0:
                            t.wait_ge(s_w.h, s_w.base + 32)
                        s_a.wait(t, 16 * 8 * (tb + 1))
                        if k >= 2:
                            s_ev.wait(t, k - 1)
                        for dc in range(8):
                            inst = t.matmul(pb[:, :],
                                            gbuf[dc][:, tc * 128:tc * 128 + 128],
                                            wslice(wsb, 2048, dc, oc * 512, oc * 512 + 512),
                                            start=(dc == 0), stop=(dc == 7))
                        s_mm.inc(inst)

                    @ctx.sec("vector")
                    def _(v, k=k, pb=pb, obuf=obuf):
                        s_mm.wait(v, k + 1)
                        if k >= 4:
                            s_st.wait(v, 16 * (k - 3))
                        s_ev.inc(v.tensor_copy(obuf[:], pb[:, :]))

                    @ctx.sec("gpsimd")
                    def _(g, gt=gt, oc=oc, k=k, obuf=obuf, scr=scr):
                        s_ev.wait(g, k + 1)
                        s_st.inc(g.dma_start(
                            out=S[scr][gt * 128:(gt + 1) * 128, oc * 512:(oc + 1) * 512],
                            in_=obuf[:]), 16)



def ph8_ret1(ctx):
    nc, ar, S = ctx.nc, ctx.ar, ctx.S
    ar.reset()
    import math as _m
    gam = [1.0 - 2.0 ** (-5.0 - h) for h in range(4)]
    gC = [g ** 128 for g in gam]

    qT_sb = ar.alloc([128, 16384], BF16)
    kT_sb = ar.alloc([128, 16384], BF16)
    mask_sb = ar.alloc([128, 512], F32)
    cross_sb = ar.alloc([128, 512], F32)
    kdec_sb = ar.alloc([128, 1024], F32)
    gnw = ar.alloc([128, 512], F32)
    ktb = [ar.alloc([128, 1024], BF16) for _ in range(2)]
    vtb = [ar.alloc([128, 2048], BF16) for _ in range(2)]
    gtb = [ar.alloc([128, 2048], BF16) for _ in range(2)]
    gsi = ar.alloc([128, 2048], F32)
    ksc = [ar.alloc([128, 1024], BF16) for _ in range(2)]
    qc_bf = ar.alloc([128, 1024], BF16)
    simb = ar.alloc([128, 512], BF16)
    S_sb = ar.alloc([128, 4096], F32)
    S_bf = ar.alloc([128, 4096], BF16)
    ssq = ar.alloc([128, 16], F32)
    vv4 = ar.alloc([128, 4], F32)
    og = ar.alloc([128, 2048], F32)
    go = [ar.alloc([128, 2048], BF16) for _ in range(2)]

    s_c = ctx.sem("p8_c")
    s_qk = ctx.sem("p8_qk")
    s_kv = ctx.sem("p8_kv")
    s_pre = ctx.sem("p8_pre")
    s_simm = ctx.sem("p8_simm")
    s_sev = ctx.sem("p8_sev")
    s_tin = ctx.sem("p8_tin")
    s_och = ctx.sem("p8_och")
    s_sta = ctx.sem("p8_sta")
    s_sst = ctx.sem("p8_sst")
    s_sbf = ctx.sem("p8_sbf")
    s_rA = ctx.sem("p8_rA")
    s_rB = ctx.sem("p8_rB")
    s_rC = ctx.sem("p8_rC")
    s_oev = ctx.sem("p8_oev")
    s_ost = ctx.sem("p8_ost")

    @ctx.sec("sync")
    def _(sy):
        s_c.inc(sy.dma_start(out=mask_sb[:], in_=S["mask1"]), 16)
        s_c.inc(sy.dma_start(out=cross_sb[:], in_=S["cross1"]), 16)
        s_c.inc(sy.dma_start(out=kdec_sb[:], in_=S["kdec1f"]), 16)
        s_c.inc(sy.dma_start(out=gnw[:], in_=S["gnw_bc"]), 16)

    for b in range(2):
        @ctx.sec("sync")
        def _(sy, b=b):
            if b == 1:
                s_och.wait(sy, 16)
                s_pre.wait(sy, 16)
            src_q = S["qTr"].rearrange("(c p) t -> p c t", p=128)[:, :, b * 2048:(b + 1) * 2048]
            src_k = S["kTr"].rearrange("(c p) t -> p c t", p=128)[:, :, b * 2048:(b + 1) * 2048]
            s_qk.inc(sy.dma_start(out=qT_sb.rearrange("p (c t) -> p c t", c=8), in_=src_q), 16)
            s_qk.inc(sy.dma_start(out=kT_sb.rearrange("p (c t) -> p c t", c=8), in_=src_k), 16)

        @ctx.sec("vector")
        def _(v, b=b):
            if b == 1:
                s_och.wait(v, 16)
            v.memset(S_sb[:], 0.0)
            s_sbf.inc(v.memset(S_bf[:], 0.0))

        for j in range(16):
            jj = b * 16 + j

            sta_now = s_sta.rel
            sst_now = s_sst.rel

            @ctx.sec("sync")
            def _(sy, jj=jj, j=j, b=b, sta_now=sta_now):
                if jj >= 2:
                    s_och.wait(sy, jj - 1)
                    s_pre.wait(sy, jj - 1)
                    s_rA.wait(sy, jj - 1)
                    s_sta.wait(sy, sta_now)
                buf = jj % 2
                s_kv.inc(sy.dma_start(out=ktb[buf][:],
                                      in_=S["kTr"][:, jj * 128:(jj + 1) * 128],
                                      transpose=True), 16)
                s_kv.inc(sy.dma_start(out=vtb[buf][:],
                                      in_=S["vtm1"][jj * 128:(jj + 1) * 128, :]), 16)
                s_kv.inc(sy.dma_start(out=gtb[buf][:],
                                      in_=S["gtm1"][jj * 128:(jj + 1) * 128, :]), 16)

            @ctx.sec("vector")
            def _(v, b=b, j=j, jj=jj):
                buf = jj % 2
                s_kv.wait(v, 48 * (jj + 1))
                s_qk.wait(v, 32 * (b + 1))
                if jj == 0:
                    s_c.wait(v, 64)
                if jj >= 1:
                    s_och.wait(v, jj)
                v.tensor_mul(ksc[buf][:], ktb[buf][:], kdec_sb[:])
                for h in range(4):
                    for dkc in range(2):
                        c = 2 * h + dkc
                        v.tensor_mul(qc_bf[:, c * 128:(c + 1) * 128],
                                     qT_sb[:, c * 2048 + j * 128:c * 2048 + j * 128 + 128],
                                     cross_sb[:, h * 128:(h + 1) * 128])
                s_pre.inc(v.engine_nop())

            for h in range(4):
                @ctx.sec("tensor")
                def _(t, b=b, j=j, jj=jj, h=h, sst_now=sst_now):
                    if jj == 0 and h == 0:
                        t.wait_ge(s_c.h, s_c.base + 64)
                    s_qk.wait(t, 32 * (b + 1))
                    s_sst.wait(t, sst_now)
                    hh = jj * 4 + h
                    if hh >= 4:
                        s_sev.wait(t, hh - 3)
                    for dkc in range(2):
                        c = 2 * h + dkc
                        inst = t.matmul(
                            ctx.pp[4 + h][:, 0:128],
                            kT_sb[:, c * 2048 + j * 128:c * 2048 + j * 128 + 128],
                            qT_sb[:, c * 2048 + j * 128:c * 2048 + j * 128 + 128],
                            start=(dkc == 0), stop=(dkc == 1))
                    s_simm.inc(inst)

            for h in range(4):
                @ctx.sec("vector")
                def _(v, jj=jj, h=h):
                    hh = jj * 4 + h
                    s_simm.wait(v, hh + 1)
                    if jj >= 1:
                        s_tin.wait(v, (jj - 1) * 4 + h + 1)
                    s_sev.inc(v.tensor_tensor(simb[:, h * 128:(h + 1) * 128],
                                              ctx.pp[4 + h][:, 0:128],
                                              mask_sb[:, h * 128:(h + 1) * 128],
                                              ALU.mult))

            @ctx.sec("tensor")
            def _(t, b=b, j=j, jj=jj):
                buf = jj % 2
                s_pre.wait(t, jj + 1)
                s_oev.wait(t, 4 * jj)
                s_rA.wait(t, jj)
                last = None
                for h in range(4):
                    hh = jj * 4 + h
                    s_sev.wait(t, hh + 1)
                    last = t.matmul(ctx.pp[h][:, :], simb[:, h * 128:(h + 1) * 128],
                                    vtb[buf][:, h * 512:(h + 1) * 512],
                                    start=True, stop=False, skip_group_check=True)
                    s_tin.inc(last)
                if True:
                    s_sbf.wait(t, b * 17 + 1 + j)
                    for h in range(4):
                        for dkc in range(2):
                            c = 2 * h + dkc
                            last = t.matmul(ctx.pp[h][:, :],
                                            qc_bf[:, c * 128:(c + 1) * 128],
                                            S_bf[:, h * 1024 + dkc * 512:h * 1024 + (dkc + 1) * 512],
                                            start=False, stop=(dkc == 1),
                                            skip_group_check=True)
                s_och.inc(last)

            if j < 15:
                for wave in range(2):
                    sst_t = s_sst.rel
                    sta_t = s_sta.rel

                    @ctx.sec("tensor")
                    def _(t, jj=jj, wave=wave, sst_t=sst_t):
                        buf = jj % 2
                        s_sst.wait(t, sst_t)
                        s_sev.wait(t, 4 * (jj + 1))
                        last = None
                        for h in range(4):
                            last = t.matmul(
                                ctx.pp[4 + h][:, :],
                                ksc[buf][:, h * 256 + wave * 128:h * 256 + wave * 128 + 128],
                                vtb[buf][:, h * 512:(h + 1) * 512],
                                start=True, stop=True, skip_group_check=True)
                        s_sta.inc(last)

                    @ctx.sec("vector")
                    def _(v, b=b, j=j, jj=jj, wave=wave, sta_t=sta_t):
                        s_sta.wait(v, sta_t + 1)
                        if wave == 0:
                            s_och.wait(v, jj + 1)
                        for h in range(4):
                            sl = slice(h * 1024 + wave * 512, h * 1024 + (wave + 1) * 512)
                            s_sst.inc(v.scalar_tensor_tensor(
                                S_sb[:, sl], S_sb[:, sl], gC[h], ctx.pp[4 + h][:, :],
                                ALU.mult, ALU.add))

                @ctx.sec("vector")
                def _(v, b=b, j=j, jj=jj):
                    v.tensor_copy(S_bf[:], S_sb[:])
                    s_sbf.inc(v.engine_nop())
            else:
                @ctx.sec("vector")
                def _(v, jj=jj):
                    s_och.wait(v, jj + 1)
                    s_sbf.inc(v.engine_nop())

            @ctx.sec("scalar")
            def _(sc, jj=jj):
                buf = jj % 2
                s_och.wait(sc, jj + 1)
                s_kv.wait(sc, 48 * (jj + 1))
                if jj >= 2:
                    s_oev.wait(sc, 4 * (jj - 1))
                for h in range(4):
                    sc.activation(og[:, h * 512:(h + 1) * 512], ctx.pp[h][:, :],
                                  AF.Square, scale=1.0 / (512.0 ** 0.5),
                                  accum_out=ssq[:, 4 * h:4 * h + 1])
                s_rA.inc(sc.activation(gsi[:], gtb[buf][:], AF.Silu))

            @ctx.sec("vector")
            def _(v, jj=jj):
                s_rA.wait(v, jj + 1)
                ssq_v = ssq[:, 0:16].rearrange("p (g c) -> p g c", c=4)
                s_rB.inc(v.tensor_scalar_add(vv4[:], ssq_v[:, :, 0:1], EPS))

            @ctx.sec("scalar")
            def _(sc, jj=jj):
                s_rB.wait(sc, jj + 1)
                inst = sc.activation(vv4[:], vv4[:], AF.Ln)
                ctx.fence(sc, inst)
                s_rC.inc(sc.activation(vv4[:], vv4[:], AF.Exp, scale=-0.5))

            @ctx.sec("vector")
            def _(v, jj=jj):
                s_rC.wait(v, jj + 1)
                if jj >= 2:
                    s_ost.wait(v, 16 * (jj - 1))
                gob = go[jj % 2]
                for h in range(4):
                    sl = slice(h * 512, (h + 1) * 512)
                    v.scalar_tensor_tensor(og[:, sl], ctx.pp[h][:, :],
                                           vv4[:, h:h + 1], gnw[:],
                                           ALU.mult, ALU.mult)
                for h in range(4):
                    sl = slice(h * 512, (h + 1) * 512)
                    v.tensor_mul(gob[:, sl], og[:, sl], gsi[:, sl])
                s_oev.inc(v.engine_nop(), 4)

            @ctx.sec("sync")
            def _(sy, jj=jj):
                s_oev.wait(sy, 4 * (jj + 1))
                s_ost.inc(sy.dma_start(out=S["gotm1"][jj * 128:(jj + 1) * 128, :],
                                       in_=go[jj % 2][:]), 16)



def ph9_oproj1(ctx):
    nc, ar, S = ctx.nc, ctx.ar, ctx.S
    ar.reset()

    wo = ar.alloc([128, 16384], BF16)
    oT = [ar.alloc([128, 512], BF16) for _ in range(32)]
    s2f = [ar.alloc([128, 1024], F32) for _ in range(2)]

    s_w = ctx.sem("p9_w")
    s_a = ctx.sem("p9_a")
    s_mm = ctx.sem("p9_mm")
    s_ev = ctx.sem("p9_ev")
    s_st = ctx.sem("p9_st")

    @ctx.sec("gpsimd")
    def _(g):
        load_wT(ctx, g, S["WoT"], 2048, 1024, wo, s_w)

    for i in range(NTILE):
        b = i // 16
        buf = i % 2
        tb = i // 4

        if i % 4 == 0:
            @ctx.sec("sync")
            def _(sy, i=i, tb=tb):
                if tb >= 2:
                    s_mm.wait(sy, 2 * (tb - 1) * 4)
                for fc in range(16):
                    tload(ctx, sy, S["gotm1"], tb * 512, 512, fc * 128, 128,
                          oT[(tb % 2) * 16 + fc], s_a)

        for oc in range(2):
            @ctx.sec("tensor")
            def _(t, i=i, tb=tb, oc=oc):
                if i == 0 and oc == 0:
                    t.wait_ge(s_w.h, s_w.base + 16)
                s_a.wait(t, 256 * (tb + 1))
                k = 2 * i + oc
                if k >= 4:
                    s_ev.wait(t, i - 1)
                for fc in range(16):
                    inst = t.matmul(ctx.pp[k % 4][:, :],
                                    oT[(tb % 2) * 16 + fc][:, (i % 4) * 128:(i % 4) * 128 + 128],
                                    wslice(wo, 1024, fc, oc * 512, oc * 512 + 512),
                                    start=(fc == 0), stop=(fc == 15))
                s_mm.inc(inst)

        @ctx.sec("vector")
        def _(v, i=i, buf=buf):
            s_mm.wait(v, 2 * (i + 1))
            if i >= 2:
                s_st.wait(v, 16 * (i - 1))
            for oc in range(2):
                sl = slice(oc * 512, oc * 512 + 512)
                v.tensor_copy(s2f[buf][:, sl], ctx.pp[(2 * i + oc) % 4][:, :])
            s_ev.inc(v.engine_nop())

        @ctx.sec("gpsimd")
        def _(g, i=i, buf=buf):
            s_ev.wait(g, i + 1)
            s_st.inc(g.dma_start(out=S["s2_f32"][i * 128:(i + 1) * 128, :],
                                  in_=s2f[buf][:]), 16)



def ph10_rtr1(ctx):
    nc, ar, S = ctx.nc, ctx.ar, ctx.S
    ar.reset()

    rw = ar.alloc([128, 64], BF16)
    ga = [ar.alloc([128, 512], BF16) for _ in range(16)]
    lg = ar.alloc([128, 256], F32)
    ee = ar.alloc([128, 256], F32)
    e2 = ar.alloc([128, 256], F32)
    m1 = ar.alloc([128, 32], F32)
    m2 = ar.alloc([128, 32], F32)
    den = ar.alloc([128, 32], F32)
    cw = ar.alloc([128, 256], F32)

    s_w = ctx.sem("p10_w")
    s_a = ctx.sem("p10_a")
    s_mm = ctx.sem("p10_mm")
    s_sx = ctx.sem("p10_sx")
    s_st = ctx.sem("p10_st")

    @ctx.sec("gpsimd")
    def _(g):
        load_wT(ctx, g, S["gate_wT"], 1024, 8, rw, s_w)

    for tb in range(8):
        gbuf = [ga[(tb % 2) * 8 + dc] for dc in range(8)]

        @ctx.sec("sync")
        def _(sy, tb=tb, gbuf=gbuf):
            if tb >= 2:
                s_mm.wait(sy, 4 * (tb - 1))
            for dc in range(8):
                tload(ctx, sy, S["xn1_bf"], tb * 512, 512, dc * 128, 128, gbuf[dc], s_a)

        for tc in range(4):
            i = tb * 4 + tc

            @ctx.sec("tensor")
            def _(t, tb=tb, tc=tc, i=i, gbuf=gbuf):
                if i == 0:
                    t.wait_ge(s_w.h, s_w.base + 16)
                s_a.wait(t, 16 * 8 * (tb + 1))
                for dc in range(8):
                    inst = t.matmul(ctx.pp[4][:, i * 8:(i + 1) * 8],
                                    gbuf[dc][:, tc * 128:tc * 128 + 128],
                                    wslice(rw, 8, dc, 0, 8),
                                    start=(i == 0 and dc == 0), stop=(dc == 7),
                                    skip_group_check=True)
                s_mm.inc(inst)

    v3 = lambda ap: ap.rearrange("p (g c) -> p g c", c=8)
    bc = lambda ap: ap.rearrange("p (g u) -> p g u", u=1).broadcast_to([128, 32, 8])

    @ctx.sec("vector")
    def _(v):
        s_mm.wait(v, 32)
        inst = v.tensor_copy(lg[:], ctx.pp[4][:, 0:256])
        ctx.fence(v, inst)
        inst = v.tensor_reduce(m1[:], v3(lg[:]), axis=AX.X, op=ALU.max, negate=True)
        ctx.fence(v, inst)
        s_sx.inc(v.tensor_tensor(v3(ee[:]), v3(lg[:]), bc(m1[:]), ALU.add))

    @ctx.sec("scalar")
    def _(sc):
        s_sx.wait(sc, 1)
        s_sx.inc(sc.activation(ee[:], ee[:], AF.Exp))

    @ctx.sec("vector")
    def _(v):
        s_sx.wait(v, 2)
        inst = v.reduce_max(m1[:], v3(ee[:]), axis=AX.X)
        ctx.fence(v, inst)
        v.tensor_tensor(v3(cw[:]), v3(ee[:]), bc(m1[:]), ALU.is_ge)
        inst = v.scalar_tensor_tensor(e2[:], cw[:], -1e6, ee[:], ALU.mult, ALU.add)
        ctx.fence(v, inst)
        inst = v.reduce_max(m2[:], v3(e2[:]), axis=AX.X)
        ctx.fence(v, inst)
        v.tensor_tensor(v3(e2[:]), v3(e2[:]), bc(m2[:]), ALU.is_ge)
        inst = v.tensor_add(den[:], m1[:], m2[:])
        ctx.fence(v, inst)
        inst = v.reciprocal(den[:], den[:])
        ctx.fence(v, inst)
        v.tensor_add(cw[:], cw[:], e2[:])
        v.tensor_mul(cw[:], cw[:], ee[:])
        s_sx.inc(v.tensor_tensor(v3(cw[:]), v3(cw[:]), bc(den[:]), ALU.mult))

    @ctx.sec("sync")
    def _(sy):
        s_sx.wait(sy, 3)
        s_st.inc(sy.dma_start(out=S["cmb1"], in_=cw[:]), 16)



def ph11_moe1(ctx):
    nc, ar, S = ctx.nc, ctx.ar, ctx.S
    ar.reset()

    wgu = [ar.alloc([128, 8192], BF16) for _ in range(2)]
    wosb = ar.alloc([128, 32768], BF16)
    cwsb = ar.alloc([128, 256], F32)
    ga = [ar.alloc([128, 512], BF16) for _ in range(24)]
    hT = [ar.alloc([128, 16384], BF16)]
    sil = ar.alloc([128, 512], BF16)
    tmp = [ar.alloc([128, 512], F32) for _ in range(4)]
    zt = ar.alloc([128, 1024], F32)

    s_w = ctx.sem("p11_w")
    s_wo = ctx.sem("p11_wo")
    s_z = ctx.sem("p11_z")
    s_a = ctx.sem("p11_a")
    s_a2 = ctx.sem("p11_a2")
    s_m1 = ctx.sem("p11_m1")
    s_si = ctx.sem("p11_si")
    s_h = ctx.sem("p11_h")
    s_m2 = ctx.sem("p11_m2")
    s_ev = ctx.sem("p11_ev")
    s_ac = ctx.sem("p11_ac")

    @ctx.sec("vector")
    def _(v):
        v.memset(zt[:], 0.0)
        s_z.inc(v.engine_nop())

    @ctx.sec("sync")
    def _(sy):
        s_z.wait(sy, 1)
        for i in range(NTILE):
            s_z.inc(sy.dma_start(out=S["moe1"][i * 128:(i + 1) * 128, :], in_=zt[:]), 16)
        s_z.inc(sy.dma_start(out=cwsb[:], in_=S["cmb1"]), 16)

    for e in range(8):
        @ctx.sec("gpsimd")
        def _(g, e=e):
            if e >= 1:
                s_m2.wait(g, 64 * e)
            load_wT(ctx, g, S["woT_e"][e], 4096, 1024, wosb, s_wo)

        for tb in range(8):
            et = e * 8 + tb
            gbuf = [ga[(et % 3) * 8 + dc] for dc in range(8)]
            hbuf = hT[0]

            @ctx.sec("sync")
            def _(sy, et=et, tb=tb, gbuf=gbuf):
                if et >= 3:
                    s_m1.wait(sy, (et - 2) * 32)
                for dc in range(8):
                    tload(ctx, sy, S["xn1_bf"], tb * 512, 512, dc * 128, 128, gbuf[dc], s_a)

            for ob in range(8):
                ko = et * 8 + ob
                wb = wgu[ko % 2]

                @ctx.sec("gpsimd")
                def _(g, e=e, ob=ob, ko=ko, wb=wb):
                    if ko >= 2:
                        s_m1.wait(g, (ko - 1) * 4)
                    src_g = S["wgT_e"][e].rearrange("(c p) o -> p c o", p=128)[:, :, ob * 512:(ob + 1) * 512]
                    src_u = S["wuT_e"][e].rearrange("(c p) o -> p c o", p=128)[:, :, ob * 512:(ob + 1) * 512]
                    s_w.inc(g.dma_start(out=wb.rearrange("p (c o) -> p c o", c=8)[:, :, 0:512],
                                        in_=src_g), 16)
                    s_w.inc(g.dma_start(out=wb.rearrange("p (c o) -> p c o", c=8)[:, :, 512:1024],
                                        in_=src_u), 16)

                for fc in range(4):
                    kf = ko * 4 + fc
                    hc = ob * 4 + fc

                    @ctx.sec("tensor")
                    def _(t, tb=tb, et=et, ob=ob, fc=fc, ko=ko, kf=kf, wb=wb, gbuf=gbuf):
                        s_w.wait(t, 32 * (ko + 1))
                        s_a.wait(t, 128 * (et + 1))
                        if kf >= 2:
                            s_si.wait(t, kf - 1)
                        if kf >= 2:
                            s_h.wait(t, kf - 1)
                        for dc in range(8):
                            inst = t.matmul(ctx.pp[kf % 2][:, :],
                                            wb[:, dc * 1024 + fc * 128:dc * 1024 + fc * 128 + 128],
                                            gbuf[dc][:], start=(dc == 0), stop=(dc == 7))
                        for dc in range(8):
                            inst2 = t.matmul(ctx.pp[2 + kf % 2][:, :],
                                             wb[:, dc * 1024 + 512 + fc * 128:dc * 1024 + 512 + fc * 128 + 128],
                                             gbuf[dc][:], start=(dc == 0), stop=(dc == 7))
                        s_m1.inc(inst2)

                    @ctx.sec("scalar")
                    def _(sc, kf=kf):
                        s_m1.wait(sc, kf + 1)
                        if kf >= 1:
                            s_h.wait(sc, kf)
                        s_si.inc(sc.activation(sil[:], ctx.pp[kf % 2][:, :], AF.Silu))

                    @ctx.sec("vector")
                    def _(v, et=et, hc=hc, kf=kf, hbuf=hbuf):
                        s_si.wait(v, kf + 1)
                        if et >= 1 and hc == 0:
                            s_m2.wait(v, 8 * et)
                        s_h.inc(v.tensor_tensor(hbuf[:, hc * 512:(hc + 1) * 512],
                                                sil[:], ctx.pp[2 + kf % 2][:, :],
                                                ALU.mult))

            for tc in range(4):
                i = tb * 4 + tc
                for oc2 in range(2):
                    kk = et * 8 + tc * 2 + oc2
                    pb2 = ctx.pp[4 + kk % 4]

                    @ctx.sec("tensor")
                    def _(t, e=e, tb=tb, tc=tc, oc2=oc2, et=et, kk=kk, pb2=pb2, hbuf=hbuf):
                        s_h.wait(t, 32 * (et + 1))
                        s_wo.wait(t, 16 * (e + 1))
                        if kk >= 4:
                            s_ev.wait(t, kk - 3)
                        for hc in range(32):
                            inst = t.matmul(
                                pb2[:, :],
                                hbuf[:, hc * 512 + tc * 128:hc * 512 + tc * 128 + 128],
                                wslice(wosb, 1024, hc, oc2 * 512, oc2 * 512 + 512),
                                start=(hc == 0), stop=(hc == 31))
                        s_m2.inc(inst)

                    @ctx.sec("vector")
                    def _(v, e=e, i=i, kk=kk, pb2=pb2):
                        s_m2.wait(v, kk + 1)
                        if kk == 0:
                            s_z.wait(v, 1 + 16 * NTILE + 16)
                        if kk >= 4:
                            s_ac.wait(v, 16 * (kk - 3))
                        s_ev.inc(v.tensor_scalar_mul(tmp[kk % 4][:], pb2[:, :],
                                                     cwsb[:, i * 8 + e:i * 8 + e + 1]))

                    @ctx.sec("gpsimd")
                    def _(g, i=i, oc2=oc2, kk=kk):
                        s_ev.wait(g, kk + 1)
                        if kk == 0:
                            s_z.wait(g, 1 + 16 * NTILE)
                        s_ac.inc(g.dma_start(
                            out=S["moe1"][i * 128:(i + 1) * 128, oc2 * 512:(oc2 + 1) * 512],
                            in_=tmp[kk % 4][:], accum_op=ALU.add), 16)



def ph12_final(ctx):
    nc, ar, S = ctx.nc, ctx.ar, ctx.S
    ar.reset()

    x1t = [ar.alloc([128, 1024], F32) for _ in range(2)]
    s2t = [ar.alloc([128, 1024], F32) for _ in range(2)]
    mot = [ar.alloc([128, 1024], F32) for _ in range(2)]
    g1t = [ar.alloc([128, 1024], F32) for _ in range(2)]
    yt = [ar.alloc([128, 1024], F32) for _ in range(2)]

    s_l = ctx.sem("p12_l")
    s_lg = ctx.sem("p12_lg")
    s_si = ctx.sem("p12_si")
    s_v2 = ctx.sem("p12_v2")
    s_st = ctx.sem("p12_st")

    for i in range(NTILE):
        buf = i % 2
        rows = slice(i * 128, (i + 1) * 128)

        @ctx.sec("sync")
        def _(sy, i=i, buf=buf, rows=rows):
            if i >= 2:
                s_v2.wait(sy, i - 1)
                s_si.wait(sy, i - 1)
            s_l.inc(sy.dma_start(out=x1t[buf][:], in_=S["x1_f32"][rows, :]), 16)
            s_l.inc(sy.dma_start(out=s2t[buf][:], in_=S["s2_f32"][rows, :]), 16)

        @ctx.sec("gpsimd")
        def _(g, i=i, buf=buf, rows=rows):
            if i >= 2:
                s_v2.wait(g, i - 1)
                s_si.wait(g, i - 1)
            s_lg.inc(g.dma_start(out=mot[buf][:], in_=S["moe1"][rows, :]), 16)

        @ctx.sec("scalar")
        def _(sc, i=i, buf=buf):
            s_l.wait(sc, 32 * (i + 1))
            s_lg.wait(sc, 16 * (i + 1))
            if i >= 2:
                s_v2.wait(sc, i - 1)
            s_si.inc(sc.activation(g1t[buf][:], mot[buf][:], AF.Silu))

        @ctx.sec("vector")
        def _(v, i=i, buf=buf):
            s_si.wait(v, i + 1)
            s_l.wait(v, 32 * (i + 1))
            if i >= 2:
                s_st.wait(v, 16 * (i - 1))
            v.tensor_mul(yt[buf][:], s2t[buf][:], g1t[buf][:])
            s_v2.inc(v.tensor_add(yt[buf][:], yt[buf][:], x1t[buf][:]))

        @ctx.sec("sync")
        def _(sy, i=i, buf=buf, rows=rows):
            s_v2.wait(sy, i + 1)
            s_st.inc(sy.dma_start(out=S["y"][rows, :], in_=yt[buf][:]), 16)




def ph_rte(ctx, CMB, IDX, CWTM, XE, SRC, NSLOT, BOUNCE=True):
    nc, ar, S = ctx.nc, ctx.ar, ctx.S
    ar.reset()

    cw = ar.alloc([128, 256], F32)
    ltc = ar.alloc([128, 128], F32)
    tokf = ar.alloc([128, 32], F32)
    toki = ar.alloc([128, 32], I32)
    ones32 = ar.alloc([128, 32], F32)
    oobt = ar.alloc([128, 128], I32)
    mask = ar.alloc([128, 32], F32)
    incl = ar.alloc([128, 32], F32)
    excl = ar.alloc([128, 32], F32)
    nmask = ar.alloc([128, 32], F32)
    slotf = ar.alloc([128, 32], F32)
    sloti = [ar.alloc([128, 32], I32) for _ in range(2)]
    pfx = ar.alloc([128, 1], F32)
    idxc = [ar.alloc([128, 1], I32) for _ in range(4)]
    xg = [ar.alloc([128, 1024], BF16) for _ in range(4)]

    s_l = ctx.sem("rt_l")
    s_pf = ctx.sem("rt_pf")
    s_sc = ctx.sem("rt_sc")
    s_ix = ctx.sem("rt_ix")
    s_gx = ctx.sem("rt_gx")
    s_sx = ctx.sem("rt_sx")

    @ctx.sec("vector")
    def _(v):
        v.memset(ones32[:], 1.0)
        s_l.inc(v.memset(oobt[:], 1000000))

    @ctx.sec("sync")
    def _(sy):
        s_l.wait(sy, 1)
        s_l.inc(sy.dma_start(out=cw[:], in_=S[CMB]), 16)
        s_l.inc(sy.dma_start(out=ltc[:], in_=S["ltstrict"]), 16)
        s_l.inc(sy.dma_start(out=tokf[:], in_=S["tokid_f"]), 16)
        s_l.inc(sy.dma_start(out=toki[:], in_=S["tokid_i"]), 16)
        s_l.inc(sy.dma_start(out=S[IDX].rearrange("(a pf) -> a pf", a=128),
                             in_=oobt[:, 0:NSLOT * 8 // 128]), 16)
        s_l.inc(sy.dma_start(
            out=S[CWTM].rearrange("(t p) e -> p t e", p=128),
            in_=cw[:].rearrange("p (t e) -> p t e", e=8),
            allow_slow_non_contiguous=True), 16)

    for e in range(8):
        @ctx.sec("vector")
        def _(v, e=e):
            v.wait_ge(s_l.h, s_l.base + 97)
            if e >= 2:
                s_sc.wait(v, 512 * (e - 1))
            inst = v.tensor_single_scalar(
                mask[:].rearrange("p (t u) -> p t u", u=1),
                cw[:].rearrange("p (t c) -> p t c", c=8)[:, :, e:e + 1],
                0.0, ALU.is_gt)
            ctx.fence(v, inst)
            inst = v.tensor_tensor_scan(incl[:], ones32[:], mask[:], 0.0, ALU.mult, ALU.add)
            ctx.fence(v, inst)
            inst = v.tensor_sub(excl[:], incl[:], mask[:])
            ctx.fence(v, inst)
            inst = v.tensor_single_scalar(slotf[:], excl[:], float(NSLOT), ALU.is_lt)
            ctx.fence(v, inst)
            inst = v.tensor_mul(mask[:], mask[:], slotf[:])
            ctx.fence(v, inst)
            s_pf.inc(v.engine_nop())

        @ctx.sec("tensor")
        def _(t, e=e):
            s_pf.wait(t, 3 * e + 1)
            s_pf.inc(t.matmul(ctx.pp[0][:, 0:1], ltc[:], incl[:, 31:32],
                              start=True, stop=True))

        @ctx.sec("vector")
        def _(v, e=e):
            s_pf.wait(v, 3 * e + 2)
            inst = v.tensor_copy(pfx[:], ctx.pp[0][:, 0:1])
            ctx.fence(v, inst)
            inst = v.tensor_scalar(slotf[:], excl[:], pfx[:, 0:1], float(e * NSLOT),
                                   ALU.add, ALU.add)
            ctx.fence(v, inst)
            inst = v.tensor_scalar(nmask[:], mask[:], -1.0, 1.0, ALU.mult, ALU.add)
            ctx.fence(v, inst)
            inst = v.scalar_tensor_tensor(slotf[:], nmask[:], 1000000.0, slotf[:],
                                          ALU.mult, ALU.add)
            ctx.fence(v, inst)
            inst = v.tensor_copy(sloti[e % 2][:], slotf[:])
            ctx.fence(v, inst)
            s_pf.inc(v.engine_nop())

        @ctx.sec("gpsimd")
        def _(g, e=e):
            s_pf.wait(g, 3 * (e + 1))
            for tc in range(32):
                s_sc.inc(g.indirect_dma_start(
                    S[IDX].rearrange("(s u) -> s u", u=1),
                    IndirectOffsetOnAxis(sloti[e % 2][:, tc:tc + 1], 0),
                    toki[:, tc:tc + 1], None), 16)

    if not BOUNCE:
        return
    nst = NSLOT // 128
    for e in range(8):
        for st in range(nst):
            k = e * nst + st
            buf = k % 4

            @ctx.sec("sync")
            def _(sy, e=e, st=st, k=k, buf=buf):
                sy.wait_ge(s_sc.h, s_sc.base + 4096)
                if k >= 4:
                    s_gx.wait(sy, 16 * (k - 3))
                s_ix.inc(sy.dma_start(
                    out=idxc[buf][:],
                    in_=S[IDX].rearrange("(s u) -> s u", u=1)[e * NSLOT + st * 128:
                                                              e * NSLOT + (st + 1) * 128, :]), 16)

            @ctx.sec("gpsimd")
            def _(g, k=k, buf=buf):
                s_ix.wait(g, 16 * (k + 1))
                if k >= 4:
                    s_sx.wait(g, 16 * (k - 3))
                s_gx.inc(g.indirect_dma_start(
                    xg[buf][:], None, S[SRC],
                    IndirectOffsetOnAxis(idxc[buf][:, 0:1], 0)), 16)

            @ctx.sec("sync")
            def _(sy, e=e, st=st, k=k, buf=buf):
                s_gx.wait(sy, 16 * (k + 1))
                s_sx.inc(sy.dma_start(
                    out=S[XE][e * NSLOT + st * 128:e * NSLOT + (st + 1) * 128, :],
                    in_=xg[buf][:]), 16)


def ph_rte1(ctx):
    ph_rte(ctx, "cmb1", "idx1", "cw_tm", "xe_tm", "xn1_bf", 1536, BOUNCE=False)


def ph_rte0(ctx):
    ph_rte(ctx, "cmb0", "idx0", "cw0_tm", "me_tm", "mla_tm", 2048)



def ph11s_moe1(ctx):
    nc, ar, S = ctx.nc, ctx.ar, ctx.S
    ar.reset()

    wgu = [ar.alloc([128, 8192], BF16) for _ in range(2)]
    wosb = ar.alloc([128, 32768], BF16)
    ga = [ar.alloc([128, 512], BF16) for _ in range(24)]
    hT = [ar.alloc([128, 16384], BF16)]
    sil = ar.alloc([128, 512], BF16)
    tmp = [ar.alloc([128, 1024], F32) for _ in range(2)]
    zt = ar.alloc([128, 1024], F32)
    idxc = [ar.alloc([128, 4], I32) for _ in range(2)]
    cwg = [ar.alloc([128, 32], F32) for _ in range(2)]
    xgb = [ar.alloc([128, 1024], BF16) for _ in range(4)]
    idxb = [ar.alloc([128, 1], I32) for _ in range(4)]

    s_w = ctx.sem("p1s_w")
    s_wo = ctx.sem("p1s_wo")
    s_z = ctx.sem("p1s_z")
    s_a = ctx.sem("p1s_a")
    s_ix = ctx.sem("p1s_ix")
    s_cg = ctx.sem("p1s_cg")
    s_m1 = ctx.sem("p1s_m1")
    s_si = ctx.sem("p1s_si")
    s_h = ctx.sem("p1s_h")
    s_m2 = ctx.sem("p1s_m2")
    s_ev = ctx.sem("p1s_ev")
    s_ac = ctx.sem("p1s_ac")
    s_bix = ctx.sem("p1s_bix")
    s_bgx = ctx.sem("p1s_bgx")
    s_bsx = ctx.sem("p1s_bsx")

    def emit_bounce(e):
        for st in range(12):
            k = e * 12 + st
            buf = k % 4

            @ctx.sec("sync")
            def _(sy, e=e, st=st, k=k, buf=buf):
                if k >= 4:
                    s_bgx.wait(sy, 16 * (k - 3))
                s_bix.inc(sy.dma_start(
                    out=idxb[buf][:],
                    in_=S["idx1"].rearrange("(s u) -> s u", u=1)[e * 1536 + st * 128:
                                                                e * 1536 + (st + 1) * 128, :]), 16)

            @ctx.sec("gpsimd")
            def _(g, k=k, buf=buf):
                s_bix.wait(g, 16 * (k + 1))
                if k >= 4:
                    s_bsx.wait(g, 16 * (k - 3))
                s_bgx.inc(g.indirect_dma_start(
                    xgb[buf][:], None, S["xn1_bf"],
                    IndirectOffsetOnAxis(idxb[buf][:, 0:1], 0)), 16)

            @ctx.sec("sync")
            def _(sy, e=e, st=st, k=k, buf=buf):
                s_bgx.wait(sy, 16 * (k + 1))
                s_bsx.inc(sy.dma_start(
                    out=S["xe_tm"][e * 1536 + st * 128:e * 1536 + (st + 1) * 128, :],
                    in_=xgb[buf][:]), 16)

    @ctx.sec("vector")
    def _(v):
        v.memset(zt[:], 0.0)
        s_z.inc(v.engine_nop())

    @ctx.sec("sync")
    def _(sy):
        s_z.wait(sy, 1)
        for i in range(NTILE):
            s_z.inc(sy.dma_start(out=S["moe1"][i * 128:(i + 1) * 128, :], in_=zt[:]), 16)

    emit_bounce(0)
    for e in range(8):
        if e + 1 < 8:
            emit_bounce(e + 1)

        @ctx.sec("gpsimd")
        def _(g, e=e):
            if e >= 1:
                s_m2.wait(g, 24 * e)
            load_wT(ctx, g, S["woT_e"][e], 4096, 1024, wosb, s_wo)

        for tb in range(3):
            et = e * 3 + tb
            gbuf = [ga[(et % 2) * 8 + dc] for dc in range(8)]
            hbuf = hT[0]
            base = e * 1536 + tb * 512

            @ctx.sec("sync")
            def _(sy, e=e, et=et, tb=tb, base=base, gbuf=gbuf):
                s_bsx.wait(sy, 192 * (e + 1))
                if et >= 2:
                    s_m1.wait(sy, (et - 1) * 32)
                if et >= 2:
                    s_cg.wait(sy, 64 * (et - 1))
                    s_ac.wait(sy, 64 * (et - 1))
                s_ix.inc(sy.dma_start(
                    out=idxc[et % 2][:],
                    in_=S["idx1"].rearrange("(s u) -> u s", u=1)[:, base:base + 512]
                        .rearrange("u (t p) -> p (u t)", p=128),
                    allow_slow_non_contiguous=True), 16)
                for dc in range(8):
                    tload(ctx, sy, S["xe_tm"], base, 512, dc * 128, 128, gbuf[dc], s_a)

            @ctx.sec("gpsimd")
            def _(g, e=e, et=et, base=base):
                s_ix.wait(g, 16 * (et + 1))
                if et >= 2:
                    s_ev.wait(g, 8 * (et - 1))
                for tc in range(4):
                    s_cg.inc(g.indirect_dma_start(
                        cwg[et % 2][:, tc * 8:(tc + 1) * 8], None, S["cw_tm"],
                        IndirectOffsetOnAxis(idxc[et % 2][:, tc:tc + 1], 0)), 16)

            for ob in range(8):
                ko = et * 8 + ob
                wb = wgu[ko % 2]

                @ctx.sec("gpsimd")
                def _(g, e=e, ob=ob, ko=ko, wb=wb):
                    if ko >= 2:
                        s_m1.wait(g, (ko - 1) * 4)
                    src_g = S["wgT_e"][e].rearrange("(c p) o -> p c o", p=128)[:, :, ob * 512:(ob + 1) * 512]
                    src_u = S["wuT_e"][e].rearrange("(c p) o -> p c o", p=128)[:, :, ob * 512:(ob + 1) * 512]
                    s_w.inc(g.dma_start(out=wb.rearrange("p (c o) -> p c o", c=8)[:, :, 0:512],
                                        in_=src_g), 16)
                    s_w.inc(g.dma_start(out=wb.rearrange("p (c o) -> p c o", c=8)[:, :, 512:1024],
                                        in_=src_u), 16)

                for fc in range(4):
                    kf = ko * 4 + fc
                    hc = ob * 4 + fc

                    @ctx.sec("tensor")
                    def _(t, tb=tb, et=et, ob=ob, fc=fc, ko=ko, kf=kf, wb=wb, gbuf=gbuf):
                        s_w.wait(t, 32 * (ko + 1))
                        s_a.wait(t, 128 * (et + 1))
                        if kf >= 2:
                            s_si.wait(t, kf - 1)
                            s_h.wait(t, kf - 1)
                        for dc in range(8):
                            inst = t.matmul(ctx.pp[kf % 2][:, :],
                                            wb[:, dc * 1024 + fc * 128:dc * 1024 + fc * 128 + 128],
                                            gbuf[dc][:], start=(dc == 0), stop=(dc == 7))
                        for dc in range(8):
                            inst2 = t.matmul(ctx.pp[2 + kf % 2][:, :],
                                             wb[:, dc * 1024 + 512 + fc * 128:dc * 1024 + 512 + fc * 128 + 128],
                                             gbuf[dc][:], start=(dc == 0), stop=(dc == 7))
                        s_m1.inc(inst2)

                    @ctx.sec("scalar")
                    def _(sc, kf=kf):
                        s_m1.wait(sc, kf + 1)
                        if kf >= 1:
                            s_h.wait(sc, kf)
                        s_si.inc(sc.activation(sil[:], ctx.pp[kf % 2][:, :], AF.Silu))

                    @ctx.sec("vector")
                    def _(v, et=et, hc=hc, kf=kf, hbuf=hbuf):
                        s_si.wait(v, kf + 1)
                        if et >= 1 and hc == 0:
                            s_m2.wait(v, 8 * et)
                        s_h.inc(v.tensor_tensor(hbuf[:, hc * 512:(hc + 1) * 512],
                                                sil[:], ctx.pp[2 + kf % 2][:, :],
                                                ALU.mult))

            for tc in range(4):
                for oc2 in range(2):
                    kk = et * 8 + tc * 2 + oc2
                    pb2 = ctx.pp[4 + kk % 4]

                    @ctx.sec("tensor")
                    def _(t, e=e, tb=tb, tc=tc, oc2=oc2, et=et, kk=kk, pb2=pb2, hbuf=hbuf):
                        s_h.wait(t, 32 * (et + 1))
                        s_wo.wait(t, 16 * (e + 1))
                        if kk >= 4:
                            s_ev.wait(t, kk - 3)
                        for hc in range(32):
                            inst = t.matmul(
                                pb2[:, :],
                                hbuf[:, hc * 512 + tc * 128:hc * 512 + tc * 128 + 128],
                                wslice(wosb, 1024, hc, oc2 * 512, oc2 * 512 + 512),
                                start=(hc == 0), stop=(hc == 31))
                        s_m2.inc(inst)

                    @ctx.sec("vector")
                    def _(v, e=e, et=et, tc=tc, oc2=oc2, kk=kk, pb2=pb2):
                        s_m2.wait(v, kk + 1)
                        s_cg.wait(v, 64 * (et + 1))
                        kt = et * 4 + tc
                        if kt >= 2 and oc2 == 0:
                            s_ac.wait(v, 16 * (kt - 1))
                        s_ev.inc(v.tensor_scalar_mul(
                            tmp[kt % 2][:, oc2 * 512:(oc2 + 1) * 512], pb2[:, :],
                            cwg[et % 2][:, tc * 8 + e:tc * 8 + e + 1]))

                    if oc2 == 1:
                        @ctx.sec("gpsimd")
                        def _(g, et=et, tc=tc, kk=kk):
                            kt = et * 4 + tc
                            s_ev.wait(g, kk + 1)
                            if kt == 0:
                                s_z.wait(g, 1 + 16 * NTILE)
                            s_ac.inc(g.indirect_dma_start(
                                S["moe1"],
                                IndirectOffsetOnAxis(idxc[et % 2][:, tc:tc + 1], 0),
                                tmp[kt % 2][:], None, compute_op=ALU.add), 16)




def ph5es_moe0(ctx):
    nc, ar, S = ctx.nc, ctx.ar, ctx.S
    ar.reset()

    w1sb = [ar.alloc([128, 16384], BF16) for _ in range(2)]
    w2sb = [ar.alloc([128, 16384], BF16)]
    b1c = [ar.alloc([128, 16], F32) for _ in range(2)]
    b2r = [ar.alloc([128, 1024], BF16) for _ in range(2)]
    onesb = ar.alloc([128, 128], BF16)
    ga = [ar.alloc([128, 512], BF16) for _ in range(24)]
    hT = [ar.alloc([128, 8192], BF16) for _ in range(2)]
    tmp = [ar.alloc([128, 1024], F32) for _ in range(2)]
    zt = ar.alloc([128, 1024], F32)
    idxc = [ar.alloc([128, 4], I32) for _ in range(2)]
    cwg = [ar.alloc([128, 32], F32) for _ in range(2)]

    s_w = ctx.sem("p5s_w")
    s_z = ctx.sem("p5s_z")
    s_a = ctx.sem("p5s_a")
    s_ix = ctx.sem("p5s_ix")
    s_cg = ctx.sem("p5s_cg")
    s_m1 = ctx.sem("p5s_m1")
    s_h = ctx.sem("p5s_h")
    s_m2 = ctx.sem("p5s_m2")
    s_ev = ctx.sem("p5s_ev")
    s_ac = ctx.sem("p5s_ac")

    @ctx.sec("vector")
    def _(v):
        v.memset(zt[:], 0.0)
        v.memset(onesb[:], 1.0)
        s_z.inc(v.engine_nop())

    @ctx.sec("sync")
    def _(sy):
        s_z.wait(sy, 1)
        for i in range(NTILE):
            s_z.inc(sy.dma_start(out=S["moe0"][i * 128:(i + 1) * 128, :], in_=zt[:]), 16)

    for e in range(8):
        eb = e % 2

        @ctx.sec("gpsimd")
        def _(g, e=e, eb=eb):
            if e >= 2:
                g.wait_ge(s_m1.h, s_m1.base + 64 * (e - 1))
            if e >= 1:
                g.wait_ge(s_m2.h, s_m2.base + 32 * e)
            load_wT(ctx, g, S["f_w1"][e], 1024, 2048, w1sb[eb], s_w)
            load_wT(ctx, g, S["f_w2"][e], 2048, 1024, w2sb[0], s_w)
            s_w.inc(g.dma_start(out=b2r[eb][0:1, :],
                                in_=S["f_b2"][e].rearrange("(u f) -> u f", u=1)), 16)

        @ctx.sec("sync")
        def _(sy, e=e, eb=eb):
            if e >= 2:
                sy.wait_ge(s_h.h, s_h.base + 64 * (e - 1))
            s_w.inc(sy.dma_start(out=b1c[eb][:], in_=S["f_b1c"][e]), 16)

        for tb in range(4):
            et = e * 4 + tb
            gbuf = [ga[(et % 3) * 8 + dc] for dc in range(8)]
            hbuf = hT[et % 2]
            base = e * 2048 + tb * 512

            @ctx.sec("sync")
            def _(sy, et=et, base=base, gbuf=gbuf):
                if et >= 3:
                    s_m1.wait(sy, (et - 2) * 16)
                if et >= 2:
                    s_cg.wait(sy, 64 * (et - 1))
                    s_ac.wait(sy, 64 * (et - 1))
                s_ix.inc(sy.dma_start(
                    out=idxc[et % 2][:],
                    in_=S["idx0"].rearrange("(s u) -> u s", u=1)[:, base:base + 512]
                        .rearrange("u (t p) -> p (u t)", p=128),
                    allow_slow_non_contiguous=True), 16)
                for dc in range(8):
                    tload(ctx, sy, S["me_tm"], base, 512, dc * 128, 128, gbuf[dc], s_a)

            @ctx.sec("gpsimd")
            def _(g, et=et):
                s_ix.wait(g, 16 * (et + 1))
                if et >= 2:
                    s_ev.wait(g, 8 * (et - 1))
                for tc in range(4):
                    s_cg.inc(g.indirect_dma_start(
                        cwg[et % 2][:, tc * 8:(tc + 1) * 8], None, S["cw0_tm"],
                        IndirectOffsetOnAxis(idxc[et % 2][:, tc:tc + 1], 0)), 16)

            for oc in range(16):
                k = et * 16 + oc
                pb = ctx.pp[oc % 4]

                @ctx.sec("tensor")
                def _(t, e=e, eb=eb, et=et, oc=oc, k=k, pb=pb, gbuf=gbuf):
                    t.wait_ge(s_w.h, s_w.base + 64 * (e + 1))
                    s_a.wait(t, 128 * (et + 1))
                    if k >= 4:
                        s_h.wait(t, k - 3)
                    for dc in range(8):
                        inst = t.matmul(pb[:, :],
                                        wslice(w1sb[eb], 2048, dc, oc * 128, oc * 128 + 128),
                                        gbuf[dc][:], start=(dc == 0), stop=(dc == 7))
                    s_m1.inc(inst)

                @ctx.sec("scalar")
                def _(sc, eb=eb, et=et, oc=oc, k=k, pb=pb, hbuf=hbuf):
                    s_m1.wait(sc, k + 1)
                    if et >= 2 and oc == 0:
                        s_m2.wait(sc, 8 * (et - 1))
                    s_h.inc(sc.activation(hbuf[:, oc * 512:(oc + 1) * 512], pb[:, :],
                                          AF.Gelu, bias=b1c[eb][:, oc:oc + 1]))

            for tc in range(4):
                for oc2 in range(2):
                    kk = et * 8 + tc * 2 + oc2
                    pb2 = ctx.pp[4 + kk % 4]

                    @ctx.sec("tensor")
                    def _(t, eb=eb, et=et, tc=tc, oc2=oc2, kk=kk, pb2=pb2, hbuf=hbuf):
                        s_h.wait(t, 16 * (et + 1))
                        if kk >= 4:
                            s_ev.wait(t, kk - 3)
                        for hc in range(16):
                            t.matmul(pb2[:, :],
                                     hbuf[:, hc * 512 + tc * 128:hc * 512 + tc * 128 + 128],
                                     wslice(w2sb[0], 1024, hc, oc2 * 512, oc2 * 512 + 512),
                                     start=(hc == 0), stop=False, skip_group_check=True)
                        s_m2.inc(t.matmul(pb2[:, :], onesb[0:1, :],
                                          b2r[eb][0:1, oc2 * 512:(oc2 + 1) * 512],
                                          start=False, stop=True, skip_group_check=True))

                    @ctx.sec("vector")
                    def _(v, e=e, et=et, tc=tc, oc2=oc2, kk=kk, pb2=pb2):
                        s_m2.wait(v, kk + 1)
                        s_cg.wait(v, 64 * (et + 1))
                        kt = et * 4 + tc
                        if kt >= 2 and oc2 == 0:
                            s_ac.wait(v, 16 * (kt - 1))
                        s_ev.inc(v.tensor_scalar_mul(
                            tmp[kt % 2][:, oc2 * 512:(oc2 + 1) * 512], pb2[:, :],
                            cwg[et % 2][:, tc * 8 + e:tc * 8 + e + 1]))

                    if oc2 == 1:
                        @ctx.sec("gpsimd")
                        def _(g, et=et, tc=tc, kk=kk):
                            kt = et * 4 + tc
                            s_ev.wait(g, kk + 1)
                            if kt == 0:
                                s_z.wait(g, 1 + 16 * NTILE)
                            s_ac.inc(g.indirect_dma_start(
                                S["moe0"],
                                IndirectOffsetOnAxis(idxc[et % 2][:, tc:tc + 1], 0),
                                tmp[kt % 2][:], None, compute_op=ALU.add), 16)


def load_wT(ctx, sy, dram_wT, D_in, O, dst_sb, sem):
    src = dram_wT.rearrange("(c p) o -> p c o", p=128)
    sem.inc(sy.dma_start(out=dst_sb[:, :], in_=src), 16)


def wslice(wsb, O, dc, o0, o1):
    return wsb[:, dc * O + o0: dc * O + o1]


def tload(ctx, sy, scratch_tm, t0, nt, f0, nf, dst_sb, sem):
    sem.inc(sy.dma_start(out=dst_sb[:, :], in_=scratch_tm[t0:t0 + nt, f0:f0 + nf],
                         transpose=True), 16)



def stage_host(inputs):
    H = {}
    H["ones"] = np.ones((128, 1), np.float32)
    H["gn1_g"] = np.ascontiguousarray(
        np.asarray(inputs["L0_attn_norm_g"], np.float32).reshape(8, 128).T)
    H["gn1_b"] = np.ascontiguousarray(
        np.asarray(inputs["L0_attn_norm_b"], np.float32).reshape(8, 128).T)
    H["qwT"] = np.ascontiguousarray(np.asarray(inputs["L0_attn_qw"], np.float32).T)
    H["kwT"] = np.ascontiguousarray(np.asarray(inputs["L0_attn_kw"], np.float32).T)
    H["vwT"] = np.ascontiguousarray(np.asarray(inputs["L0_attn_vw"], np.float32).T)
    H["owT"] = np.ascontiguousarray(np.asarray(inputs["L0_attn_ow"], np.float32).T)
    H["ob_bc"] = np.ascontiguousarray(np.broadcast_to(
        np.asarray(inputs["L0_attn_ob"], np.float32)[None, :], (128, 1024)))
    H["gn2_g"] = np.ascontiguousarray(
        np.asarray(inputs["L0_attn_ffn_norm_g"], np.float32).reshape(8, 128).T)
    H["gn2_b"] = np.ascontiguousarray(
        np.asarray(inputs["L0_attn_ffn_norm_b"], np.float32).reshape(8, 128).T)
    gam = 1.0 - np.exp(np.linspace(np.log(1 / 32.), np.log(1 / 512.), 16))
    i_ = np.arange(128, dtype=np.float64)
    mask0 = np.zeros((128, 2048), np.float64)
    for h in range(16):
        dif = i_[None, :] - i_[:, None]
        mask0[:, h * 128:(h + 1) * 128] = np.where(dif >= 0, gam[h] ** dif, 0.0) / 8.0
    H["mask0"] = mask0.astype(np.float32)
    cross0 = np.zeros((128, 1024), np.float64)
    for g in range(8):
        for par in range(2):
            hh = 2 * g + par
            cross0[par * 64:(par + 1) * 64, g * 128:(g + 1) * 128] = \
                (gam[hh] ** (i_ + 1.0))[None, :] / 8.0
    H["cross0"] = cross0.astype(np.float32)
    kdec0f = np.zeros((128, 1024), np.float64)
    for h in range(16):
        kdec0f[:, h * 64:(h + 1) * 64] = (gam[h] ** (127.0 - i_))[:, None]
    H["kdec0f"] = kdec0f.astype(np.float32)
    H["w1T"] = np.ascontiguousarray(np.asarray(inputs["L0_attn_w1"], np.float32).T)
    H["w2T"] = np.ascontiguousarray(np.asarray(inputs["L0_attn_w2"], np.float32).T)
    H["b1c"] = np.ascontiguousarray(
        np.asarray(inputs["L0_attn_b1"], np.float32).reshape(16, 128).T)
    H["b2_bc"] = np.ascontiguousarray(np.broadcast_to(
        np.asarray(inputs["L0_attn_b2"], np.float32)[None, :], (128, 1024)))
    cols = np.array([h * 64 + j for h in range(16) for j in range(32)])
    H["f_qwsl"] = np.ascontiguousarray(np.asarray(inputs["L0_ffn_q_w"], np.float32)[:, cols])
    H["f_kw"] = np.ascontiguousarray(np.asarray(inputs["L0_ffn_k_w"], np.float32))
    H["f_vw"] = np.ascontiguousarray(np.asarray(inputs["L0_ffn_v_w"], np.float32))
    H["f_qbc"] = np.ascontiguousarray(
        np.asarray(inputs["L0_ffn_q_b"], np.float32)[cols].reshape(4, 128).T)
    H["f_kbc"] = np.ascontiguousarray(
        np.asarray(inputs["L0_ffn_k_b"], np.float32).reshape(4, 128).T)
    H["f_vb_bc"] = np.ascontiguousarray(np.broadcast_to(
        np.asarray(inputs["L0_ffn_v_b"], np.float32)[None, :], (128, 512)))
    H["f_ow"] = np.ascontiguousarray(np.asarray(inputs["L0_ffn_o_w"], np.float32)[:512])
    H["f_ob_bc"] = np.ascontiguousarray(np.broadcast_to(
        np.asarray(inputs["L0_ffn_o_b"], np.float32)[None, :], (128, 1024)))
    H["f_rw"] = np.ascontiguousarray(np.asarray(inputs["L0_ffn_router_w"], np.float32))
    H["f_rb_bc"] = np.ascontiguousarray(np.broadcast_to(
        np.asarray(inputs["L0_ffn_router_b"], np.float32)[None, :], (128, 8)))
    H["f_w1"] = np.ascontiguousarray(np.asarray(inputs["L0_ffn_w1"], np.float32))
    H["f_w2"] = np.ascontiguousarray(np.asarray(inputs["L0_ffn_w2"], np.float32))
    H["f_b1c"] = np.ascontiguousarray(
        np.asarray(inputs["L0_ffn_b1"], np.float32).reshape(8, 16, 128).transpose(0, 2, 1))
    H["f_b2"] = np.ascontiguousarray(np.asarray(inputs["L0_ffn_b2"], np.float32))
    H["WqT"] = np.ascontiguousarray(np.asarray(inputs["L1_attn_Wq"], np.float32).T)
    H["WkT"] = np.ascontiguousarray(np.asarray(inputs["L1_attn_Wk"], np.float32).T)
    H["WvT"] = np.ascontiguousarray(np.asarray(inputs["L1_attn_Wv"], np.float32).T)
    H["WgT"] = np.ascontiguousarray(np.asarray(inputs["L1_attn_Wg"], np.float32).T)
    H["WoT"] = np.ascontiguousarray(np.asarray(inputs["L1_attn_Wo"], np.float32).T)
    t_ = np.arange(2048, dtype=np.float64)
    invf = 1.0 / 10000.0 ** (np.arange(0, 256, 2, dtype=np.float64) / 256.0)
    fr = t_[None, :] * invf[:, None]
    H["cosT"] = np.cos(fr).astype(np.float32)
    H["sinT"] = np.sin(fr).astype(np.float32)
    gam1 = 1.0 - np.exp2(-5.0 - np.arange(4, dtype=np.float64))
    i_ = np.arange(128, dtype=np.float64)
    mask1 = np.zeros((128, 512), np.float64)
    for h in range(4):
        dif = i_[None, :] - i_[:, None]
        mask1[:, h * 128:(h + 1) * 128] = np.where(dif >= 0, gam1[h] ** dif, 0.0)
    H["mask1"] = mask1.astype(np.float32)
    cross1 = np.zeros((128, 512), np.float64)
    for h in range(4):
        cross1[:, h * 128:(h + 1) * 128] = (gam1[h] ** (i_ + 1.0))[None, :]
    H["cross1"] = cross1.astype(np.float32)
    kdec1f = np.zeros((128, 1024), np.float64)
    for h in range(4):
        kdec1f[:, h * 256:(h + 1) * 256] = (gam1[h] ** (127.0 - i_))[:, None]
    H["kdec1f"] = kdec1f.astype(np.float32)
    H["gnw_bc"] = np.ascontiguousarray(np.broadcast_to(
        np.asarray(inputs["L1_attn_gnorm_w"], np.float32)[None, :], (128, 512)))
    H["gate_wT"] = np.ascontiguousarray(np.asarray(inputs["L1_ffn_gate_w"], np.float32).T)
    H["ltstrict"] = np.ascontiguousarray(np.tril(np.ones((128, 128), np.float32), -1).T)
    tok = (np.arange(32)[None, :] * 128 + np.arange(128)[:, None])
    H["tokid_f"] = tok.astype(np.float32)
    H["tokid_i"] = tok.astype(np.int32)
    H["wgT_e"] = np.ascontiguousarray(np.asarray(inputs["L1_ffn_wi_gate"], np.float32)
                                      .transpose(0, 2, 1).astype(ml_dtypes.bfloat16))
    H["wuT_e"] = np.ascontiguousarray(np.asarray(inputs["L1_ffn_wi_up"], np.float32)
                                      .transpose(0, 2, 1).astype(ml_dtypes.bfloat16))
    H["woT_e"] = np.ascontiguousarray(np.asarray(inputs["L1_ffn_wo"], np.float32)
                                      .transpose(0, 2, 1).astype(ml_dtypes.bfloat16))
    return H


SCRATCH = {
    "p1dbg": ((128, 128), F32),
    "qT0": ((D, NT), BF16),
    "kT0": ((D, NT), BF16),
    "vtm0": ((NT, D), BF16),
    "ktm0": ((NT, D), BF16),
    "fq0": ((128, 32), F32),
    "fk0": ((128, 32), F32),
    "otm0": ((NT, D), BF16),
    "s1_f32": ((NT, D), F32),
    "s1_bf": ((NT, D), BF16),
    "gn2ab": ((2, 2, 128, 8), F32),
    "ph3_stat": ((2, 2, 16), F32),
    "sbr_f32": ((NT, D), F32),
    "fQT": ((512, NT), BF16),
    "fKT": ((512, NT), BF16),
    "fVtm": ((NT, 512), BF16),
    "aoTs": ((512, NT), BF16),
    "mla_tm": ((NT, D), BF16),
    "cmb0": ((128, 256), F32),
    "moe0": ((NT, D), F32),
    "x1_f32": ((NT, D), F32),
    "xn1_bf": ((NT, D), BF16),
    "qTr": ((D, NT), BF16),
    "kTr": ((D, NT), BF16),
    "vtm1": ((NT, 2048), BF16),
    "gtm1": ((NT, 2048), BF16),
    "gotm1": ((NT, 2048), BF16),
    "s2_f32": ((NT, D), F32),
    "cmb1": ((128, 256), F32),
    "moe1": ((NT, D), F32),
    "idx1": ((12288,), I32),
    "idx0": ((16384,), I32),
    "cw0_tm": ((NT, 8), F32),
    "me_tm": ((16384, D), BF16),
    "cw_tm": ((NT, 8), F32),
    "xe_tm": ((12288, D), BF16),
    "y": ((NT, D), F32),
    "xn0_f32": ((NT, D), F32),
    "xn0_bf": ((NT, D), BF16),
    "gn1ab": ((2, 2, 128, 8), F32),
    "ph1_stat": ((2, 2, 16), F32),
}
CONST_SB = {"ones", "gn1_g", "gn1_b", "gn2_g", "gn2_b"}
EXT = {
    "x": ((NT, D), F32),
    "qwT": ((D, D), F32),
    "kwT": ((D, D), F32),
    "vwT": ((D, D), F32),
    "owT": ((D, D), F32),
    "mask0": ((128, 2048), F32),
    "cross0": ((128, 1024), F32),
    "kdec0f": ((128, 1024), F32),
    "ob_bc": ((128, 1024), F32),
    "gn2_g": ((128, 8), F32),
    "gn2_b": ((128, 8), F32),
    "w1T": ((D, 2048), F32),
    "w2T": ((2048, D), F32),
    "b1c": ((128, 16), F32),
    "b2_bc": ((128, 1024), F32),
    "f_qwsl": ((D, 512), F32),
    "f_kw": ((D, 512), F32),
    "f_vw": ((D, 512), F32),
    "f_qbc": ((128, 4), F32),
    "f_kbc": ((128, 4), F32),
    "f_vb_bc": ((128, 512), F32),
    "f_ow": ((512, D), F32),
    "f_ob_bc": ((128, 1024), F32),
    "f_rw": ((D, 8), F32),
    "f_rb_bc": ((128, 8), F32),
    "f_w1": ((8, D, 2048), F32),
    "f_w2": ((8, 2048, D), F32),
    "f_b1c": ((8, 128, 16), F32),
    "f_b2": ((8, 1024), F32),
    "WqT": ((D, D), F32),
    "WkT": ((D, D), F32),
    "WvT": ((D, 2048), F32),
    "WgT": ((D, 2048), F32),
    "WoT": ((2048, D), F32),
    "cosT": ((128, 2048), F32),
    "sinT": ((128, 2048), F32),
    "mask1": ((128, 512), F32),
    "cross1": ((128, 512), F32),
    "kdec1f": ((128, 1024), F32),
    "gnw_bc": ((128, 512), F32),
    "gate_wT": ((D, 8), F32),
    "ltstrict": ((128, 128), F32),
    "tokid_f": ((128, 32), F32),
    "tokid_i": ((128, 32), I32),
    "wgT_e": ((8, D, 4096), BF16),
    "wuT_e": ((8, D, 4096), BF16),
    "woT_e": ((8, 4096, D), BF16),
    "ones": ((128, 1), F32),
    "gn1_g": ((128, 8), F32),
    "gn1_b": ((128, 8), F32),
}

PHASES = [
    ("ln0", ph1_ln0, ["x", "ones", "gn1_g", "gn1_b"],
     ["xn0_f32", "xn0_bf", "gn1ab", "p1dbg"], ["ph1_stat"]),
    ("qk0", ph2a_qk0, ["xn0_bf", "gn1ab", "qwT", "kwT", "ones"],
     ["qT0", "kT0", "fq0", "fk0"], []),
    ("kv0", ph2b_kv0, ["xn0_bf", "gn1ab", "vwT", "kwT", "fk0"],
     ["vtm0", "ktm0"], []),
    ("ret0", ph3a_ret0, ["qT0", "kT0", "fq0", "fk0", "ktm0", "vtm0",
                         "mask0", "cross0", "kdec0f", "ones"],
     ["otm0"], []),
    ("opr0", ph3b_oproj0, ["otm0", "xn0_f32", "owT", "ob_bc", "ones", "gn2_g", "gn2_b"],
     ["s1_f32", "s1_bf", "gn2ab"], ["ph3_stat"]),
    ("ffn0", ph4_ffn0, ["s1_bf", "s1_f32", "gn2ab", "w1T", "w2T", "b1c", "b2_bc"],
     ["sbr_f32"], []),
    ("fqkv", ph5a_qkv, ["xn0_bf", "f_qwsl", "f_kw", "f_vw", "f_qbc", "f_kbc", "f_vb_bc"],
     ["fQT", "fKT", "fVtm"], []),
    ("fatt", ph5b_attn, ["fQT", "fKT", "fVtm"], ["aoTs"], []),
    ("fmla", ph5c_mla, ["aoTs", "f_ow", "f_ob_bc"], ["mla_tm"], []),
    ("frtr", ph5d_router, ["mla_tm", "f_rw", "f_rb_bc"], ["cmb0"], []),
    ("rte0", ph_rte0, ["cmb0", "mla_tm", "ltstrict", "tokid_f", "tokid_i"],
     ["idx0", "cw0_tm", "me_tm"], []),
    ("moe0s", ph5es_moe0, ["idx0", "cw0_tm", "me_tm", "f_w1", "f_w2", "f_b1c", "f_b2"],
     ["moe0"], []),
    ("fmoe", ph5e_moe0, ["mla_tm", "cmb0", "f_w1", "f_w2", "f_b1c", "f_b2"], ["moe0"], []),
    ("cmb0f", ph5f_comb0, ["x", "xn0_f32", "sbr_f32", "moe0"], ["x1_f32"], []),
    ("ln1", ph6_ln1, ["x1_f32"], ["xn1_bf"], []),
    ("qk1", ph7a_qk1, ["xn1_bf", "WqT", "WkT", "cosT", "sinT"], ["qTr", "kTr"], []),
    ("vg1", ph7b_vg1, ["xn1_bf", "WvT", "WgT"], ["vtm1", "gtm1"], []),
    ("ret1", ph8_ret1, ["qTr", "kTr", "vtm1", "gtm1", "mask1", "cross1", "kdec1f", "gnw_bc"],
     ["gotm1"], []),
    ("opr1", ph9_oproj1, ["gotm1", "WoT"], ["s2_f32"], []),
    ("rtr1", ph10_rtr1, ["xn1_bf", "gate_wT"], ["cmb1"], []),
    ("moe1p", ph11_moe1, ["xn1_bf", "cmb1", "wgT_e", "wuT_e", "woT_e"], ["moe1"], []),
    ("rte1", ph_rte1, ["cmb1", "xn1_bf", "ltstrict", "tokid_f", "tokid_i"],
     ["idx1", "cw_tm", "xe_tm"], []),

    ("moe1s", ph11s_moe1, ["idx1", "cw_tm", "xe_tm", "wgT_e", "wuT_e", "woT_e"],
     ["moe1"], []),
    ("fin", ph12_final, ["x1_f32", "s2_f32", "moe1"], ["y"], []),
]


def build_group(phase_names, final_outputs=None):
    todo = [p for p in PHASES if p[0] in phase_names]
    assert len(todo) == len(phase_names)
    produced, consumed, internal = set(), set(), set()
    for _, _, ins, outs, tmps in todo:
        for n in ins:
            if n not in produced:
                consumed.add(n)
        produced.update(outs)
        internal.update(tmps)

    nc = bass.Bass("TRN2", target_bir_lowering=False, debug=False)
    ctx = Ctx(nc)
    for n in sorted(consumed):
        shape, dt = SCRATCH[n] if n in SCRATCH else EXT[n]
        ctx.S[n] = nc.dram_tensor(n, list(shape), dt, kind="ExternalInput").ap()
    for n in sorted(produced):
        shape, dt = SCRATCH[n]
        kind = "ExternalOutput" if (final_outputs is None or n in final_outputs) \
            else "Internal"
        ctx.S[n] = nc.dram_tensor(n, list(shape), dt, kind=kind).ap()
    for n in sorted(internal):
        shape, dt = SCRATCH[n]
        ctx.S[n] = nc.dram_tensor(n, list(shape), dt, kind="Internal").ap()
    block = nc.Block().__enter__()
    ctx.block = block

    const_names = [n for n in sorted(consumed) if n in CONST_SB]
    ctx.csb = {}
    cld = ctx.sem("c_ld")
    for n in const_names:
        shape, dt = EXT[n]
        ctx.csb[n] = nc.alloc_sbuf_tensor("c_" + n, list(shape), dt).ap()

    @ctx.sec("sync")
    def _(sy):
        for n in const_names:
            cld.inc(sy.dma_start(out=ctx.csb[n][:, :], in_=ctx.S[n]), 16)
    ctx.cld = cld
    ctx.consts = ctx.csb

    for pi, (name, emit, ins, outs, tmps) in enumerate(todo):
        if pi > 0:
            snap = [(s, s.n) for s in ctx.sems.values() if s.n > 0]
            for eng in ["sync", "scalar", "vector", "tensor", "gpsimd"]:
                @ctx.sec(eng)
                def _(e, snap=snap):
                    for s, n in snap:
                        e.wait_ge(s.h, n)
        ctx.new_phase()
        emit(ctx)

    @ctx.sec("sync")
    def _(sy):
        for s in ctx.sems.values():
            sy.wait_ge(s.h, s.n)
    block.__exit__(None, None, None)
    outs = sorted(produced) if final_outputs is None else sorted(
        n for n in produced if n in final_outputs)
    return nc, sorted(consumed), outs


ALL_PHASES = [p[0] for p in PHASES if p[0] not in ("moe1p", "rte0", "moe0s")]


def kernel(**inputs):
    H = stage_host(inputs)
    x = np.asarray(inputs["L0_attn_x"], np.float32)
    shards = [np.ascontiguousarray(x[2 * c:2 * c + 2].reshape(NT, D)) for c in range(8)]
    nc, ins, outs = build_group(ALL_PHASES, final_outputs={"y"})
    in_maps = []
    for c in range(8):
        im = {}
        for n in ins:
            im[n] = shards[c] if n == "x" else H[n]
        in_maps.append(im)
    res = run_bass_kernel_spmd(nc, in_maps, core_ids=list(range(8)))
    y = np.concatenate([np.asarray(res.results[c]["y"]).reshape(2, T, D)
                        for c in range(8)], axis=0)
    return y
